# version 02
# baseline (speedup 1.0000x reference)
import numpy as np
import ml_dtypes
import concourse.bass as bass
import concourse.mybir as mybir
from concourse.bass_utils import run_bass_kernel_spmd

F32 = mybir.dt.float32
BF16 = mybir.dt.bfloat16
I32 = mybir.dt.int32
OP = mybir.AluOpType
AF = mybir.ActivationFunctionType
AX = mybir.AxisListType.X

T = 2048
D = 2048
NCH = 16
NH = 16
HD = 128
KV = 1024
FH = 8192
DI = 4096
DCH = 32
NST = 16
MT = 256
NMT = T // MT
FT = 512
SQ8192 = float(np.float32(1.0 / np.sqrt(np.float64(8192.0)) / (2.0 ** -7)))
LN255 = float(np.float32(np.log(np.float64(255.0))))
INV127 = float(np.float32(1.0 / np.float64(127.0)))
SCALEQ = float(np.float32(1.0 / np.sqrt(np.float64(128.0))))

UPTO = 99


def _had(n):
    H = np.ones((n, n), np.float64)
    for i in range(n):
        for j in range(n):
            if bin(i & j).count("1") % 2:
                H[i, j] = -1.0
    return H


class Sem:
    def __init__(self, nc, name):
        self.h = nc.alloc_semaphore(name)
        self.n = 0

    def inc(self, instr, k=1):
        instr.then_inc(self.h, k)
        self.n += k
        return instr

    def wait(self, eng, lag=0):
        if self.n - lag > 0:
            eng.wait_ge(self.h, self.n - lag)


def row_ap(vec_d, n, off=0):
    return bass.AP(vec_d.tensor, vec_d.offset + off, [[n, 1], [1, n]])


def build(upto=99, dump=(), skip_p1=False, skip_prep=False):
    nc = bass.Bass("TRN2", target_bir_lowering=False, detect_race_conditions=False)
    REG = {}

    def din(name, shape, dt=F32):
        return nc.dram_tensor(name, list(shape), dt, kind="ExternalInput").ap()

    x_d = din("x", [T, D])
    condT_d = din("condT", [D, KV])
    Wq_d = din("Wq", [D, D])
    bq_d = din("bq", [D])
    Wkv_d = din("Wkv", [D, 2 * D])
    bkv_d = din("bkv", [2 * D])
    Wp_d = din("Wp", [D, D])
    bp_d = din("bp", [D])
    W1T_d = din("W1T", [D, FH])
    W3T_d = din("W3T", [D, FH])
    W2TR_d = din("W2TR", [FH, D])
    WinT_d = din("WinT", [D, 2 * DI])
    Wconv_d = din("Wconv", [DI, 4])
    bconv_d = din("bconv", [DI])
    WxT_d = din("WxT", [DI, 160])
    WdtT_d = din("WdtT", [128, DI])
    bdt_d = din("bdt", [DI])
    WoutT_d = din("WoutT", [DI, D])
    Alog_d = din("Alog", [DI, NST])
    Dv_d = din("Dvec", [DI])
    wgt_d = din("wgt", [D, D])
    bias1_d = din("bias1", [D])
    onesbf_d = din("onesbf", [1, 128], BF16)
    onesf_d = din("onesf", [1, 128])
    I128_d = din("I128", [128, 128], BF16)
    H128_d = din("H128", [128, 128], BF16)
    H64B_d = din("H64B", [128, 128], BF16)

    out_d = nc.dram_tensor("out", [T, D], F32, kind="ExternalOutput").ap()

    def dscr(name, shape, dt=BF16):
        return nc.dram_tensor(name, list(shape), dt, kind="Internal").ap()

    xnT_s = dscr("xnT_s", [D, T])
    w2b_s = dscr("w2b_s", [FH, D])
    kscT_s = dscr("kscT_s", [D, KV])
    vtok_s = dscr("vtok_s", [KV, D])
    cqT_s = dscr("cqT_s", [D, T])
    otT_s = dscr("otT_s", [D, T])
    wtT_s = dscr("wtT_s", [D, D])
    winb_s = dscr("winb_s", [D, 2 * DI])
    woutb_s = dscr("woutb_s", [DI, D])

    def sbt(name, shape, dt):
        return nc.alloc_sbuf_tensor(name, list(shape), dt).ap()

    x_tm = [sbt(f"x_tm{i}", [128, D], BF16) for i in range(NCH)]
    ARN = sbt("ARN", [128, 32768], BF16)
    ROWS = sbt("ROWS", [1, 4096], BF16)
    SCRA = sbt("SCRA", [128, D], F32)
    SCRB = sbt("SCRB", [128, D], F32)
    SCRD = sbt("SCRD", [128, 1024], F32)
    SCRI = sbt("SCRI", [128, D], I32)
    sbfA = sbt("sbfA", [128, D], BF16)
    sbfB = sbt("sbfB", [128, D], BF16)
    WB0 = sbt("WB0", [128, 4096], BF16)
    WB1 = sbt("WB1", [128, 2048], BF16)
    WB2 = sbt("WB2", [128, 2048], BF16)
    onesbf = sbt("onesbf_t", [1, 128], BF16)
    onesf = sbt("onesf_t", [1, 128], F32)
    I128 = sbt("I128_t", [128, 128], BF16)
    H128 = sbt("H128_t", [128, 128], BF16)
    H64B = sbt("H64B_t", [128, 128], BF16)
    ln_m = sbt("ln_m", [128, NCH], F32)
    ln_r = sbt("ln_r", [128, NCH], F32)
    ln_b = sbt("ln_b", [128, NCH], F32)
    colA = sbt("colA", [128, NCH], F32)
    colB = sbt("colB", [128, NCH], F32)
    colC = sbt("colC", [128, NCH], F32)
    sq_all = sbt("sq_all", [128, 256], F32)
    nsq_all = sbt("nsq_all", [128, 256], F32)
    acols = sbt("acols", [128, 512], F32)
    ccols = sbt("ccols", [128, 128], F32)
    CARR = sbt("CARR", [128, 512], F32)
    BCR = sbt("BCR", [32, MT], BF16)

    import os as _osd
    DBGB = (sbt("DBGB", [128, 1024], BF16)
            if (_osd.environ.get("P1DEBUG") or _osd.environ.get("V1DEBUG")
                or _osd.environ.get("P2DEBUG")) else None)
    ROWSC = sbt("ROWSC", [1, 4096], BF16)
    psA = nc.alloc_psum_tensor("psA", [128, 2048], F32).ap()
    psB = nc.alloc_psum_tensor("psB", [128, 1024], F32).ap()
    psT = nc.alloc_psum_tensor("psT", [128, 1024], BF16).ap()
    psU = nc.alloc_psum_tensor("psU", [128, 1024], BF16).ap()

    v, g, s, t, sy = nc.vector, nc.gpsimd, nc.scalar, nc.tensor, nc.sync

    sdma = Sem(nc, "sdma")
    gdma = Sem(nc, "gdma")
    adma = Sem(nc, "adma")
    spe = Sem(nc, "spe")
    sdv = Sem(nc, "sdv")
    sac = Sem(nc, "sac")
    sgp = Sem(nc, "sgp")
    sout = Sem(nc, "sout")

    REG.update(locals())

    def vseal(instr):
        sdv.inc(instr)
        sdv.wait(v)
        return instr

    def sseal(instr):
        sac.inc(instr)
        sac.wait(s)
        return instr

    def castload(dst_ap, src_ap, slow=False):
        return gdma.inc(g.dma_start(out=dst_ap, in_=src_ap,
                                    allow_slow_non_contiguous=slow), 16)

    def load(dst_ap, src_ap, eng=None, sem=None, slow=False):
        e = eng or sy
        sm = sem or sdma
        return sm.inc(e.dma_start(out=dst_ap, in_=src_ap,
                                  allow_slow_non_contiguous=slow), 16)

    def load_cols(dst_ap, vec_d, n, off=0):
        src = bass.AP(vec_d.tensor, vec_d.offset + off, [[1, 128], [128, n]])
        return sdma.inc(sy.dma_start(out=dst_ap, in_=src,
                                     allow_slow_non_contiguous=True), 16)

    constp = sbt("constp", [128, 8], F32)
    cvals = [1.0, -1.0, 0.5, -0.5, 1e-5]
    for i_, val_ in enumerate(cvals):
        v.memset(constp[:, i_:i_ + 1], val_)
        nc.const_aps.register(F32, val_, constp[:, i_:i_ + 1])
    sdv.inc(v.nop())
    sdv.wait(s)

    load(onesbf, onesbf_d)
    load(onesf, onesf_d)
    load(I128, I128_d)
    load(H128, H128_d)
    load(H64B, H64B_d)
    
    import os as _os4
    if not _os4.environ.get("SKIP_COLS"):
      for k in range(4):
        src = bass.AP(Wconv_d.tensor, Wconv_d.offset + k, [[4, 128], [512, 32]])
        sdma.inc(sy.dma_start(out=ccols[:, k * 32:(k + 1) * 32], in_=src,
                              allow_slow_non_contiguous=True), 16)
    bdt_cols = sbt("bdt_cols", [128, 32], F32)
    bcv_cols = sbt("bcv_cols", [128, 32], F32)
    dv_cols = sbt("dv_cols", [128, 32], F32)
    if not _os4.environ.get("SKIP_COLS"):
        load_cols(bdt_cols, bdt_d, 32)
        load_cols(bcv_cols, bconv_d, 32)
        load_cols(dv_cols, Dv_d, 32)
        srcA = bass.AP(Alog_d.tensor, Alog_d.offset,
                       [[NST, 128], [128 * NST, 32], [1, NST]])
        sdma.inc(sy.dma_start(out=acols, in_=srcA, allow_slow_non_contiguous=True), 16)
    for i in range(NCH):
        castload(x_tm[i], x_d[i * 128:(i + 1) * 128, :])
    sdma.wait(sy); sdma.wait(t); sdma.wait(v); sdma.wait(s)
    gdma.wait(g)
    nc.all_engine_barrier()
    import os as _os2
    if not _os2.environ.get("SKIP_ACOLS"):
        sac.inc(s.activation(out=acols, in_=acols, func=AF.Exp))
        sac.wait(v)
        sdv.inc(v.tensor_scalar(out=acols, in0=acols, scalar1=-1.0, scalar2=None,
                                op0=OP.mult))

    for ch in range(FH // 128 if not skip_prep else 0):
        wb = WB1 if ch % 2 == 0 else WB2
        if ch >= 2:
            spe.wait(g)
        castload(wb[:, 0:D], W2TR_d[ch * 128:(ch + 1) * 128, :])
        gdma.wait(t)
        sdv.wait(t)
        for gc in range(4):
            spe.inc(t.matmul(psA[:, gc * 512:(gc + 1) * 512], H64B,
                             wb[:, gc * 512:(gc + 1) * 512], start=True, stop=True))
        spe.wait(v)
        sdma.wait(v)
        sdv.inc(v.tensor_scalar(out=sbfB[:, 0:D], in0=psA, scalar1=SQ8192,
                                scalar2=None, op0=OP.mult))
        sdv.wait(sy)
        dst = bass.AP(w2b_s.tensor, w2b_s.offset + (2 * ch) * D,
                      [[D, 2], [128 * D, 64], [1, D]])
        sdma.inc(sy.dma_start(out=dst, in_=sbfB[:, 0:D]), 16)

    for ic in range(NCH if not skip_prep else 0):
        if ic > 0:
            sdv.wait(g)
            sac.wait(g)
        castload(WB0[:, 0:D], wgt_d[ic * 128:(ic + 1) * 128, :])
        gdma.wait(s)
        gdma.wait(v)
        sac.inc(s.activation(out=SCRA, in_=WB0[:, 0:D], func=AF.Abs))
        sac.wait(v)
        vseal(v.reduce_sum(colA[:, 0:1], SCRA, axis=AX))
        vseal(v.tensor_scalar(out=colA[:, 0:1], in0=colA[:, 0:1], scalar1=1.0 / D,
                              scalar2=None, op0=OP.mult))
        vseal(v.tensor_scalar(out=colB[:, 0:1], in0=colA[:, 0:1], scalar1=1e-8,
                              scalar2=None, op0=OP.add))
        vseal(v.reciprocal(colB[:, 1:2], colB[:, 0:1]))
        v.tensor_scalar(out=SCRA, in0=WB0[:, 0:D], scalar1=colB[:, 1:2],
                        scalar2=None, op0=OP.mult)
        v.tensor_copy(SCRI, SCRA)
        v.scalar_tensor_tensor(out=SCRB, in0=SCRA, scalar=-0.5, in1=SCRI,
                               op0=OP.add, op1=OP.is_ge)
        v.tensor_add(SCRB, SCRI, SCRB)
        for hh in range(2):
            v.scalar_tensor_tensor(out=SCRD, in0=SCRA[:, hh * 1024:(hh + 1) * 1024],
                                   scalar=0.5, in1=SCRI[:, hh * 1024:(hh + 1) * 1024],
                                   op0=OP.add, op1=OP.is_le)
            v.tensor_sub(SCRB[:, hh * 1024:(hh + 1) * 1024],
                         SCRB[:, hh * 1024:(hh + 1) * 1024], SCRD)
        v.tensor_scalar(out=SCRB, in0=SCRB, scalar1=-1.0, scalar2=1.0,
                        op0=OP.max, op1=OP.min)
        sdv.inc(v.tensor_scalar(out=sbfA[:, 0:D], in0=SCRB, scalar1=colA[:, 0:1],
                                scalar2=None, op0=OP.mult))
        sdv.wait(t)
        sdma.wait(v)
        for kc in range(NCH):
            slot = psT if kc % 2 == 0 else psU
            sdv.wait(t, lag=1)
            t.transpose(slot[:, 0:128], sbfA[:, kc * 128:(kc + 1) * 128], I128)
            spe.inc(t.nop())
            spe.wait(v)
            sdv.inc(v.tensor_copy(sbfB[:, kc * 128:(kc + 1) * 128], slot[:, 0:128]))
        sdv.wait(sy)
        for kc in range(NCH):
            sdma.inc(sy.dma_start(
                out=wtT_s[kc * 128:(kc + 1) * 128, ic * 128:(ic + 1) * 128],
                in_=sbfB[:, kc * 128:(kc + 1) * 128]), 16)

    sdv.wait(g)
    sac.wait(g)
    for src_d, dst_s, rows, cols in (() if skip_prep else
                                     ((WinT_d, winb_s, D, 2 * DI),
                                      (WoutT_d, woutb_s, DI, D))):
        for rc in range(rows // 128):
            for cb in range(0, cols, 4096):
                cw = min(4096, cols - cb)
                gdma.wait(g)
                castload(WB0[:, 0:cw], src_d[rc * 128:(rc + 1) * 128, cb:cb + cw])
                gdma.wait(g)
                gdma.inc(g.dma_start(out=dst_s[rc * 128:(rc + 1) * 128, cb:cb + cw],
                                     in_=WB0[:, 0:cw]), 16)
    nc.all_engine_barrier()
    if upto == 0:
        _emit_dump(nc, REG, dump)
        _finalize(nc, REG)
        return nc

    cndT = [ARN[:, i * KV:(i + 1) * KV] for i in range(NCH)]
    if skip_p1:
        nc.all_engine_barrier()
    else:
        pass
    if not skip_p1:
      for i in range(NCH):
        castload(cndT[i], condT_d[i * 128:(i + 1) * 128, :])
      castload(ROWS[:, 0:2 * D], row_ap(bkv_d, 2 * D))
      gdma.wait(t)
      sdma.wait(t)
    bkvrow = ROWS[:, 0:2 * D]

    def fq_signed_perpart(src_f32, w, scol, icol, out_bf, scale_back):
        v.tensor_scalar(out=SCRA[:, 0:w], in0=src_f32, scalar1=icol, scalar2=None,
                        op0=OP.mult)
        v.tensor_copy(SCRI[:, 0:w], SCRA[:, 0:w])
        v.scalar_tensor_tensor(out=SCRB[:, 0:w], in0=SCRA[:, 0:w], scalar=-0.5,
                               in1=SCRI[:, 0:w], op0=OP.add, op1=OP.is_ge)
        v.tensor_add(SCRB[:, 0:w], SCRI[:, 0:w], SCRB[:, 0:w])
        for hb in range(0, w, 1024):
            hw = min(1024, w - hb)
            v.scalar_tensor_tensor(out=SCRD[:, 0:hw], in0=SCRA[:, hb:hb + hw],
                                   scalar=0.5, in1=SCRI[:, hb:hb + hw],
                                   op0=OP.add, op1=OP.is_le)
            v.tensor_sub(SCRB[:, hb:hb + hw], SCRB[:, hb:hb + hw], SCRD[:, 0:hw])
        if scale_back:
            sdv.inc(v.tensor_scalar(out=out_bf, in0=SCRB[:, 0:w], scalar1=scol,
                                    scalar2=None, op0=OP.mult))
        else:
            sdv.inc(v.tensor_copy(out_bf, SCRB[:, 0:w]))

    import os as _os9
    _p1dbg = bool(_os9.environ.get("P1DEBUG")) and not skip_p1
    _v1dbg = bool(_os9.environ.get("V1DEBUG")) and not skip_p1
    if _v1dbg:
        _p1dbg = True
    for chalf in range((2 if not _p1dbg else (0 if _v1dbg else 1)) if not skip_p1 else 0):
        for jp in range(4 if not _p1dbg else 1):
            sdv.wait(t)
            for dm in range(NCH):
                wb = WB1 if dm % 2 == 0 else WB2
                spe.wait(g)
                castload(wb[:, 0:1024],
                         Wkv_d[dm * 128:(dm + 1) * 128, chalf * 1024:(chalf + 1) * 1024])
                gdma.wait(t)
                nob = bool(_os9.environ.get("NOBIAS"))
                for jj in range(2):
                    jc = jp * 2 + jj
                    pk = psA[:, jj * 1024:(jj + 1) * 1024]
                    if dm == 0 and not nob:
                        spe.inc(t.matmul(pk[:, 0:512], onesbf,
                                         bkvrow[:, chalf * 1024:chalf * 1024 + 512],
                                         start=True, stop=False))
                        spe.inc(t.matmul(pk[:, 512:1024], onesbf,
                                         bkvrow[:, chalf * 1024 + 512:(chalf + 1) * 1024],
                                         start=True, stop=False))
                    for gc in range(2):
                        spe.inc(t.matmul(pk[:, gc * 512:(gc + 1) * 512],
                                         cndT[dm][:, jc * 128:(jc + 1) * 128],
                                         wb[:, gc * 512:(gc + 1) * 512],
                                         start=(dm == 0 and nob),
                                         stop=(dm == NCH - 1)))
            spe.wait(v)
            if _p1dbg:
                v.tensor_copy(DBGB, psA[:, 0:1024])
            for jj in range(2 if not _p1dbg else 1):
                jc = jp * 2 + jj
                pk = psA[:, jj * 1024:(jj + 1) * 1024]
                vseal(v.reduce_max(colA[:, 0:8], pk.rearrange("p (h q) -> p h q", q=128),
                                   axis=AX, apply_absolute_value=True))
                vseal(v.tensor_scalar(out=colA[:, 0:8], in0=colA[:, 0:8], scalar1=INV127,
                                      scalar2=1e-8, op0=OP.mult, op1=OP.max))
                vseal(v.reciprocal(colB[:, 0:8], colA[:, 0:8]))
                v.tensor_tensor(out=SCRA[:, 0:1024].rearrange("p (h q) -> p h q", q=128),
                                in0=pk.rearrange("p (h q) -> p h q", q=128),
                                in1=colB[:, 0:8].unsqueeze(2).broadcast_to(
                                    (128, 8, 128)),
                                op=OP.mult)
                v.tensor_copy(SCRI[:, 0:1024], SCRA[:, 0:1024])
                v.scalar_tensor_tensor(out=SCRB[:, 0:1024], in0=SCRA[:, 0:1024],
                                       scalar=-0.5, in1=SCRI[:, 0:1024],
                                       op0=OP.add, op1=OP.is_ge)
                v.tensor_add(SCRB[:, 0:1024], SCRI[:, 0:1024], SCRB[:, 0:1024])
                v.scalar_tensor_tensor(out=SCRD, in0=SCRA[:, 0:1024], scalar=0.5,
                                       in1=SCRI[:, 0:1024], op0=OP.add, op1=OP.is_le)
                v.tensor_sub(SCRB[:, 0:1024], SCRB[:, 0:1024], SCRD)
                spe.wait(v)
                sdv.inc(v.tensor_tensor(
                    out=sbfA[:, 0:1024].rearrange("p (h q) -> p h q", q=128),
                    in0=SCRB[:, 0:1024].rearrange("p (h q) -> p h q", q=128),
                    in1=colA[:, 0:8].unsqueeze(2).broadcast_to((128, 8, 128)),
                    op=OP.mult))
                sdv.wait(t)
                sdma.wait(v)
                for hh in range(8):
                    h = chalf * 8 + hh
                    slot = psT if hh % 2 == 0 else psU
                    sdv.wait(t, lag=1)
                    t.transpose(slot[:, 0:128], sbfA[:, hh * 128:(hh + 1) * 128], I128)
                    spe.inc(t.nop())
                    spe.wait(v)
                    sdv.inc(v.tensor_copy(sbfB[:, hh * 128:(hh + 1) * 128],
                                          slot[:, 0:128]))
                sdv.wait(sy)
                for hh in range(8):
                    h = chalf * 8 + hh
                    sdma.inc(sy.dma_start(
                        out=kscT_s[h * 128:(h + 1) * 128, jc * 128:(jc + 1) * 128],
                        in_=sbfB[:, hh * 128:(hh + 1) * 128]), 16)

    if not skip_p1 and (not _p1dbg or _v1dbg):
        load_cols(colC[:, 0:NCH], bkv_d, NCH, off=2048)
    for cc in range((NCH if not _p1dbg else (1 if _v1dbg else 0)) if not skip_p1 else 0):
        spe.wait(g)
        castload(WB1[:, 0:2048].rearrange("p (a b) -> p a b", b=128),
                 Wkv_d[:, 2048 + cc * 128:2048 + (cc + 1) * 128]
                 .rearrange("(a p) b -> p a b", p=128), slow=True)
        gdma.wait(t)
        spe.wait(v, lag=0)
        sdv.wait(t)
        for dm in range(NCH):
            for gc in range(2):
                spe.inc(t.matmul(psA[:, gc * 512:(gc + 1) * 512],
                                 WB1[:, dm * 128:(dm + 1) * 128],
                                 cndT[dm][:, gc * 512:(gc + 1) * 512],
                                 start=(dm == 0), stop=(dm == NCH - 1)))
        spe.wait(v)
        if _v1dbg:
            v.tensor_copy(DBGB, psA[:, 0:1024])
        vcol = colC[:, cc:cc + 1]
        vseal(v.tensor_scalar(out=SCRA[:, 0:1024], in0=psA[:, 0:1024], scalar1=vcol,
                              scalar2=None, op0=OP.add))
        vseal(v.reduce_max(colA[:, 0:1], SCRA[:, 0:1024], axis=AX,
                           apply_absolute_value=True))
        vseal(v.tensor_scalar(out=colA[:, 0:1], in0=colA[:, 0:1], scalar1=INV127,
                              scalar2=1e-8, op0=OP.mult, op1=OP.max))
        vseal(v.reciprocal(colB[:, 0:1], colA[:, 0:1]))
        spe.wait(v)
        fq_signed_perpart(SCRA[:, 0:1024], 1024, colA[:, 0:1], colB[:, 0:1],
                          sbfA[:, 0:1024], True)
        sdv.wait(t)
        sdma.wait(v)
        for jc in range(8):
            slot = psT if jc % 2 == 0 else psU
            sdv.wait(t, lag=1)
            t.transpose(slot[:, 0:128], sbfA[:, jc * 128:(jc + 1) * 128], I128)
            spe.inc(t.nop())
            spe.wait(v)
            sdv.inc(v.tensor_copy(sbfB[:, jc * 128:(jc + 1) * 128], slot[:, 0:128]))
        sdv.wait(sy)
        for jc in range(8):
            sdma.inc(sy.dma_start(
                out=vtok_s[jc * 128:(jc + 1) * 128, cc * 128:(cc + 1) * 128],
                in_=sbfB[:, jc * 128:(jc + 1) * 128]), 16)
    nc.all_engine_barrier()
    if upto == 1:
        _emit_dump(nc, REG, dump)
        _finalize(nc, REG)
        return nc

    def emit_layernorm(src_tiles):
        sdv.wait(s)
        spe.wait(s)
        for ic in range(NCH):
            sdv.inc(v.reduce_sum(colA[:, ic:ic + 1], src_tiles[ic], axis=AX))
            sdv.wait(s)
            sac.inc(s.activation(out=SCRA, in_=src_tiles[ic], func=AF.Square))
            sac.wait(v)
            sdv.inc(v.reduce_sum(colB[:, ic:ic + 1], SCRA, axis=AX))
        sdv.wait(v)
        vseal(v.tensor_scalar(out=ln_m, in0=colA, scalar1=1.0 / D, scalar2=None,
                              op0=OP.mult))
        vseal(v.tensor_scalar(out=colB, in0=colB, scalar1=1.0 / D, scalar2=None,
                              op0=OP.mult))
        vseal(v.tensor_mul(colA, ln_m, ln_m))
        vseal(v.tensor_sub(colB, colB, colA))
        sdv.wait(s)
        sseal(s.activation(out=colB, in_=colB, func=AF.Ln, bias=1e-5, scale=1.0))
        sseal(s.activation(out=ln_r, in_=colB, func=AF.Exp, scale=-0.5))
        sac.wait(v)
        vseal(v.tensor_mul(ln_b, ln_m, ln_r))
        vseal(v.tensor_scalar(out=ln_b, in0=ln_b, scalar1=-1.0, scalar2=None,
                              op0=OP.mult))
        sdv.wait(s)

    def emit_xnT_to_hbm(src_tiles):
        for ic in range(NCH):
            spe.wait(s)
            sac.inc(s.activation(out=sbfA[:, 0:D], in_=src_tiles[ic],
                                 func=AF.Identity, scale=ln_r[:, ic:ic + 1],
                                 bias=ln_b[:, ic:ic + 1]))
            sac.wait(t)
            sdma.wait(v)
            for dc in range(NCH):
                slot = psT if dc % 2 == 0 else psU
                sdv.wait(t, lag=1)
                t.transpose(slot[:, 0:128], sbfA[:, dc * 128:(dc + 1) * 128], I128)
                spe.inc(t.nop())
                spe.wait(v)
                sdv.inc(v.tensor_copy(sbfB[:, dc * 128:(dc + 1) * 128],
                                      slot[:, 0:128]))
            sdv.wait(sy)
            for dc in range(NCH):
                sdma.inc(sy.dma_start(
                    out=xnT_s[dc * 128:(dc + 1) * 128, ic * 128:(ic + 1) * 128],
                    in_=sbfB[:, dc * 128:(dc + 1) * 128]), 16)

    emit_layernorm(x_tm)
    import os as _os3
    if not _os3.environ.get("SKIP_XNT"):
        emit_xnT_to_hbm(x_tm)
    castload(ROWS[:, 0:D], row_ap(bq_d, D))
    gdma.wait(t)
    nc.all_engine_barrier()
    if upto == 2:
        _emit_dump(nc, REG, dump)
        _finalize(nc, REG)
        return nc

    _p2dbg = bool(_os9.environ.get("P2DEBUG"))
    bqrow = ROWS[:, 0:D]
    for dqb in range(8):
        spe.wait(g)
        castload(WB0[:, 0:4096].rearrange("p (a b) -> p a b", b=256),
                 Wq_d[:, dqb * 256:(dqb + 1) * 256]
                 .rearrange("(a p) b -> p a b", p=128), slow=True)
        gdma.wait(t)
        for ic in range(NCH):
            spe.wait(sy)
            load(WB1[:, 0:2048].rearrange("p (a b) -> p a b", b=128),
                 xnT_s[:, ic * 128:(ic + 1) * 128]
                 .rearrange("(a p) b -> p a b", p=128), eng=sy, slow=True)
            sdma.wait(t)
            sdv.wait(t)
            ps_q = psA[:, 0:256]
            spe.inc(t.matmul(ps_q[:, 0:128], onesbf,
                             bqrow[:, dqb * 256:dqb * 256 + 128],
                             start=True, stop=False))
            spe.inc(t.matmul(ps_q[:, 128:256], onesbf,
                             bqrow[:, dqb * 256 + 128:(dqb + 1) * 256],
                             start=False, stop=False))
            for dm in range(NCH):
                spe.inc(t.matmul(ps_q,
                                 WB1[:, dm * 128:(dm + 1) * 128],
                                 WB0[:, dm * 256:(dm + 1) * 256],
                                 start=False, stop=(dm == NCH - 1)))
            spe.wait(v)
            spe.wait(s)
            if _p2dbg and dqb == 0 and ic == 0:
                v.tensor_copy(DBGB[:, 0:256], ps_q)
            vseal(v.reduce_max(colA[:, 0:2],
                               ps_q.rearrange("p (h q) -> p h q", q=128),
                               axis=AX, apply_absolute_value=True))
            vseal(v.tensor_scalar(out=colA[:, 0:2], in0=colA[:, 0:2], scalar1=INV127,
                                  scalar2=1e-8, op0=OP.mult, op1=OP.max))
            vseal(v.reciprocal(colB[:, 0:2], colA[:, 0:2]))
            v.tensor_scalar(out=sq_all[:, ic * 16 + dqb * 2:ic * 16 + dqb * 2 + 2],
                            in0=colA[:, 0:2], scalar1=SCALEQ, scalar2=None,
                            op0=OP.mult)
            v.tensor_tensor(out=SCRA[:, 0:256].rearrange("p (h q) -> p h q", q=128),
                            in0=ps_q.rearrange("p (h q) -> p h q", q=128),
                            in1=colB[:, 0:2].unsqueeze(2)
                            .broadcast_to((128, 2, 128)), op=OP.mult)
            v.tensor_copy(SCRI[:, 0:256], SCRA[:, 0:256])
            v.scalar_tensor_tensor(out=SCRB[:, 0:256], in0=SCRA[:, 0:256],
                                   scalar=-0.5, in1=SCRI[:, 0:256],
                                   op0=OP.add, op1=OP.is_ge)
            v.tensor_add(SCRB[:, 0:256], SCRI[:, 0:256], SCRB[:, 0:256])
            v.scalar_tensor_tensor(out=SCRD[:, 0:256], in0=SCRA[:, 0:256],
                                   scalar=0.5, in1=SCRI[:, 0:256],
                                   op0=OP.add, op1=OP.is_le)
            v.tensor_sub(SCRB[:, 0:256], SCRB[:, 0:256], SCRD[:, 0:256])
            spe.wait(v)
            sdv.inc(v.tensor_copy(sbfA[:, 0:256], SCRB[:, 0:256]))
            sdv.wait(t)
            sdma.wait(v)
            for hh in range(2):
                slot = psT if hh % 2 == 0 else psU
                sdv.wait(t, lag=1)
                t.transpose(slot[:, 0:128],
                            sbfA[:, hh * 128:(hh + 1) * 128], I128)
                spe.inc(t.nop())
                spe.wait(v)
                sdv.inc(v.tensor_copy(sbfB[:, hh * 128:(hh + 1) * 128],
                                      slot[:, 0:128]))
            sdv.wait(sy)
            for hh in range(2):
                dq0 = dqb * 256 + hh * 128
                sdma.inc(sy.dma_start(
                    out=cqT_s[dq0:dq0 + 128, ic * 128:(ic + 1) * 128],
                    in_=sbfB[:, hh * 128:(hh + 1) * 128]), 16)
    spe.wait(v)
    sdv.inc(v.tensor_scalar(out=nsq_all, in0=sq_all, scalar1=-1.0, scalar2=None,
                            op0=OP.mult))
    nc.all_engine_barrier()
    if upto == 3:
        _emit_dump(nc, REG, dump)
        _finalize(nc, REG)
        return nc

    for h in range(NH):
        sdv.wait(sy)
        spe.wait(sy)
        load(WB0[:, 0:2048], cqT_s[h * 128:(h + 1) * 128, :])
        load(WB1[:, 0:1024], kscT_s[h * 128:(h + 1) * 128, :])
        load(WB2[:, 0:1024].rearrange("p (a b) -> p a b", b=128),
             vtok_s[:, h * 128:(h + 1) * 128].rearrange("(a p) b -> p a b", p=128),
             slow=True)
        sdma.wait(t)
        sdma.wait(s)
        sdma.wait(v)
        for ic in range(NCH):
            par = ic % 2
            psS = psA[:, par * 1024:(par + 1) * 1024]
            sac.wait(t, lag=1)
            spe.inc(t.matmul(psS[:, 0:512], WB0[:, ic * 128:(ic + 1) * 128],
                             WB1[:, 0:512], start=True, stop=True))
            spe.inc(t.matmul(psS[:, 512:1024], WB0[:, ic * 128:(ic + 1) * 128],
                             WB1[:, 512:1024], start=True, stop=True))
            spe.wait(v)
            sqc = sq_all[:, ic * 16 + h:ic * 16 + h + 1]
            nsqc = nsq_all[:, ic * 16 + h:ic * 16 + h + 1]
            vseal(v.reduce_max(colA[:, 0:1], psS, axis=AX))
            vseal(v.tensor_scalar(out=colB[:, 0:1], in0=colA[:, 0:1], scalar1=nsqc,
                                  scalar2=LN255, op0=OP.mult, op1=OP.add))
            sdv.wait(s)
            E = SCRA[:, par * 1024:(par + 1) * 1024]
            sac.inc(s.activation(out=E, in_=psS, func=AF.Exp, scale=sqc,
                                 bias=colB[:, 0:1]))
            sac.wait(v)
            Ri = SCRI[:, par * 1024:(par + 1) * 1024]
            Cg = SCRB[:, par * 1024:(par + 1) * 1024]
            vseal(v.reduce_sum(colC[:, 0:1], E, axis=AX))
            vseal(v.reciprocal(colC[:, 1:2], colC[:, 0:1]))
            v.tensor_copy(Ri, E)
            v.scalar_tensor_tensor(out=Cg, in0=E, scalar=-0.5, in1=Ri,
                                   op0=OP.add, op1=OP.is_ge)
            v.tensor_add(E, Ri, Cg)
            ca = sbfA if par == 0 else sbfB
            sdma.wait(v)
            adma.wait(v)
            sdv.inc(v.tensor_scalar(out=ca[:, 0:1024], in0=E, scalar1=colC[:, 1:2],
                                    scalar2=None, op0=OP.mult))
            sdv.wait(sy)
            sdv.wait(s)
            caT = ca[:, 1024:2048]
            for jc in range(8):
                src = ca[:, jc * 128:(jc + 1) * 128]
                dst = caT[:, jc * 128:(jc + 1) * 128]
                if jc % 2 == 0:
                    sdma.inc(sy.dma_start_transpose(dst, src), 16)
                else:
                    adma.inc(s.dma_start_transpose(dst, src), 16)
            sdma.wait(t)
            adma.wait(t)
            sdv.wait(t)
            psO = psB[:, par * 512:par * 512 + 128]
            for jc in range(8):
                spe.inc(t.matmul(psO, WB2[:, jc * 128:(jc + 1) * 128],
                                 caT[:, jc * 128:(jc + 1) * 128],
                                 start=(jc == 0), stop=(jc == 7)))
            spe.wait(v)
            sdv.inc(v.tensor_copy(sbfA[:, 1024:1152] if par == 0 else
                                  sbfB[:, 1024:1152], psO))
            sdv.wait(sy)
            sdma.inc(sy.dma_start(
                out=otT_s[h * 128:(h + 1) * 128, ic * 128:(ic + 1) * 128],
                in_=(sbfA if par == 0 else sbfB)[:, 1024:1152]), 16)
    castload(ROWS[:, 0:D], row_ap(bp_d, D))
    gdma.wait(t)
    nc.all_engine_barrier()
    if upto == 4:
        _emit_dump(nc, REG, dump)
        _finalize(nc, REG)
        return nc

    bprow = ROWS[:, 0:D]
    for ig in range(8):
        sdv.wait(sy)
        spe.wait(sy)
        for icb in range(2):
            ic = ig * 2 + icb
            load(WB0[:, icb * 2048:icb * 2048 + 2048]
                 .rearrange("p (a b) -> p a b", b=128),
                 otT_s[:, ic * 128:(ic + 1) * 128]
                 .rearrange("(a p) b -> p a b", p=128), eng=sy, slow=True)
        sdma.wait(t)
        for dpb in range(4):
            sdv.wait(t)
            for dc in range(NCH):
                wb = WB1 if dc % 2 == 0 else WB2
                spe.wait(g)
                castload(wb[:, 0:512],
                         Wp_d[dc * 128:(dc + 1) * 128, dpb * 512:(dpb + 1) * 512])
                gdma.wait(t)
                for icb in range(2):
                    psP = psA[:, icb * 512:(icb + 1) * 512]
                    if dc == 0:
                        spe.inc(t.matmul(psP[:, 0:128], onesbf,
                                         bprow[:, dpb * 512:dpb * 512 + 128],
                                         start=True, stop=False))
                        spe.inc(t.matmul(psP[:, 128:512], onesbf,
                                         bprow[:, dpb * 512 + 128:(dpb + 1) * 512],
                                         start=False, stop=False))
                    spe.inc(t.matmul(psP,
                                     WB0[:, icb * 2048 + dc * 128:icb * 2048 + (dc + 1) * 128],
                                     wb[:, 0:512], start=False, stop=(dc == NCH - 1)))
            spe.wait(v)
            for icb in range(2):
                ic = ig * 2 + icb
                psP = psA[:, icb * 512:(icb + 1) * 512]
                xs = x_tm[ic][:, dpb * 512:(dpb + 1) * 512]
                sdv.inc(v.scalar_tensor_tensor(out=xs, in0=psP, scalar=1.0, in1=xs,
                                               op0=OP.mult, op1=OP.add))
    nc.all_engine_barrier()
    if upto == 5:
        _emit_dump(nc, REG, dump)
        _finalize(nc, REG)
        return nc

    xsl = [ARN[:, i * 512:(i + 1) * 512] for i in range(NCH)]
    v2 = [ARN[:, 8192 + i * 512:8192 + (i + 1) * 512] for i in range(DCH)]
    w2c = [ARN[:, 24576 + i * 512:24576 + (i + 1) * 512] for i in range(NCH)]
    for ft in range(4):
        sdv.wait(sy)
        spe.wait(sy)
        for dm in range(NCH):
            load(xsl[dm], xnT_s[dm * 128:(dm + 1) * 128, ft * 512:(ft + 1) * 512])
        sdma.wait(t)
        for half in range(2):
            sdv.wait(t)
            for hcl in range(DCH):
                hc = half * DCH + hcl
                spe.wait(g)
                sdv.wait(g)
                sac.wait(g)
                castload(WB1[:, 0:2048].rearrange("p (a b) -> p a b", b=128),
                         W1T_d[:, hc * 128:(hc + 1) * 128]
                         .rearrange("(a p) b -> p a b", p=128), slow=True)
                castload(WB2[:, 0:2048].rearrange("p (a b) -> p a b", b=128),
                         W3T_d[:, hc * 128:(hc + 1) * 128]
                         .rearrange("(a p) b -> p a b", p=128), slow=True)
                gdma.wait(t)
                sac.wait(t)
                sdv.wait(t)
                for dm in range(NCH):
                    spe.inc(t.matmul(psA[:, 1024:1536], WB1[:, dm * 128:(dm + 1) * 128],
                                     xsl[dm], start=(dm == 0), stop=(dm == NCH - 1)))
                for dm in range(NCH):
                    spe.inc(t.matmul(psA[:, 1536:2048], WB2[:, dm * 128:(dm + 1) * 128],
                                     xsl[dm], start=(dm == 0), stop=(dm == NCH - 1)))
                spe.wait(s)
                sac.inc(s.activation(out=sbfA[:, 0:512], in_=psA[:, 1024:1536],
                                     func=AF.Silu))
                sac.wait(v)
                sdv.inc(v.tensor_mul(sbfB[:, 0:512], sbfA[:, 0:512],
                                     psA[:, 1536:2048]))
                sdv.wait(t)
                spe.inc(t.matmul(psB[:, 0:512], H128, sbfB[:, 0:512],
                                 start=True, stop=True))
                spe.wait(v)
                sdv.inc(v.tensor_copy(v2[hcl], psB[:, 0:512]))
                sdv.wait(t)
            for dpb in range(4):
                for sub in range(2):
                    sdv.wait(sy)
                    spe.wait(sy)
                    for hcl in range(NCH):
                        hh = half * DCH + sub * NCH + hcl
                        load(w2c[hcl],
                             w2b_s[hh * 128:(hh + 1) * 128, dpb * 512:(dpb + 1) * 512])
                    sdma.wait(t)
                    for icq in range(4):
                        ic = ft * 4 + icq
                        psF = psA[:, icq * 512:(icq + 1) * 512]
                        if sub == 0 and icq == 0:
                            sdv.wait(t)
                        for hcl in range(NCH):
                            spe.inc(t.matmul(
                                psF, v2[sub * NCH + hcl][:, icq * 128:(icq + 1) * 128],
                                w2c[hcl], start=(hcl == 0 and sub == 0),
                                stop=(hcl == NCH - 1 and sub == 1)))
                        if sub == 1:
                            spe.wait(v)
                            xs = x_tm[ic][:, dpb * 512:(dpb + 1) * 512]
                            sdv.inc(v.scalar_tensor_tensor(
                                out=xs, in0=psF, scalar=1.0, in1=xs,
                                op0=OP.mult, op1=OP.add))
    nc.all_engine_barrier()
    if upto == 6:
        _emit_dump(nc, REG, dump)
        _finalize(nc, REG)
        return nc

    emit_layernorm(x_tm)
    castload(ROWS[:, 0:D], row_ap(bias1_d, D))
    gdma.wait(t)
    spe.wait(v)
    b1bc = ARN[:, 28672:32768].bitcast(F32)
    sdv.wait(t)
    for gc in range(4):
        spe.inc(t.matmul(psA[:, gc * 512:(gc + 1) * 512], onesbf,
                         ROWS[:, gc * 512:(gc + 1) * 512], start=True, stop=True))
    spe.wait(v)
    sdv.inc(v.tensor_copy(b1bc, psA))
    nc.all_engine_barrier()
    if upto == 7:
        _emit_dump(nc, REG, dump)
        _finalize(nc, REG)
        return nc

    hsT = [ARN[:, i * 256:(i + 1) * 256] for i in range(DCH)]
    gtT = [ARN[:, 8192 + i * 256:8192 + (i + 1) * 256] for i in range(DCH)]
    yT = [ARN[:, 16384 + i * 256:16384 + (i + 1) * 256] for i in range(DCH)]
    xqT = [ARN[:, 24576 + i * 256:24576 + (i + 1) * 256] for i in range(NCH)]
    xn1T = xqT
    sdv.inc(v.memset(CARR, 0.0))
    hs_stash = sbt("hs_stash", [128, DCH * 4], BF16)
    sdv.inc(v.memset(hs_stash, 0.0))
    sdv.wait(t); sdv.wait(s); sdv.wait(g)

    for mt in range(NMT):
        ics = [mt * (MT // 128) + k for k in range(MT // 128)]
        for k, ic in enumerate(ics):
            spe.wait(s)
            sdv.wait(s)
            sac.inc(s.activation(out=sbfA[:, 0:D], in_=x_tm[ic], func=AF.Identity,
                                 scale=ln_r[:, ic:ic + 1], bias=ln_b[:, ic:ic + 1]))
            sac.wait(t)
            for dm in range(NCH):
                slot = psT if dm % 2 == 0 else psU
                sdv.wait(t, lag=1)
                t.transpose(slot[:, 0:128], sbfA[:, dm * 128:(dm + 1) * 128], I128)
                spe.inc(t.nop())
                spe.wait(v)
                sdv.inc(v.tensor_copy(xn1T[dm][:, k * 128:(k + 1) * 128],
                                      slot[:, 0:128]))
            sdv.wait(t)
        for ec in range(2 * DCH):
            wb = WB1 if ec % 2 == 0 else WB2
            spe.wait(sy)
            load(wb[:, 0:2048].rearrange("p (a b) -> p a b", b=128),
                 winb_s[:, ec * 128:(ec + 1) * 128]
                 .rearrange("(a p) b -> p a b", p=128), eng=sy, slow=True)
            sdma.wait(t)
            par = ec % 2
            psW = psA[:, 1024 + par * 512:1024 + par * 512 + MT]
            sdv.wait(t)
            sac.wait(t)
            for dm in range(NCH):
                spe.inc(t.matmul(psW, wb[:, dm * 128:(dm + 1) * 128],
                                 xn1T[dm][:, 0:MT], start=(dm == 0),
                                 stop=(dm == NCH - 1)))
            spe.wait(v)
            if ec < DCH:
                dc = ec
                st = hs_stash[:, dc * 4:dc * 4 + 4]
                v.tensor_copy(SCRB[:, 0:3], st[:, 1:4])
                vseal(v.tensor_copy(SCRB[:, 3:MT + 3], psW))
                vseal(v.tensor_copy(st[:, 1:4], SCRB[:, MT:MT + 3]))
                v.tensor_scalar(out=SCRA[:, 0:MT], in0=SCRB[:, 3:MT + 3],
                                scalar1=ccols[:, 96 + dc:96 + dc + 1],
                                scalar2=None, op0=OP.mult)
                ii_ = None
                for kk in range(3):
                    ii_ = v.scalar_tensor_tensor(
                        out=SCRA[:, 0:MT], in0=SCRB[:, kk:MT + kk],
                        scalar=ccols[:, kk * 32 + dc:kk * 32 + dc + 1],
                        in1=SCRA[:, 0:MT], op0=OP.mult, op1=OP.add)
                sdv.inc(ii_)
                sdv.wait(s)
                sac.inc(s.activation(out=hsT[dc], in_=SCRA[:, 0:MT], func=AF.Silu,
                                     bias=bcv_cols[:, dc:dc + 1], scale=1.0))
            else:
                dc = ec - DCH
                sdv.inc(v.tensor_copy(gtT[dc], psW))
        sac.wait(t)
        sdv.wait(t)
        sac.wait(v)
        for dc in range(DCH):
            wb = WB1 if dc % 2 == 0 else WB2
            spe.wait(g)
            castload(wb[:, 0:160], WxT_d[dc * 128:(dc + 1) * 128, :])
            gdma.wait(t)
            spe.inc(t.matmul(psB[:, 0:MT], wb[:, 0:128], hsT[dc],
                             start=(dc == 0), stop=(dc == DCH - 1)))
            spe.inc(t.matmul(psB[0:32, 512:512 + MT], wb[:, 128:160], hsT[dc],
                             start=(dc == 0), stop=(dc == DCH - 1)))
        spe.wait(v)
        sdv.inc(v.tensor_copy(sbfB[:, 0:MT], psB[:, 0:MT]))
        sdv.inc(v.tensor_copy(BCR, psB[0:32, 512:512 + MT]))
        sdv.wait(g)
        gdma.inc(g.dma_start(out=ROWS[:, 0:16 * MT], in_=BCR[0:16, :]), 16)
        gdma.inc(g.dma_start(out=ROWSC[:, 0:16 * MT], in_=BCR[16:32, :]), 16)
        gdma.wait(t)
        sdv.wait(t)
        for dc in range(DCH):
            wb = WB1 if dc % 2 == 0 else WB2
            spe.wait(g)
            castload(wb[:, 0:128], WdtT_d[:, dc * 128:(dc + 1) * 128], slow=True)
            gdma.wait(t)
            sac.wait(t)
            sdv.wait(t)
            spe.inc(t.matmul(psB[:, 512:512 + MT], wb[:, 0:128], sbfB[:, 0:MT],
                             start=True, stop=True))
            spe.wait(s)
            spe.wait(v)
            sac.wait(v)
            sseal(s.activation(out=SCRB[:, 256:256 + MT], in_=psB[:, 512:512 + MT],
                               func=AF.Abs, bias=bdt_cols[:, dc:dc + 1], scale=1.0))
            sseal(s.activation(out=SCRB[:, 256:256 + MT], in_=SCRB[:, 256:256 + MT],
                               func=AF.Exp, scale=-1.0))
            sseal(s.activation(out=SCRB[:, 256:256 + MT], in_=SCRB[:, 256:256 + MT],
                               func=AF.Ln, bias=1.0, scale=1.0))
            sac.wait(v)
            v.tensor_scalar(out=SCRA[:, 0:MT], in0=psB[:, 512:512 + MT],
                            scalar1=bdt_cols[:, dc:dc + 1], scalar2=0.0,
                            op0=OP.add, op1=OP.max)
            vseal(v.tensor_add(SCRA[:, 0:MT], SCRA[:, 0:MT], SCRB[:, 256:256 + MT]))
            sdv.wait(g)
            sdv.inc(v.tensor_mul(sbfA[:, 0:MT], SCRA[:, 0:MT], hsT[dc]))
            sdv.wait(s)
            sdv.wait(g)
            for n in range(NST):
                sdv.wait(t)
                spe.inc(t.matmul(psA[:, 0:MT], onesbf,
                                 ROWS[:, n * MT:(n + 1) * MT],
                                 start=True, stop=True))
                spe.inc(t.matmul(psA[:, 512:512 + MT], onesbf,
                                 ROWSC[:, n * MT:(n + 1) * MT], start=True, stop=True))
                sdv.wait(s)
                sac.inc(s.activation(out=SCRB[:, 0:MT], in_=SCRA[:, 0:MT],
                                     func=AF.Exp,
                                     scale=acols[:, dc * 16 + n:dc * 16 + n + 1]))
                spe.wait(v)
                v.tensor_copy(SCRB[:, 256 + 256:256 + 256 + MT] if False else
                              SCRB[:, 512:512 + MT], psA[:, 0:MT])
                sdv.inc(v.tensor_copy(SCRB[:, 768:768 + MT] if False else
                                      SCRD[:, 256:256 + MT], psA[:, 512:512 + MT]))
                sdv.wait(g)
                sgp.inc(g.tensor_mul(SCRD[:, 0:MT], sbfA[:, 0:MT],
                                     SCRB[:, 512:512 + MT]))
                sgp.wait(v)
                sac.wait(v)
                cini = CARR[:, dc * 16 + n:dc * 16 + n + 1]
                vseal(v.tensor_tensor_scan(out=SCRB[:, 768:768 + MT],
                                           data0=SCRB[:, 0:MT],
                                           data1=SCRD[:, 0:MT], initial=cini,
                                           op0=OP.mult, op1=OP.add))
                vseal(v.tensor_copy(cini, SCRB[:, 768 + MT - 1:768 + MT]))
                spe.wait(v)
                sdv.inc(v.tensor_mul(sbfB[:, 256:256 + MT], SCRB[:, 768:768 + MT],
                                     SCRD[:, 256:256 + MT]))
                sdv.wait(t)
                spe.inc(t.matmul(psB[:, 0:MT], I128, sbfB[:, 256:256 + MT],
                                 start=(n == 0), stop=(n == NST - 1)))
            spe.wait(s)
            sac.inc(s.activation(out=sbfB[:, 512:512 + MT], in_=gtT[dc],
                                 func=AF.Silu))
            sac.wait(v)
            spe.wait(v)
            v.scalar_tensor_tensor(out=SCRB[:, 0:MT], in0=hsT[dc],
                                   scalar=dv_cols[:, dc:dc + 1], in1=psB[:, 0:MT],
                                   op0=OP.mult, op1=OP.add)
            sdv.inc(v.tensor_mul(yT[dc], SCRB[:, 0:MT], sbfB[:, 512:512 + MT]))
            sdv.wait(t)
        for k, ic in enumerate(ics):
            spe.wait(s)
            sdv.wait(s)
            sac.inc(s.activation(out=sbfA[:, 0:D], in_=x_tm[ic], func=AF.Identity,
                                 scale=ln_r[:, ic:ic + 1], bias=ln_b[:, ic:ic + 1]))
            sseal(s.activation(out=SCRB, in_=sbfA[:, 0:D], func=AF.Square))
            sac.wait(v)
            vseal(v.reduce_sum(colA[:, 0:1], SCRB, axis=AX))
            vseal(v.tensor_scalar(out=colA[:, 0:1], in0=colA[:, 0:1],
                                  scalar1=1.0 / D, scalar2=1e-8,
                                  op0=OP.mult, op1=OP.add))
            sdv.wait(s)
            sseal(s.activation(out=colA[:, 1:2], in_=colA[:, 0:1], func=AF.Ln))
            sseal(s.activation(out=colA[:, 2:3], in_=colA[:, 1:2], func=AF.Exp,
                               scale=0.5))
            sac.wait(v)
            vseal(v.reduce_max(colA[:, 3:4], sbfA[:, 0:D], axis=AX,
                               apply_absolute_value=True))
            vseal(v.reciprocal(colA[:, 4:5], colA[:, 2:3]))
            vseal(v.tensor_mul(colA[:, 5:6], colA[:, 3:4], colA[:, 4:5]))
            vseal(v.tensor_scalar(out=colA[:, 5:6], in0=colA[:, 5:6], scalar1=INV127,
                                  scalar2=1e-8, op0=OP.mult, op1=OP.add))
            vseal(v.tensor_mul(colA[:, 6:7], colA[:, 2:3], colA[:, 5:6]))
            vseal(v.reciprocal(colA[:, 7:8], colA[:, 6:7]))
            vseal(v.tensor_mul(colA[:, 8:9], colA[:, 3:4], colA[:, 4:5]))
            vseal(v.tensor_scalar(out=colA[:, 8:9], in0=colA[:, 8:9], scalar1=INV127,
                                  scalar2=None, op0=OP.mult))
            for hb in range(2):
                sl = slice(hb * 1024, (hb + 1) * 1024)
                v.tensor_scalar(out=SCRB[:, sl], in0=sbfA[:, sl],
                                scalar1=colA[:, 7:8], scalar2=None, op0=OP.mult)
                v.tensor_copy(SCRI[:, sl], SCRB[:, sl])
                v.scalar_tensor_tensor(out=SCRD, in0=SCRB[:, sl], scalar=-0.5,
                                       in1=SCRI[:, sl], op0=OP.add, op1=OP.is_ge)
                v.tensor_add(SCRD, SCRI[:, sl], SCRD)
                v.scalar_tensor_tensor(out=SCRB[:, sl], in0=SCRB[:, sl], scalar=0.5,
                                       in1=SCRI[:, sl], op0=OP.add, op1=OP.is_le)
                v.tensor_sub(SCRD, SCRD, SCRB[:, sl])
                spe.wait(v)
                sdv.inc(v.tensor_copy(sbfB[:, sl], SCRD))
            sdv.wait(t)
            for dm in range(NCH):
                slot = psT if dm % 2 == 0 else psU
                sdv.wait(t, lag=1)
                t.transpose(slot[:, 0:128], sbfB[:, dm * 128:(dm + 1) * 128], I128)
                spe.inc(t.nop())
                spe.wait(v)
                sdv.inc(v.tensor_copy(xqT[dm][:, k * 128:(k + 1) * 128],
                                      slot[:, 0:128]))
            sdv.wait(t)
            for kc in range(NCH):
                wb = WB1 if kc % 2 == 0 else WB2
                spe.wait(sy)
                load(wb[:, 0:2048], wtT_s[kc * 128:(kc + 1) * 128, :], eng=sy)
                sdma.wait(t)
                if kc == 0:
                    sdv.wait(t)
                for gc in range(4):
                    spe.inc(t.matmul(psA[:, gc * 512:(gc + 1) * 512],
                                     xqT[kc][:, k * 128:(k + 1) * 128],
                                     wb[:, gc * 512:(gc + 1) * 512],
                                     start=(kc == 0), stop=(kc == NCH - 1)))
            spe.wait(v)
            spe.wait(s)
            if DBGB is not None and mt == 0 and k == 0:
                v.tensor_copy(DBGB[:, 0:512], psA[:, 1536:2048])
            sac.inc(s.activation(out=sbfA[:, 0:D], in_=psA, func=AF.Identity,
                                 scale=colA[:, 8:9]))
            sac.wait(t)
            sac.wait(v)
            for dc in range(DCH):
                wb = WB1 if dc % 2 == 0 else WB2
                spe.wait(sy)
                load(wb[:, 0:2048], woutb_s[dc * 128:(dc + 1) * 128, :], eng=sy)
                sdma.wait(t)
                for gc in range(4):
                    spe.inc(t.matmul(psA[:, gc * 512:(gc + 1) * 512],
                                     yT[dc][:, k * 128:(k + 1) * 128],
                                     wb[:, gc * 512:(gc + 1) * 512],
                                     start=(dc == 0), stop=(dc == DCH - 1)))
            spe.wait(v)
            if DBGB is not None and mt == 0 and k == 0:
                v.tensor_copy(DBGB[:, 512:1024], psA[:, 1536:2048])
            sout.wait(v)
            v.scalar_tensor_tensor(out=SCRA, in0=psA, scalar=1.0, in1=b1bc,
                                   op0=OP.mult, op1=OP.add)
            v.tensor_add(SCRA, SCRA, x_tm[ic])
            sdv.inc(v.tensor_add(SCRA, SCRA, sbfA[:, 0:D]))
            sdv.wait(sy)
            sout.inc(sy.dma_start(out=out_d[ic * 128:(ic + 1) * 128, :], in_=SCRA), 16)
    nc.all_engine_barrier()
    sout.wait(sy)
    _emit_dump(nc, REG, dump)
    return nc


def _emit_dump(nc, R, dump):
    if not dump:
        return
    nc.all_engine_barrier()
    sy = R["sy"]
    sdma = R["sdma"]
    for name in dump:
        obj = R[name]
        if isinstance(obj, list):
            w = obj[0].shape[1]
            dt = obj[0].dtype
            dd = nc.dram_tensor(f"dbg_{name}", [len(obj), 128, w], dt,
                                kind="ExternalOutput").ap()
            for i, ap in enumerate(obj):
                sdma.inc(sy.dma_start(out=dd[i], in_=ap), 16)
        else:
            sh = list(obj.shape)
            dd = nc.dram_tensor(f"dbg_{name}", sh, obj.dtype,
                                kind="ExternalOutput").ap()
            sdma.inc(sy.dma_start(out=dd, in_=obj), 16)
    sdma.wait(sy)


def _finalize(nc, R):
    nc.all_engine_barrier()
    sy, v = R["sy"], R["v"]
    sdv, sout = R["sdv"], R["sout"]
    out_d, x_tm = R["out_d"], R["x_tm"]
    scro = R["SCRA"]
    for ic in range(NCH):
        sout.wait(v)
        sdv.inc(v.tensor_copy(scro, x_tm[ic]))
        sdv.wait(sy)
        sout.inc(sy.dma_start(out=out_d[ic * 128:(ic + 1) * 128, :], in_=scro), 16)
    sout.wait(sy)


def _host_layout(inputs):
    f32 = np.float32
    x = np.ascontiguousarray(np.asarray(inputs["L0_attn_x"], dtype=f32))
    cond = np.asarray(inputs["L0_attn_cond"], dtype=f32).reshape(KV, D)
    condT = np.ascontiguousarray(cond.T)
    W2 = np.asarray(inputs["L0_ffn_W2"], dtype=f32)
    W2TR = np.ascontiguousarray(
        W2.T.reshape(64, 128, D).transpose(1, 0, 2).reshape(FH, D))
    H128 = _had(128).astype(ml_dtypes.bfloat16)
    H64 = _had(64) * (2.0 ** -7)
    H64B = np.zeros((128, 128), np.float64)
    H64B[:64, :64] = H64
    H64B[64:, 64:] = H64
    common = {
        "condT": condT,
        "Wq": np.ascontiguousarray(np.asarray(inputs["L0_attn_Wq"], dtype=f32)),
        "bq": np.ascontiguousarray(np.asarray(inputs["L0_attn_bq"], dtype=f32)),
        "Wkv": np.ascontiguousarray(np.asarray(inputs["L0_attn_Wkv"], dtype=f32)),
        "bkv": np.ascontiguousarray(np.asarray(inputs["L0_attn_bkv"], dtype=f32)),
        "Wp": np.ascontiguousarray(np.asarray(inputs["L0_attn_Wp"], dtype=f32)),
        "bp": np.ascontiguousarray(np.asarray(inputs["L0_attn_bp"], dtype=f32)),
        "W1T": np.ascontiguousarray(np.asarray(inputs["L0_ffn_W1"], dtype=f32).T),
        "W3T": np.ascontiguousarray(np.asarray(inputs["L0_ffn_W3"], dtype=f32).T),
        "W2TR": W2TR,
        "WinT": np.ascontiguousarray(np.asarray(inputs["L1_attn_Win"], dtype=f32).T),
        "Wconv": np.ascontiguousarray(
            np.asarray(inputs["L1_attn_Wconv"], dtype=f32).reshape(DI, 4)),
        "bconv": np.ascontiguousarray(np.asarray(inputs["L1_attn_bconv"], dtype=f32)),
        "WxT": np.ascontiguousarray(np.asarray(inputs["L1_attn_Wx"], dtype=f32).T),
        "WdtT": np.ascontiguousarray(np.asarray(inputs["L1_attn_Wdt"], dtype=f32).T),
        "bdt": np.ascontiguousarray(np.asarray(inputs["L1_attn_bdt"], dtype=f32)),
        "WoutT": np.ascontiguousarray(np.asarray(inputs["L1_attn_Wout"], dtype=f32).T),
        "Alog": np.ascontiguousarray(np.asarray(inputs["L1_attn_A_log"], dtype=f32)),
        "Dvec": np.ascontiguousarray(np.asarray(inputs["L1_attn_D"], dtype=f32)),
        "wgt": np.ascontiguousarray(np.asarray(inputs["L1_ffn_weight"], dtype=f32)),
        "bias1": np.ascontiguousarray(np.asarray(inputs["L1_ffn_bias"], dtype=f32)),
        "onesbf": np.ones((1, 128), ml_dtypes.bfloat16),
        "onesf": np.ones((1, 128), f32),
        "I128": np.eye(128).astype(ml_dtypes.bfloat16),
        "H128": H128,
        "H64B": H64B.astype(ml_dtypes.bfloat16),
    }
    return [dict(common, x=np.ascontiguousarray(x[i])) for i in range(8)]


def kernel(**inputs):
    in_maps = _host_layout(inputs)
    nc = build(UPTO)
    res = run_bass_kernel_spmd(nc, in_maps, core_ids=list(range(8)))
    return np.stack([np.asarray(r["out"], dtype=np.float32) for r in res.results])


# version 03
# speedup vs baseline: 1.2972x; 1.2972x over previous
import numpy as np
import ml_dtypes
import concourse.bass as bass
import concourse.mybir as mybir
from concourse.bass_utils import run_bass_kernel_spmd

F32 = mybir.dt.float32
BF16 = mybir.dt.bfloat16
I32 = mybir.dt.int32
OP = mybir.AluOpType
AF = mybir.ActivationFunctionType
AX = mybir.AxisListType.X

T = 2048
D = 2048
NCH = 16
NH = 16
HD = 128
KV = 1024
FH = 8192
DI = 4096
DCH = 32
NST = 16
MT = 256
NMT = T // MT
FT = 512
SQ8192 = float(np.float32(1.0 / np.sqrt(np.float64(8192.0)) / (2.0 ** -7)))
LN255 = float(np.float32(np.log(np.float64(255.0))))
INV127 = float(np.float32(1.0 / np.float64(127.0)))
SCALEQ = float(np.float32(1.0 / np.sqrt(np.float64(128.0))))

UPTO = 99


def _had(n):
    H = np.ones((n, n), np.float64)
    for i in range(n):
        for j in range(n):
            if bin(i & j).count("1") % 2:
                H[i, j] = -1.0
    return H


class Sem:
    def __init__(self, nc, name):
        self.h = nc.alloc_semaphore(name)
        self.n = 0

    def inc(self, instr, k=1):
        instr.then_inc(self.h, k)
        self.n += k
        return instr

    def wait(self, eng, lag=0):
        if self.n - lag > 0:
            eng.wait_ge(self.h, self.n - lag)


def row_ap(vec_d, n, off=0):
    return bass.AP(vec_d.tensor, vec_d.offset + off, [[n, 1], [1, n]])


def build(upto=99, dump=(), skip_p1=False, skip_prep=False):
    nc = bass.Bass("TRN2", target_bir_lowering=False, detect_race_conditions=False)
    REG = {}

    def din(name, shape, dt=F32):
        return nc.dram_tensor(name, list(shape), dt, kind="ExternalInput").ap()

    x_d = din("x", [T, D])
    condT_d = din("condT", [D, KV])
    Wq_d = din("Wq", [D, D])
    bq_d = din("bq", [D])
    Wkv_d = din("Wkv", [D, 2 * D])
    bkv_d = din("bkv", [2 * D])
    Wp_d = din("Wp", [D, D])
    bp_d = din("bp", [D])
    W1T_d = din("W1T", [D, FH])
    W3T_d = din("W3T", [D, FH])
    W2TR_d = din("W2TR", [FH, D])
    WinT_d = din("WinT", [D, 2 * DI])
    Wconv_d = din("Wconv", [DI, 4])
    bconv_d = din("bconv", [DI])
    WxT_d = din("WxT", [DI, 160])
    WdtT_d = din("WdtT", [128, DI])
    bdt_d = din("bdt", [DI])
    WoutT_d = din("WoutT", [DI, D])
    Alog_d = din("Alog", [DI, NST])
    Dv_d = din("Dvec", [DI])
    wgt_d = din("wgt", [D, D])
    bias1_d = din("bias1", [D])
    onesbf_d = din("onesbf", [1, 128], BF16)
    onesf_d = din("onesf", [1, 128])
    I128_d = din("I128", [128, 128], BF16)
    H128_d = din("H128", [128, 128], BF16)
    H64B_d = din("H64B", [128, 128], BF16)

    out_d = nc.dram_tensor("out", [T, D], F32, kind="ExternalOutput").ap()

    def dscr(name, shape, dt=BF16):
        return nc.dram_tensor(name, list(shape), dt, kind="Internal").ap()

    xnT_s = dscr("xnT_s", [D, T])
    w2b_s = dscr("w2b_s", [FH, D])
    kscT_s = dscr("kscT_s", [D, KV])
    vtok_s = dscr("vtok_s", [KV, D])
    cqT_s = dscr("cqT_s", [D, T])
    otT_s = dscr("otT_s", [D, T])
    wtT_s = dscr("wtT_s", [D, D])
    winb_s = dscr("winb_s", [D, 2 * DI])
    woutb_s = dscr("woutb_s", [DI, D])

    def sbt(name, shape, dt):
        return nc.alloc_sbuf_tensor(name, list(shape), dt).ap()

    x_tm = [sbt(f"x_tm{i}", [128, D], BF16) for i in range(NCH)]
    ARN = sbt("ARN", [128, 32768], BF16)
    ROWS = sbt("ROWS", [1, 4096], BF16)
    SCRA = sbt("SCRA", [128, D], F32)
    SCRB = sbt("SCRB", [128, D], F32)
    SCRD = sbt("SCRD", [128, 1024], F32)
    SCRI = sbt("SCRI", [128, D], I32)
    sbfA = sbt("sbfA", [128, D], BF16)
    sbfB = sbt("sbfB", [128, D], BF16)
    WB0 = sbt("WB0", [128, 4096], BF16)
    WB1 = sbt("WB1", [128, 2048], BF16)
    WB2 = sbt("WB2", [128, 2048], BF16)
    onesbf = sbt("onesbf_t", [1, 128], BF16)
    onesf = sbt("onesf_t", [1, 128], F32)
    I128 = sbt("I128_t", [128, 128], BF16)
    H128 = sbt("H128_t", [128, 128], BF16)
    H64B = sbt("H64B_t", [128, 128], BF16)
    ln_m = sbt("ln_m", [128, NCH], F32)
    ln_r = sbt("ln_r", [128, NCH], F32)
    ln_b = sbt("ln_b", [128, NCH], F32)
    colA = sbt("colA", [128, NCH], F32)
    colB = sbt("colB", [128, NCH], F32)
    colC = sbt("colC", [128, NCH], F32)
    sq_all = sbt("sq_all", [128, 256], F32)
    nsq_all = sbt("nsq_all", [128, 256], F32)
    acols = sbt("acols", [128, 512], F32)
    ccols = sbt("ccols", [128, 128], F32)
    CARR = sbt("CARR", [128, 512], F32)
    BCR = sbt("BCR", [32, MT], BF16)

    import os as _osd
    DBGB = (sbt("DBGB", [128, 1024], BF16)
            if (_osd.environ.get("P1DEBUG") or _osd.environ.get("V1DEBUG")
                or _osd.environ.get("P2DEBUG")) else None)
    ROWSC = sbt("ROWSC", [1, 4096], BF16)
    psA = nc.alloc_psum_tensor("psA", [128, 2048], F32).ap()
    psB = nc.alloc_psum_tensor("psB", [128, 1024], F32).ap()
    psT = nc.alloc_psum_tensor("psT", [128, 1024], BF16).ap()
    psU = nc.alloc_psum_tensor("psU", [128, 1024], BF16).ap()

    v, g, s, t, sy = nc.vector, nc.gpsimd, nc.scalar, nc.tensor, nc.sync

    sdma = Sem(nc, "sdma")
    gdma = Sem(nc, "gdma")
    adma = Sem(nc, "adma")
    spe = Sem(nc, "spe")
    sdv = Sem(nc, "sdv")
    sac = Sem(nc, "sac")
    sgp = Sem(nc, "sgp")
    sout = Sem(nc, "sout")

    REG.update(locals())

    def vseal(instr):
        sdv.inc(instr)
        sdv.wait(v)
        return instr

    def sseal(instr):
        sac.inc(instr)
        sac.wait(s)
        return instr

    def castload(dst_ap, src_ap, slow=False):
        return gdma.inc(g.dma_start(out=dst_ap, in_=src_ap,
                                    allow_slow_non_contiguous=slow), 16)

    def load(dst_ap, src_ap, eng=None, sem=None, slow=False):
        e = eng or sy
        sm = sem or sdma
        return sm.inc(e.dma_start(out=dst_ap, in_=src_ap,
                                  allow_slow_non_contiguous=slow), 16)

    def load_cols(dst_ap, vec_d, n, off=0):
        src = bass.AP(vec_d.tensor, vec_d.offset + off, [[1, 128], [128, n]])
        return sdma.inc(sy.dma_start(out=dst_ap, in_=src,
                                     allow_slow_non_contiguous=True), 16)

    constp = sbt("constp", [128, 8], F32)
    cvals = [1.0, -1.0, 0.5, -0.5, 1e-5]
    for i_, val_ in enumerate(cvals):
        v.memset(constp[:, i_:i_ + 1], val_)
        nc.const_aps.register(F32, val_, constp[:, i_:i_ + 1])
    sdv.inc(v.nop())
    sdv.wait(s)

    load(onesbf, onesbf_d)
    load(onesf, onesf_d)
    load(I128, I128_d)
    load(H128, H128_d)
    load(H64B, H64B_d)
    
    import os as _os4
    if not _os4.environ.get("SKIP_COLS"):
      for k in range(4):
        src = bass.AP(Wconv_d.tensor, Wconv_d.offset + k, [[4, 128], [512, 32]])
        sdma.inc(sy.dma_start(out=ccols[:, k * 32:(k + 1) * 32], in_=src,
                              allow_slow_non_contiguous=True), 16)
    bdt_cols = sbt("bdt_cols", [128, 32], F32)
    bcv_cols = sbt("bcv_cols", [128, 32], F32)
    dv_cols = sbt("dv_cols", [128, 32], F32)
    if not _os4.environ.get("SKIP_COLS"):
        load_cols(bdt_cols, bdt_d, 32)
        load_cols(bcv_cols, bconv_d, 32)
        load_cols(dv_cols, Dv_d, 32)
        srcA = bass.AP(Alog_d.tensor, Alog_d.offset,
                       [[NST, 128], [128 * NST, 32], [1, NST]])
        sdma.inc(sy.dma_start(out=acols, in_=srcA, allow_slow_non_contiguous=True), 16)
    for i in range(NCH):
        castload(x_tm[i], x_d[i * 128:(i + 1) * 128, :])
    sdma.wait(sy); sdma.wait(t); sdma.wait(v); sdma.wait(s)
    gdma.wait(g)
    nc.all_engine_barrier()
    import os as _os2
    if not _os2.environ.get("SKIP_ACOLS"):
        sac.inc(s.activation(out=acols, in_=acols, func=AF.Exp))
        sac.wait(v)
        sdv.inc(v.tensor_scalar(out=acols, in0=acols, scalar1=-1.0, scalar2=None,
                                op0=OP.mult))

    for ch in range(FH // 128 if not skip_prep else 0):
        wb = WB1 if ch % 2 == 0 else WB2
        if ch >= 2:
            spe.wait(g)
        castload(wb[:, 0:D], W2TR_d[ch * 128:(ch + 1) * 128, :])
        gdma.wait(t)
        sdv.wait(t)
        for gc in range(4):
            spe.inc(t.matmul(psA[:, gc * 512:(gc + 1) * 512], H64B,
                             wb[:, gc * 512:(gc + 1) * 512], start=True, stop=True))
        spe.wait(v)
        sdma.wait(v)
        sdv.inc(v.tensor_scalar(out=sbfB[:, 0:D], in0=psA, scalar1=SQ8192,
                                scalar2=None, op0=OP.mult))
        sdv.wait(sy)
        dst = bass.AP(w2b_s.tensor, w2b_s.offset + (2 * ch) * D,
                      [[D, 2], [128 * D, 64], [1, D]])
        sdma.inc(sy.dma_start(out=dst, in_=sbfB[:, 0:D]), 16)

    for ic in range(NCH if not skip_prep else 0):
        if ic > 0:
            sdv.wait(g)
            sac.wait(g)
        castload(WB0[:, 0:D], wgt_d[ic * 128:(ic + 1) * 128, :])
        gdma.wait(s)
        gdma.wait(v)
        sac.inc(s.activation(out=SCRA, in_=WB0[:, 0:D], func=AF.Abs))
        sac.wait(v)
        vseal(v.reduce_sum(colA[:, 0:1], SCRA, axis=AX))
        vseal(v.tensor_scalar(out=colA[:, 0:1], in0=colA[:, 0:1], scalar1=1.0 / D,
                              scalar2=None, op0=OP.mult))
        vseal(v.tensor_scalar(out=colB[:, 0:1], in0=colA[:, 0:1], scalar1=1e-8,
                              scalar2=None, op0=OP.add))
        vseal(v.reciprocal(colB[:, 1:2], colB[:, 0:1]))
        v.tensor_scalar(out=SCRA, in0=WB0[:, 0:D], scalar1=colB[:, 1:2],
                        scalar2=None, op0=OP.mult)
        v.tensor_copy(SCRI, SCRA)
        v.scalar_tensor_tensor(out=SCRB, in0=SCRA, scalar=-0.5, in1=SCRI,
                               op0=OP.add, op1=OP.is_ge)
        v.tensor_add(SCRB, SCRI, SCRB)
        for hh in range(2):
            v.scalar_tensor_tensor(out=SCRD, in0=SCRA[:, hh * 1024:(hh + 1) * 1024],
                                   scalar=0.5, in1=SCRI[:, hh * 1024:(hh + 1) * 1024],
                                   op0=OP.add, op1=OP.is_le)
            v.tensor_sub(SCRB[:, hh * 1024:(hh + 1) * 1024],
                         SCRB[:, hh * 1024:(hh + 1) * 1024], SCRD)
        v.tensor_scalar(out=SCRB, in0=SCRB, scalar1=-1.0, scalar2=1.0,
                        op0=OP.max, op1=OP.min)
        sdv.inc(v.tensor_scalar(out=sbfA[:, 0:D], in0=SCRB, scalar1=colA[:, 0:1],
                                scalar2=None, op0=OP.mult))
        sdv.wait(t)
        sdma.wait(v)
        for kc in range(NCH):
            slot = psT if kc % 2 == 0 else psU
            sdv.wait(t, lag=1)
            t.transpose(slot[:, 0:128], sbfA[:, kc * 128:(kc + 1) * 128], I128)
            spe.inc(t.nop())
            spe.wait(v)
            sdv.inc(v.tensor_copy(sbfB[:, kc * 128:(kc + 1) * 128], slot[:, 0:128]))
        sdv.wait(sy)
        for kc in range(NCH):
            sdma.inc(sy.dma_start(
                out=wtT_s[kc * 128:(kc + 1) * 128, ic * 128:(ic + 1) * 128],
                in_=sbfB[:, kc * 128:(kc + 1) * 128]), 16)

    sdv.wait(g)
    sac.wait(g)
    for src_d, dst_s, rows, cols in (() if skip_prep else
                                     ((WinT_d, winb_s, D, 2 * DI),
                                      (WoutT_d, woutb_s, DI, D))):
        for rc in range(rows // 128):
            for cb in range(0, cols, 4096):
                cw = min(4096, cols - cb)
                gdma.wait(g)
                castload(WB0[:, 0:cw], src_d[rc * 128:(rc + 1) * 128, cb:cb + cw])
                gdma.wait(g)
                gdma.inc(g.dma_start(out=dst_s[rc * 128:(rc + 1) * 128, cb:cb + cw],
                                     in_=WB0[:, 0:cw]), 16)
    nc.all_engine_barrier()
    if upto == 0:
        _emit_dump(nc, REG, dump)
        _finalize(nc, REG)
        return nc

    cndT = [ARN[:, i * KV:(i + 1) * KV] for i in range(NCH)]
    if skip_p1:
        nc.all_engine_barrier()
    else:
        pass
    if not skip_p1:
      for i in range(NCH):
        castload(cndT[i], condT_d[i * 128:(i + 1) * 128, :])
      castload(ROWS[:, 0:2 * D], row_ap(bkv_d, 2 * D))
      gdma.wait(t)
      sdma.wait(t)
    bkvrow = ROWS[:, 0:2 * D]

    def fq_signed_perpart(src_f32, w, scol, icol, out_bf, scale_back):
        v.tensor_scalar(out=SCRA[:, 0:w], in0=src_f32, scalar1=icol, scalar2=None,
                        op0=OP.mult)
        v.tensor_copy(SCRI[:, 0:w], SCRA[:, 0:w])
        v.scalar_tensor_tensor(out=SCRB[:, 0:w], in0=SCRA[:, 0:w], scalar=-0.5,
                               in1=SCRI[:, 0:w], op0=OP.add, op1=OP.is_ge)
        v.tensor_add(SCRB[:, 0:w], SCRI[:, 0:w], SCRB[:, 0:w])
        for hb in range(0, w, 1024):
            hw = min(1024, w - hb)
            v.scalar_tensor_tensor(out=SCRD[:, 0:hw], in0=SCRA[:, hb:hb + hw],
                                   scalar=0.5, in1=SCRI[:, hb:hb + hw],
                                   op0=OP.add, op1=OP.is_le)
            v.tensor_sub(SCRB[:, hb:hb + hw], SCRB[:, hb:hb + hw], SCRD[:, 0:hw])
        if scale_back:
            sdv.inc(v.tensor_scalar(out=out_bf, in0=SCRB[:, 0:w], scalar1=scol,
                                    scalar2=None, op0=OP.mult))
        else:
            sdv.inc(v.tensor_copy(out_bf, SCRB[:, 0:w]))

    import os as _os9
    _p1dbg = bool(_os9.environ.get("P1DEBUG")) and not skip_p1
    _v1dbg = bool(_os9.environ.get("V1DEBUG")) and not skip_p1
    if _v1dbg:
        _p1dbg = True
    for chalf in range((2 if not _p1dbg else (0 if _v1dbg else 1)) if not skip_p1 else 0):
        for jp in range(4 if not _p1dbg else 1):
            sdv.wait(t)
            for dm in range(NCH):
                wb = WB1 if dm % 2 == 0 else WB2
                spe.wait(g)
                castload(wb[:, 0:1024],
                         Wkv_d[dm * 128:(dm + 1) * 128, chalf * 1024:(chalf + 1) * 1024])
                gdma.wait(t)
                nob = bool(_os9.environ.get("NOBIAS"))
                for jj in range(2):
                    jc = jp * 2 + jj
                    pk = psA[:, jj * 1024:(jj + 1) * 1024]
                    if dm == 0 and not nob:
                        spe.inc(t.matmul(pk[:, 0:512], onesbf,
                                         bkvrow[:, chalf * 1024:chalf * 1024 + 512],
                                         start=True, stop=False))
                        spe.inc(t.matmul(pk[:, 512:1024], onesbf,
                                         bkvrow[:, chalf * 1024 + 512:(chalf + 1) * 1024],
                                         start=True, stop=False))
                    for gc in range(2):
                        spe.inc(t.matmul(pk[:, gc * 512:(gc + 1) * 512],
                                         cndT[dm][:, jc * 128:(jc + 1) * 128],
                                         wb[:, gc * 512:(gc + 1) * 512],
                                         start=(dm == 0 and nob),
                                         stop=(dm == NCH - 1)))
            spe.wait(v)
            if _p1dbg:
                v.tensor_copy(DBGB, psA[:, 0:1024])
            for jj in range(2 if not _p1dbg else 1):
                jc = jp * 2 + jj
                pk = psA[:, jj * 1024:(jj + 1) * 1024]
                vseal(v.reduce_max(colA[:, 0:8], pk.rearrange("p (h q) -> p h q", q=128),
                                   axis=AX, apply_absolute_value=True))
                vseal(v.tensor_scalar(out=colA[:, 0:8], in0=colA[:, 0:8], scalar1=INV127,
                                      scalar2=1e-8, op0=OP.mult, op1=OP.max))
                vseal(v.reciprocal(colB[:, 0:8], colA[:, 0:8]))
                v.tensor_tensor(out=SCRA[:, 0:1024].rearrange("p (h q) -> p h q", q=128),
                                in0=pk.rearrange("p (h q) -> p h q", q=128),
                                in1=colB[:, 0:8].unsqueeze(2).broadcast_to(
                                    (128, 8, 128)),
                                op=OP.mult)
                v.tensor_copy(SCRI[:, 0:1024], SCRA[:, 0:1024])
                v.scalar_tensor_tensor(out=SCRB[:, 0:1024], in0=SCRA[:, 0:1024],
                                       scalar=-0.5, in1=SCRI[:, 0:1024],
                                       op0=OP.add, op1=OP.is_ge)
                v.tensor_add(SCRB[:, 0:1024], SCRI[:, 0:1024], SCRB[:, 0:1024])
                v.scalar_tensor_tensor(out=SCRD, in0=SCRA[:, 0:1024], scalar=0.5,
                                       in1=SCRI[:, 0:1024], op0=OP.add, op1=OP.is_le)
                v.tensor_sub(SCRB[:, 0:1024], SCRB[:, 0:1024], SCRD)
                spe.wait(v)
                sdv.inc(v.tensor_tensor(
                    out=sbfA[:, 0:1024].rearrange("p (h q) -> p h q", q=128),
                    in0=SCRB[:, 0:1024].rearrange("p (h q) -> p h q", q=128),
                    in1=colA[:, 0:8].unsqueeze(2).broadcast_to((128, 8, 128)),
                    op=OP.mult))
                sdv.wait(t)
                sdma.wait(v)
                for hh in range(8):
                    h = chalf * 8 + hh
                    slot = psT if hh % 2 == 0 else psU
                    sdv.wait(t, lag=1)
                    t.transpose(slot[:, 0:128], sbfA[:, hh * 128:(hh + 1) * 128], I128)
                    spe.inc(t.nop())
                    spe.wait(v)
                    sdv.inc(v.tensor_copy(sbfB[:, hh * 128:(hh + 1) * 128],
                                          slot[:, 0:128]))
                sdv.wait(sy)
                for hh in range(8):
                    h = chalf * 8 + hh
                    sdma.inc(sy.dma_start(
                        out=kscT_s[h * 128:(h + 1) * 128, jc * 128:(jc + 1) * 128],
                        in_=sbfB[:, hh * 128:(hh + 1) * 128]), 16)

    if not skip_p1 and (not _p1dbg or _v1dbg):
        load_cols(colC[:, 0:NCH], bkv_d, NCH, off=2048)
    for cc in range((NCH if not _p1dbg else (1 if _v1dbg else 0)) if not skip_p1 else 0):
        spe.wait(g)
        castload(WB1[:, 0:2048].rearrange("p (a b) -> p a b", b=128),
                 Wkv_d[:, 2048 + cc * 128:2048 + (cc + 1) * 128]
                 .rearrange("(a p) b -> p a b", p=128), slow=True)
        gdma.wait(t)
        spe.wait(v, lag=0)
        sdv.wait(t)
        for dm in range(NCH):
            for gc in range(2):
                spe.inc(t.matmul(psA[:, gc * 512:(gc + 1) * 512],
                                 WB1[:, dm * 128:(dm + 1) * 128],
                                 cndT[dm][:, gc * 512:(gc + 1) * 512],
                                 start=(dm == 0), stop=(dm == NCH - 1)))
        spe.wait(v)
        if _v1dbg:
            v.tensor_copy(DBGB, psA[:, 0:1024])
        vcol = colC[:, cc:cc + 1]
        vseal(v.tensor_scalar(out=SCRA[:, 0:1024], in0=psA[:, 0:1024], scalar1=vcol,
                              scalar2=None, op0=OP.add))
        vseal(v.reduce_max(colA[:, 0:1], SCRA[:, 0:1024], axis=AX,
                           apply_absolute_value=True))
        vseal(v.tensor_scalar(out=colA[:, 0:1], in0=colA[:, 0:1], scalar1=INV127,
                              scalar2=1e-8, op0=OP.mult, op1=OP.max))
        vseal(v.reciprocal(colB[:, 0:1], colA[:, 0:1]))
        spe.wait(v)
        fq_signed_perpart(SCRA[:, 0:1024], 1024, colA[:, 0:1], colB[:, 0:1],
                          sbfA[:, 0:1024], True)
        sdv.wait(t)
        sdma.wait(v)
        for jc in range(8):
            slot = psT if jc % 2 == 0 else psU
            sdv.wait(t, lag=1)
            t.transpose(slot[:, 0:128], sbfA[:, jc * 128:(jc + 1) * 128], I128)
            spe.inc(t.nop())
            spe.wait(v)
            sdv.inc(v.tensor_copy(sbfB[:, jc * 128:(jc + 1) * 128], slot[:, 0:128]))
        sdv.wait(sy)
        for jc in range(8):
            sdma.inc(sy.dma_start(
                out=vtok_s[jc * 128:(jc + 1) * 128, cc * 128:(cc + 1) * 128],
                in_=sbfB[:, jc * 128:(jc + 1) * 128]), 16)
    nc.all_engine_barrier()
    if upto == 1:
        _emit_dump(nc, REG, dump)
        _finalize(nc, REG)
        return nc

    def emit_layernorm(src_tiles):
        sdv.wait(s)
        spe.wait(s)
        for ic in range(NCH):
            sdv.inc(v.reduce_sum(colA[:, ic:ic + 1], src_tiles[ic], axis=AX))
            sdv.wait(s)
            sac.inc(s.activation(out=SCRA, in_=src_tiles[ic], func=AF.Square))
            sac.wait(v)
            sdv.inc(v.reduce_sum(colB[:, ic:ic + 1], SCRA, axis=AX))
        sdv.wait(v)
        vseal(v.tensor_scalar(out=ln_m, in0=colA, scalar1=1.0 / D, scalar2=None,
                              op0=OP.mult))
        vseal(v.tensor_scalar(out=colB, in0=colB, scalar1=1.0 / D, scalar2=None,
                              op0=OP.mult))
        vseal(v.tensor_mul(colA, ln_m, ln_m))
        vseal(v.tensor_sub(colB, colB, colA))
        sdv.wait(s)
        sseal(s.activation(out=colB, in_=colB, func=AF.Ln, bias=1e-5, scale=1.0))
        sseal(s.activation(out=ln_r, in_=colB, func=AF.Exp, scale=-0.5))
        sac.wait(v)
        vseal(v.tensor_mul(ln_b, ln_m, ln_r))
        vseal(v.tensor_scalar(out=ln_b, in0=ln_b, scalar1=-1.0, scalar2=None,
                              op0=OP.mult))
        sdv.wait(s)

    def emit_xnT_to_hbm(src_tiles):
        for ic in range(NCH):
            spe.wait(s)
            sac.inc(s.activation(out=sbfA[:, 0:D], in_=src_tiles[ic],
                                 func=AF.Identity, scale=ln_r[:, ic:ic + 1],
                                 bias=ln_b[:, ic:ic + 1]))
            sac.wait(t)
            sdma.wait(v)
            for dc in range(NCH):
                slot = psT if dc % 2 == 0 else psU
                sdv.wait(t, lag=1)
                t.transpose(slot[:, 0:128], sbfA[:, dc * 128:(dc + 1) * 128], I128)
                spe.inc(t.nop())
                spe.wait(v)
                sdv.inc(v.tensor_copy(sbfB[:, dc * 128:(dc + 1) * 128],
                                      slot[:, 0:128]))
            sdv.wait(sy)
            for dc in range(NCH):
                sdma.inc(sy.dma_start(
                    out=xnT_s[dc * 128:(dc + 1) * 128, ic * 128:(ic + 1) * 128],
                    in_=sbfB[:, dc * 128:(dc + 1) * 128]), 16)

    emit_layernorm(x_tm)
    import os as _os3
    if not _os3.environ.get("SKIP_XNT"):
        emit_xnT_to_hbm(x_tm)
    castload(ROWS[:, 0:D], row_ap(bq_d, D))
    gdma.wait(t)
    nc.all_engine_barrier()
    if upto == 2:
        _emit_dump(nc, REG, dump)
        _finalize(nc, REG)
        return nc

    _p2dbg = bool(_os9.environ.get("P2DEBUG"))
    bqrow = ROWS[:, 0:D]
    for dqb in range(8):
        spe.wait(g)
        castload(WB0[:, 0:4096].rearrange("p (a b) -> p a b", b=256),
                 Wq_d[:, dqb * 256:(dqb + 1) * 256]
                 .rearrange("(a p) b -> p a b", p=128), slow=True)
        gdma.wait(t)
        for ic in range(NCH):
            spe.wait(sy)
            load(WB1[:, 0:2048].rearrange("p (a b) -> p a b", b=128),
                 xnT_s[:, ic * 128:(ic + 1) * 128]
                 .rearrange("(a p) b -> p a b", p=128), eng=sy, slow=True)
            sdma.wait(t)
            sdv.wait(t)
            ps_q = psA[:, 0:256]
            spe.inc(t.matmul(ps_q[:, 0:128], onesbf,
                             bqrow[:, dqb * 256:dqb * 256 + 128],
                             start=True, stop=False))
            spe.inc(t.matmul(ps_q[:, 128:256], onesbf,
                             bqrow[:, dqb * 256 + 128:(dqb + 1) * 256],
                             start=False, stop=False))
            for dm in range(NCH):
                spe.inc(t.matmul(ps_q,
                                 WB1[:, dm * 128:(dm + 1) * 128],
                                 WB0[:, dm * 256:(dm + 1) * 256],
                                 start=False, stop=(dm == NCH - 1)))
            spe.wait(v)
            spe.wait(s)
            if _p2dbg and dqb == 0 and ic == 0:
                v.tensor_copy(DBGB[:, 0:256], ps_q)
            vseal(v.reduce_max(colA[:, 0:2],
                               ps_q.rearrange("p (h q) -> p h q", q=128),
                               axis=AX, apply_absolute_value=True))
            vseal(v.tensor_scalar(out=colA[:, 0:2], in0=colA[:, 0:2], scalar1=INV127,
                                  scalar2=1e-8, op0=OP.mult, op1=OP.max))
            vseal(v.reciprocal(colB[:, 0:2], colA[:, 0:2]))
            v.tensor_scalar(out=sq_all[:, ic * 16 + dqb * 2:ic * 16 + dqb * 2 + 2],
                            in0=colA[:, 0:2], scalar1=SCALEQ, scalar2=None,
                            op0=OP.mult)
            v.tensor_tensor(out=SCRA[:, 0:256].rearrange("p (h q) -> p h q", q=128),
                            in0=ps_q.rearrange("p (h q) -> p h q", q=128),
                            in1=colB[:, 0:2].unsqueeze(2)
                            .broadcast_to((128, 2, 128)), op=OP.mult)
            v.tensor_copy(SCRI[:, 0:256], SCRA[:, 0:256])
            v.scalar_tensor_tensor(out=SCRB[:, 0:256], in0=SCRA[:, 0:256],
                                   scalar=-0.5, in1=SCRI[:, 0:256],
                                   op0=OP.add, op1=OP.is_ge)
            v.tensor_add(SCRB[:, 0:256], SCRI[:, 0:256], SCRB[:, 0:256])
            v.scalar_tensor_tensor(out=SCRD[:, 0:256], in0=SCRA[:, 0:256],
                                   scalar=0.5, in1=SCRI[:, 0:256],
                                   op0=OP.add, op1=OP.is_le)
            v.tensor_sub(SCRB[:, 0:256], SCRB[:, 0:256], SCRD[:, 0:256])
            spe.wait(v)
            sdv.inc(v.tensor_copy(sbfA[:, 0:256], SCRB[:, 0:256]))
            sdv.wait(t)
            sdma.wait(v)
            for hh in range(2):
                slot = psT if hh % 2 == 0 else psU
                sdv.wait(t, lag=1)
                t.transpose(slot[:, 0:128],
                            sbfA[:, hh * 128:(hh + 1) * 128], I128)
                spe.inc(t.nop())
                spe.wait(v)
                sdv.inc(v.tensor_copy(sbfB[:, hh * 128:(hh + 1) * 128],
                                      slot[:, 0:128]))
            sdv.wait(sy)
            for hh in range(2):
                dq0 = dqb * 256 + hh * 128
                sdma.inc(sy.dma_start(
                    out=cqT_s[dq0:dq0 + 128, ic * 128:(ic + 1) * 128],
                    in_=sbfB[:, hh * 128:(hh + 1) * 128]), 16)
    spe.wait(v)
    sdv.inc(v.tensor_scalar(out=nsq_all, in0=sq_all, scalar1=-1.0, scalar2=None,
                            op0=OP.mult))
    nc.all_engine_barrier()
    if upto == 3:
        _emit_dump(nc, REG, dump)
        _finalize(nc, REG)
        return nc

    for h in range(NH):
        sdv.wait(sy)
        spe.wait(sy)
        load(WB0[:, 0:2048], cqT_s[h * 128:(h + 1) * 128, :])
        load(WB1[:, 0:1024], kscT_s[h * 128:(h + 1) * 128, :])
        load(WB2[:, 0:1024].rearrange("p (a b) -> p a b", b=128),
             vtok_s[:, h * 128:(h + 1) * 128].rearrange("(a p) b -> p a b", p=128),
             slow=True)
        sdma.wait(t)
        sdma.wait(s)
        sdma.wait(v)
        for ic in range(NCH):
            par = ic % 2
            psS = psA[:, par * 1024:(par + 1) * 1024]
            sac.wait(t, lag=1)
            spe.inc(t.matmul(psS[:, 0:512], WB0[:, ic * 128:(ic + 1) * 128],
                             WB1[:, 0:512], start=True, stop=True))
            spe.inc(t.matmul(psS[:, 512:1024], WB0[:, ic * 128:(ic + 1) * 128],
                             WB1[:, 512:1024], start=True, stop=True))
            spe.wait(v)
            sqc = sq_all[:, ic * 16 + h:ic * 16 + h + 1]
            nsqc = nsq_all[:, ic * 16 + h:ic * 16 + h + 1]
            vseal(v.reduce_max(colA[:, 0:1], psS, axis=AX))
            vseal(v.tensor_scalar(out=colB[:, 0:1], in0=colA[:, 0:1], scalar1=nsqc,
                                  scalar2=LN255, op0=OP.mult, op1=OP.add))
            sdv.wait(s)
            E = SCRA[:, par * 1024:(par + 1) * 1024]
            sac.inc(s.activation(out=E, in_=psS, func=AF.Exp, scale=sqc,
                                 bias=colB[:, 0:1]))
            sac.wait(v)
            Ri = SCRI[:, par * 1024:(par + 1) * 1024]
            Cg = SCRB[:, par * 1024:(par + 1) * 1024]
            vseal(v.reduce_sum(colC[:, 0:1], E, axis=AX))
            vseal(v.reciprocal(colC[:, 1:2], colC[:, 0:1]))
            v.tensor_copy(Ri, E)
            v.scalar_tensor_tensor(out=Cg, in0=E, scalar=-0.5, in1=Ri,
                                   op0=OP.add, op1=OP.is_ge)
            v.tensor_add(E, Ri, Cg)
            ca = sbfA if par == 0 else sbfB
            sdma.wait(v)
            adma.wait(v)
            sdv.inc(v.tensor_scalar(out=ca[:, 0:1024], in0=E, scalar1=colC[:, 1:2],
                                    scalar2=None, op0=OP.mult))
            sdv.wait(sy)
            sdv.wait(s)
            caT = ca[:, 1024:2048]
            for jc in range(8):
                src = ca[:, jc * 128:(jc + 1) * 128]
                dst = caT[:, jc * 128:(jc + 1) * 128]
                if jc % 2 == 0:
                    sdma.inc(sy.dma_start_transpose(dst, src), 16)
                else:
                    adma.inc(s.dma_start_transpose(dst, src), 16)
            sdma.wait(t)
            adma.wait(t)
            sdv.wait(t)
            psO = psB[:, par * 512:par * 512 + 128]
            for jc in range(8):
                spe.inc(t.matmul(psO, WB2[:, jc * 128:(jc + 1) * 128],
                                 caT[:, jc * 128:(jc + 1) * 128],
                                 start=(jc == 0), stop=(jc == 7)))
            spe.wait(v)
            sdv.inc(v.tensor_copy(sbfA[:, 1024:1152] if par == 0 else
                                  sbfB[:, 1024:1152], psO))
            sdv.wait(sy)
            sdma.inc(sy.dma_start(
                out=otT_s[h * 128:(h + 1) * 128, ic * 128:(ic + 1) * 128],
                in_=(sbfA if par == 0 else sbfB)[:, 1024:1152]), 16)
    castload(ROWS[:, 0:D], row_ap(bp_d, D))
    gdma.wait(t)
    nc.all_engine_barrier()
    if upto == 4:
        _emit_dump(nc, REG, dump)
        _finalize(nc, REG)
        return nc

    bprow = ROWS[:, 0:D]
    for ig in range(8):
        sdv.wait(sy)
        spe.wait(sy)
        for icb in range(2):
            ic = ig * 2 + icb
            load(WB0[:, icb * 2048:icb * 2048 + 2048]
                 .rearrange("p (a b) -> p a b", b=128),
                 otT_s[:, ic * 128:(ic + 1) * 128]
                 .rearrange("(a p) b -> p a b", p=128), eng=sy, slow=True)
        sdma.wait(t)
        for dpb in range(4):
            sdv.wait(t)
            for dc in range(NCH):
                wb = WB1 if dc % 2 == 0 else WB2
                spe.wait(g)
                castload(wb[:, 0:512],
                         Wp_d[dc * 128:(dc + 1) * 128, dpb * 512:(dpb + 1) * 512])
                gdma.wait(t)
                for icb in range(2):
                    psP = psA[:, icb * 512:(icb + 1) * 512]
                    if dc == 0:
                        spe.inc(t.matmul(psP[:, 0:128], onesbf,
                                         bprow[:, dpb * 512:dpb * 512 + 128],
                                         start=True, stop=False))
                        spe.inc(t.matmul(psP[:, 128:512], onesbf,
                                         bprow[:, dpb * 512 + 128:(dpb + 1) * 512],
                                         start=False, stop=False))
                    spe.inc(t.matmul(psP,
                                     WB0[:, icb * 2048 + dc * 128:icb * 2048 + (dc + 1) * 128],
                                     wb[:, 0:512], start=False, stop=(dc == NCH - 1)))
            spe.wait(v)
            for icb in range(2):
                ic = ig * 2 + icb
                psP = psA[:, icb * 512:(icb + 1) * 512]
                xs = x_tm[ic][:, dpb * 512:(dpb + 1) * 512]
                sdv.inc(v.scalar_tensor_tensor(out=xs, in0=psP, scalar=1.0, in1=xs,
                                               op0=OP.mult, op1=OP.add))
    nc.all_engine_barrier()
    if upto == 5:
        _emit_dump(nc, REG, dump)
        _finalize(nc, REG)
        return nc

    xsl = [ARN[:, i * 512:(i + 1) * 512] for i in range(NCH)]
    v2 = [ARN[:, 8192 + i * 512:8192 + (i + 1) * 512] for i in range(DCH)]
    w2c = [ARN[:, 24576 + i * 512:24576 + (i + 1) * 512] for i in range(NCH)]
    for ft in range(4):
        sdv.wait(sy)
        spe.wait(sy)
        for dm in range(NCH):
            load(xsl[dm], xnT_s[dm * 128:(dm + 1) * 128, ft * 512:(ft + 1) * 512])
        sdma.wait(t)
        for half in range(2):
            sdv.wait(t)
            def _ldw13(hc2, a2):
                castload(WB0[:, a2 * 2048:(a2 + 1) * 2048]
                         .rearrange("p (a b) -> p a b", b=128),
                         W1T_d[:, hc2 * 128:(hc2 + 1) * 128]
                         .rearrange("(a p) b -> p a b", p=128), slow=True)
                castload((WB1 if a2 == 0 else WB2)[:, 0:2048]
                         .rearrange("p (a b) -> p a b", b=128),
                         W3T_d[:, hc2 * 128:(hc2 + 1) * 128]
                         .rearrange("(a p) b -> p a b", p=128), slow=True)
            spe.wait(g)
            sdv.wait(g)
            sac.wait(g)
            _ldw13(half * DCH, 0)
            for hcl in range(DCH):
                hc = half * DCH + hcl
                a = hcl % 2
                w1b = WB0[:, a * 2048:(a + 1) * 2048]
                w3b = WB1 if a == 0 else WB2
                if hcl + 1 < DCH:
                    spe.wait(g)
                    _ldw13(hc + 1, 1 - a)
                gdma.wait(t, lag=32 if hcl + 1 < DCH else 0)
                sac.wait(t)
                sdv.wait(t)
                for dm in range(NCH):
                    spe.inc(t.matmul(psA[:, 1024:1536], w1b[:, dm * 128:(dm + 1) * 128],
                                     xsl[dm], start=(dm == 0), stop=(dm == NCH - 1)))
                for dm in range(NCH):
                    spe.inc(t.matmul(psA[:, 1536:2048], w3b[:, dm * 128:(dm + 1) * 128],
                                     xsl[dm], start=(dm == 0), stop=(dm == NCH - 1)))
                spe.wait(s)
                sac.inc(s.activation(out=sbfA[:, 0:512], in_=psA[:, 1024:1536],
                                     func=AF.Silu))
                sac.wait(v)
                sdv.inc(v.tensor_mul(sbfB[:, 0:512], sbfA[:, 0:512],
                                     psA[:, 1536:2048]))
                sdv.wait(t)
                spe.inc(t.matmul(psB[:, 0:512], H128, sbfB[:, 0:512],
                                 start=True, stop=True))
                spe.wait(v)
                sdv.inc(v.tensor_copy(v2[hcl], psB[:, 0:512]))
                sdv.wait(t)
            for dpb in range(4):
                for sub in range(2):
                    sdv.wait(sy)
                    spe.wait(sy)
                    for hcl in range(NCH):
                        hh = half * DCH + sub * NCH + hcl
                        load(w2c[hcl],
                             w2b_s[hh * 128:(hh + 1) * 128, dpb * 512:(dpb + 1) * 512])
                    sdma.wait(t)
                    for icq in range(4):
                        ic = ft * 4 + icq
                        psF = psA[:, icq * 512:(icq + 1) * 512]
                        if sub == 0 and icq == 0:
                            sdv.wait(t)
                        for hcl in range(NCH):
                            spe.inc(t.matmul(
                                psF, v2[sub * NCH + hcl][:, icq * 128:(icq + 1) * 128],
                                w2c[hcl], start=(hcl == 0 and sub == 0),
                                stop=(hcl == NCH - 1 and sub == 1)))
                        if sub == 1:
                            spe.wait(v)
                            xs = x_tm[ic][:, dpb * 512:(dpb + 1) * 512]
                            sdv.inc(v.scalar_tensor_tensor(
                                out=xs, in0=psF, scalar=1.0, in1=xs,
                                op0=OP.mult, op1=OP.add))
    nc.all_engine_barrier()
    if upto == 6:
        _emit_dump(nc, REG, dump)
        _finalize(nc, REG)
        return nc

    emit_layernorm(x_tm)
    castload(ROWS[:, 0:D], row_ap(bias1_d, D))
    gdma.wait(t)
    spe.wait(v)
    b1bc = ARN[:, 28672:32768].bitcast(F32)
    sdv.wait(t)
    for gc in range(4):
        spe.inc(t.matmul(psA[:, gc * 512:(gc + 1) * 512], onesbf,
                         ROWS[:, gc * 512:(gc + 1) * 512], start=True, stop=True))
    spe.wait(v)
    sdv.inc(v.tensor_copy(b1bc, psA))
    nc.all_engine_barrier()
    if upto == 7:
        _emit_dump(nc, REG, dump)
        _finalize(nc, REG)
        return nc

    hsT = [ARN[:, i * 256:(i + 1) * 256] for i in range(DCH)]
    gtT = [ARN[:, 8192 + i * 256:8192 + (i + 1) * 256] for i in range(DCH)]
    yT = [ARN[:, 16384 + i * 256:16384 + (i + 1) * 256] for i in range(DCH)]
    xqT = [ARN[:, 24576 + i * 256:24576 + (i + 1) * 256] for i in range(NCH)]
    xn1T = xqT
    sdv.inc(v.memset(CARR, 0.0))
    hs_stash = sbt("hs_stash", [128, DCH * 4], BF16)
    sdv.inc(v.memset(hs_stash, 0.0))
    sdv.wait(t); sdv.wait(s); sdv.wait(g)

    for mt in range(NMT):
        ics = [mt * (MT // 128) + k for k in range(MT // 128)]
        for k, ic in enumerate(ics):
            spe.wait(s)
            sdv.wait(s)
            sac.inc(s.activation(out=sbfA[:, 0:D], in_=x_tm[ic], func=AF.Identity,
                                 scale=ln_r[:, ic:ic + 1], bias=ln_b[:, ic:ic + 1]))
            sac.wait(t)
            for dm in range(NCH):
                slot = psT if dm % 2 == 0 else psU
                sdv.wait(t, lag=1)
                t.transpose(slot[:, 0:128], sbfA[:, dm * 128:(dm + 1) * 128], I128)
                spe.inc(t.nop())
                spe.wait(v)
                sdv.inc(v.tensor_copy(xn1T[dm][:, k * 128:(k + 1) * 128],
                                      slot[:, 0:128]))
            sdv.wait(t)
        spe.wait(sy)
        load(WB1[:, 0:2048].rearrange("p (a b) -> p a b", b=128),
             winb_s[:, 0:128].rearrange("(a p) b -> p a b", p=128),
             eng=sy, slow=True)
        for ec in range(2 * DCH):
            wb = WB1 if ec % 2 == 0 else WB2
            if ec + 1 < 2 * DCH:
                other = WB2 if ec % 2 == 0 else WB1
                spe.wait(sy)
                load(other[:, 0:2048].rearrange("p (a b) -> p a b", b=128),
                     winb_s[:, (ec + 1) * 128:(ec + 2) * 128]
                     .rearrange("(a p) b -> p a b", p=128), eng=sy, slow=True)
            sdma.wait(t, lag=16 if ec + 1 < 2 * DCH else 0)
            par = ec % 2
            psW = psA[:, 1024 + par * 512:1024 + par * 512 + MT]
            sdv.wait(t)
            sac.wait(t)
            for dm in range(NCH):
                spe.inc(t.matmul(psW, wb[:, dm * 128:(dm + 1) * 128],
                                 xn1T[dm][:, 0:MT], start=(dm == 0),
                                 stop=(dm == NCH - 1)))
            spe.wait(v)
            if ec < DCH:
                dc = ec
                st = hs_stash[:, dc * 4:dc * 4 + 4]
                v.tensor_copy(SCRB[:, 0:3], st[:, 1:4])
                vseal(v.tensor_copy(SCRB[:, 3:MT + 3], psW))
                vseal(v.tensor_copy(st[:, 1:4], SCRB[:, MT:MT + 3]))
                v.tensor_scalar(out=SCRA[:, 0:MT], in0=SCRB[:, 3:MT + 3],
                                scalar1=ccols[:, 96 + dc:96 + dc + 1],
                                scalar2=None, op0=OP.mult)
                ii_ = None
                for kk in range(3):
                    ii_ = v.scalar_tensor_tensor(
                        out=SCRA[:, 0:MT], in0=SCRB[:, kk:MT + kk],
                        scalar=ccols[:, kk * 32 + dc:kk * 32 + dc + 1],
                        in1=SCRA[:, 0:MT], op0=OP.mult, op1=OP.add)
                sdv.inc(ii_)
                sdv.wait(s)
                sac.inc(s.activation(out=hsT[dc], in_=SCRA[:, 0:MT], func=AF.Silu,
                                     bias=bcv_cols[:, dc:dc + 1], scale=1.0))
            else:
                dc = ec - DCH
                sdv.inc(v.tensor_copy(gtT[dc], psW))
        sac.wait(t)
        sdv.wait(t)
        sac.wait(v)
        for dc in range(DCH):
            wb = WB1 if dc % 2 == 0 else WB2
            spe.wait(g)
            castload(wb[:, 0:160], WxT_d[dc * 128:(dc + 1) * 128, :])
            gdma.wait(t)
            spe.inc(t.matmul(psB[:, 0:MT], wb[:, 0:128], hsT[dc],
                             start=(dc == 0), stop=(dc == DCH - 1)))
            spe.inc(t.matmul(psB[0:32, 512:512 + MT], wb[:, 128:160], hsT[dc],
                             start=(dc == 0), stop=(dc == DCH - 1)))
        spe.wait(v)
        sdv.inc(v.tensor_copy(sbfB[:, 0:MT], psB[:, 0:MT]))
        sdv.inc(v.tensor_copy(BCR, psB[0:32, 512:512 + MT]))
        sdv.wait(g)
        gdma.inc(g.dma_start(out=ROWS[:, 0:16 * MT], in_=BCR[0:16, :]), 16)
        gdma.inc(g.dma_start(out=ROWSC[:, 0:16 * MT], in_=BCR[16:32, :]), 16)
        gdma.wait(t)
        sdv.wait(t)
        for dc in range(DCH):
            wb = WB1 if dc % 2 == 0 else WB2
            spe.wait(g)
            castload(wb[:, 0:128], WdtT_d[:, dc * 128:(dc + 1) * 128], slow=True)
            gdma.wait(t)
            sac.wait(t)
            sdv.wait(t)
            spe.inc(t.matmul(psB[:, 512:512 + MT], wb[:, 0:128], sbfB[:, 0:MT],
                             start=True, stop=True))
            spe.wait(s)
            spe.wait(v)
            sac.wait(v)
            sseal(s.activation(out=SCRB[:, 256:256 + MT], in_=psB[:, 512:512 + MT],
                               func=AF.Abs, bias=bdt_cols[:, dc:dc + 1], scale=1.0))
            sseal(s.activation(out=SCRB[:, 256:256 + MT], in_=SCRB[:, 256:256 + MT],
                               func=AF.Exp, scale=-1.0))
            sseal(s.activation(out=SCRB[:, 256:256 + MT], in_=SCRB[:, 256:256 + MT],
                               func=AF.Ln, bias=1.0, scale=1.0))
            sac.wait(v)
            v.tensor_scalar(out=SCRA[:, 0:MT], in0=psB[:, 512:512 + MT],
                            scalar1=bdt_cols[:, dc:dc + 1], scalar2=0.0,
                            op0=OP.add, op1=OP.max)
            vseal(v.tensor_add(SCRA[:, 0:MT], SCRA[:, 0:MT], SCRB[:, 256:256 + MT]))
            sdv.wait(g)
            sdv.inc(v.tensor_mul(sbfA[:, 0:MT], SCRA[:, 0:MT], hsT[dc]))
            sdv.wait(s)
            sdv.wait(g)
            for n in range(NST):
                npar = n % 2
                bsl = psA[:, npar * 1024:npar * 1024 + MT]
                csl = psA[:, npar * 1024 + 512:npar * 1024 + 512 + MT]
                sdv.wait(t)
                spe.inc(t.matmul(bsl, onesbf, ROWS[:, n * MT:(n + 1) * MT],
                                 start=True, stop=True))
                spe.inc(t.matmul(csl, onesbf, ROWSC[:, n * MT:(n + 1) * MT],
                                 start=True, stop=True))
                sdv.wait(s)
                sac.inc(s.activation(out=SCRB[:, 0:MT], in_=SCRA[:, 0:MT],
                                     func=AF.Exp,
                                     scale=acols[:, dc * 16 + n:dc * 16 + n + 1]))
                spe.wait(v)
                sdv.inc(v.tensor_mul(SCRD[:, 0:MT], sbfA[:, 0:MT], bsl))
                sac.wait(v)
                cini = CARR[:, dc * 16 + n:dc * 16 + n + 1]
                vseal(v.tensor_tensor_scan(out=SCRB[:, 768:768 + MT],
                                           data0=SCRB[:, 0:MT],
                                           data1=SCRD[:, 0:MT], initial=cini,
                                           op0=OP.mult, op1=OP.add))
                vseal(v.tensor_copy(cini, SCRB[:, 768 + MT - 1:768 + MT]))
                spe.wait(v)
                sdv.inc(v.tensor_mul(sbfB[:, 256:256 + MT], SCRB[:, 768:768 + MT],
                                     csl))
                sdv.wait(t)
                spe.inc(t.matmul(psB[:, 0:MT], I128, sbfB[:, 256:256 + MT],
                                 start=(n == 0), stop=(n == NST - 1)))
            spe.wait(s)
            sac.inc(s.activation(out=sbfB[:, 512:512 + MT], in_=gtT[dc],
                                 func=AF.Silu))
            sac.wait(v)
            spe.wait(v)
            v.scalar_tensor_tensor(out=SCRB[:, 0:MT], in0=hsT[dc],
                                   scalar=dv_cols[:, dc:dc + 1], in1=psB[:, 0:MT],
                                   op0=OP.mult, op1=OP.add)
            sdv.inc(v.tensor_mul(yT[dc], SCRB[:, 0:MT], sbfB[:, 512:512 + MT]))
            sdv.wait(t)
        for k, ic in enumerate(ics):
            spe.wait(s)
            sdv.wait(s)
            sac.inc(s.activation(out=sbfA[:, 0:D], in_=x_tm[ic], func=AF.Identity,
                                 scale=ln_r[:, ic:ic + 1], bias=ln_b[:, ic:ic + 1]))
            sseal(s.activation(out=SCRB, in_=sbfA[:, 0:D], func=AF.Square))
            sac.wait(v)
            vseal(v.reduce_sum(colA[:, 0:1], SCRB, axis=AX))
            vseal(v.tensor_scalar(out=colA[:, 0:1], in0=colA[:, 0:1],
                                  scalar1=1.0 / D, scalar2=1e-8,
                                  op0=OP.mult, op1=OP.add))
            sdv.wait(s)
            sseal(s.activation(out=colA[:, 1:2], in_=colA[:, 0:1], func=AF.Ln))
            sseal(s.activation(out=colA[:, 2:3], in_=colA[:, 1:2], func=AF.Exp,
                               scale=0.5))
            sac.wait(v)
            vseal(v.reduce_max(colA[:, 3:4], sbfA[:, 0:D], axis=AX,
                               apply_absolute_value=True))
            vseal(v.reciprocal(colA[:, 4:5], colA[:, 2:3]))
            vseal(v.tensor_mul(colA[:, 5:6], colA[:, 3:4], colA[:, 4:5]))
            vseal(v.tensor_scalar(out=colA[:, 5:6], in0=colA[:, 5:6], scalar1=INV127,
                                  scalar2=1e-8, op0=OP.mult, op1=OP.add))
            vseal(v.tensor_mul(colA[:, 6:7], colA[:, 2:3], colA[:, 5:6]))
            vseal(v.reciprocal(colA[:, 7:8], colA[:, 6:7]))
            vseal(v.tensor_mul(colA[:, 8:9], colA[:, 3:4], colA[:, 4:5]))
            vseal(v.tensor_scalar(out=colA[:, 8:9], in0=colA[:, 8:9], scalar1=INV127,
                                  scalar2=None, op0=OP.mult))
            for hb in range(2):
                sl = slice(hb * 1024, (hb + 1) * 1024)
                v.tensor_scalar(out=SCRB[:, sl], in0=sbfA[:, sl],
                                scalar1=colA[:, 7:8], scalar2=None, op0=OP.mult)
                v.tensor_copy(SCRI[:, sl], SCRB[:, sl])
                v.scalar_tensor_tensor(out=SCRD, in0=SCRB[:, sl], scalar=-0.5,
                                       in1=SCRI[:, sl], op0=OP.add, op1=OP.is_ge)
                v.tensor_add(SCRD, SCRI[:, sl], SCRD)
                v.scalar_tensor_tensor(out=SCRB[:, sl], in0=SCRB[:, sl], scalar=0.5,
                                       in1=SCRI[:, sl], op0=OP.add, op1=OP.is_le)
                v.tensor_sub(SCRD, SCRD, SCRB[:, sl])
                spe.wait(v)
                sdv.inc(v.tensor_copy(sbfB[:, sl], SCRD))
            sdv.wait(t)
            for dm in range(NCH):
                slot = psT if dm % 2 == 0 else psU
                sdv.wait(t, lag=1)
                t.transpose(slot[:, 0:128], sbfB[:, dm * 128:(dm + 1) * 128], I128)
                spe.inc(t.nop())
                spe.wait(v)
                sdv.inc(v.tensor_copy(xqT[dm][:, k * 128:(k + 1) * 128],
                                      slot[:, 0:128]))
            sdv.wait(t)
            spe.wait(sy)
            load(WB1[:, 0:2048], wtT_s[0:128, :], eng=sy)
            sdv.wait(t)
            for kc in range(NCH):
                wb = WB1 if kc % 2 == 0 else WB2
                if kc + 1 < NCH:
                    other = WB2 if kc % 2 == 0 else WB1
                    spe.wait(sy)
                    load(other[:, 0:2048],
                         wtT_s[(kc + 1) * 128:(kc + 2) * 128, :], eng=sy)
                sdma.wait(t, lag=16 if kc + 1 < NCH else 0)
                for gc in range(4):
                    spe.inc(t.matmul(psA[:, gc * 512:(gc + 1) * 512],
                                     xqT[kc][:, k * 128:(k + 1) * 128],
                                     wb[:, gc * 512:(gc + 1) * 512],
                                     start=(kc == 0), stop=(kc == NCH - 1)))
            spe.wait(v)
            spe.wait(s)
            if DBGB is not None and mt == 0 and k == 0:
                v.tensor_copy(DBGB[:, 0:512], psA[:, 1536:2048])
            sac.inc(s.activation(out=sbfA[:, 0:D], in_=psA, func=AF.Identity,
                                 scale=colA[:, 8:9]))
            sac.wait(t)
            sac.wait(v)
            spe.wait(sy)
            load(WB1[:, 0:2048], woutb_s[0:128, :], eng=sy)
            for dc in range(DCH):
                wb = WB1 if dc % 2 == 0 else WB2
                if dc + 1 < DCH:
                    other = WB2 if dc % 2 == 0 else WB1
                    spe.wait(sy)
                    load(other[:, 0:2048],
                         woutb_s[(dc + 1) * 128:(dc + 2) * 128, :], eng=sy)
                sdma.wait(t, lag=16 if dc + 1 < DCH else 0)
                for gc in range(4):
                    spe.inc(t.matmul(psA[:, gc * 512:(gc + 1) * 512],
                                     yT[dc][:, k * 128:(k + 1) * 128],
                                     wb[:, gc * 512:(gc + 1) * 512],
                                     start=(dc == 0), stop=(dc == DCH - 1)))
            spe.wait(v)
            if DBGB is not None and mt == 0 and k == 0:
                v.tensor_copy(DBGB[:, 512:1024], psA[:, 1536:2048])
            sout.wait(v)
            v.scalar_tensor_tensor(out=SCRA, in0=psA, scalar=1.0, in1=b1bc,
                                   op0=OP.mult, op1=OP.add)
            v.tensor_add(SCRA, SCRA, x_tm[ic])
            sdv.inc(v.tensor_add(SCRA, SCRA, sbfA[:, 0:D]))
            sdv.wait(sy)
            sout.inc(sy.dma_start(out=out_d[ic * 128:(ic + 1) * 128, :], in_=SCRA), 16)
    nc.all_engine_barrier()
    sout.wait(sy)
    _emit_dump(nc, REG, dump)
    return nc


def _emit_dump(nc, R, dump):
    if not dump:
        return
    nc.all_engine_barrier()
    sy = R["sy"]
    sdma = R["sdma"]
    for name in dump:
        obj = R[name]
        if isinstance(obj, list):
            w = obj[0].shape[1]
            dt = obj[0].dtype
            dd = nc.dram_tensor(f"dbg_{name}", [len(obj), 128, w], dt,
                                kind="ExternalOutput").ap()
            for i, ap in enumerate(obj):
                sdma.inc(sy.dma_start(out=dd[i], in_=ap), 16)
        else:
            sh = list(obj.shape)
            dd = nc.dram_tensor(f"dbg_{name}", sh, obj.dtype,
                                kind="ExternalOutput").ap()
            sdma.inc(sy.dma_start(out=dd, in_=obj), 16)
    sdma.wait(sy)


def _finalize(nc, R):
    nc.all_engine_barrier()
    sy, v = R["sy"], R["v"]
    sdv, sout = R["sdv"], R["sout"]
    out_d, x_tm = R["out_d"], R["x_tm"]
    scro = R["SCRA"]
    for ic in range(NCH):
        sout.wait(v)
        sdv.inc(v.tensor_copy(scro, x_tm[ic]))
        sdv.wait(sy)
        sout.inc(sy.dma_start(out=out_d[ic * 128:(ic + 1) * 128, :], in_=scro), 16)
    sout.wait(sy)


def _host_layout(inputs):
    f32 = np.float32
    x = np.ascontiguousarray(np.asarray(inputs["L0_attn_x"], dtype=f32))
    cond = np.asarray(inputs["L0_attn_cond"], dtype=f32).reshape(KV, D)
    condT = np.ascontiguousarray(cond.T)
    W2 = np.asarray(inputs["L0_ffn_W2"], dtype=f32)
    W2TR = np.ascontiguousarray(
        W2.T.reshape(64, 128, D).transpose(1, 0, 2).reshape(FH, D))
    H128 = _had(128).astype(ml_dtypes.bfloat16)
    H64 = _had(64) * (2.0 ** -7)
    H64B = np.zeros((128, 128), np.float64)
    H64B[:64, :64] = H64
    H64B[64:, 64:] = H64
    common = {
        "condT": condT,
        "Wq": np.ascontiguousarray(np.asarray(inputs["L0_attn_Wq"], dtype=f32)),
        "bq": np.ascontiguousarray(np.asarray(inputs["L0_attn_bq"], dtype=f32)),
        "Wkv": np.ascontiguousarray(np.asarray(inputs["L0_attn_Wkv"], dtype=f32)),
        "bkv": np.ascontiguousarray(np.asarray(inputs["L0_attn_bkv"], dtype=f32)),
        "Wp": np.ascontiguousarray(np.asarray(inputs["L0_attn_Wp"], dtype=f32)),
        "bp": np.ascontiguousarray(np.asarray(inputs["L0_attn_bp"], dtype=f32)),
        "W1T": np.ascontiguousarray(np.asarray(inputs["L0_ffn_W1"], dtype=f32).T),
        "W3T": np.ascontiguousarray(np.asarray(inputs["L0_ffn_W3"], dtype=f32).T),
        "W2TR": W2TR,
        "WinT": np.ascontiguousarray(np.asarray(inputs["L1_attn_Win"], dtype=f32).T),
        "Wconv": np.ascontiguousarray(
            np.asarray(inputs["L1_attn_Wconv"], dtype=f32).reshape(DI, 4)),
        "bconv": np.ascontiguousarray(np.asarray(inputs["L1_attn_bconv"], dtype=f32)),
        "WxT": np.ascontiguousarray(np.asarray(inputs["L1_attn_Wx"], dtype=f32).T),
        "WdtT": np.ascontiguousarray(np.asarray(inputs["L1_attn_Wdt"], dtype=f32).T),
        "bdt": np.ascontiguousarray(np.asarray(inputs["L1_attn_bdt"], dtype=f32)),
        "WoutT": np.ascontiguousarray(np.asarray(inputs["L1_attn_Wout"], dtype=f32).T),
        "Alog": np.ascontiguousarray(np.asarray(inputs["L1_attn_A_log"], dtype=f32)),
        "Dvec": np.ascontiguousarray(np.asarray(inputs["L1_attn_D"], dtype=f32)),
        "wgt": np.ascontiguousarray(np.asarray(inputs["L1_ffn_weight"], dtype=f32)),
        "bias1": np.ascontiguousarray(np.asarray(inputs["L1_ffn_bias"], dtype=f32)),
        "onesbf": np.ones((1, 128), ml_dtypes.bfloat16),
        "onesf": np.ones((1, 128), f32),
        "I128": np.eye(128).astype(ml_dtypes.bfloat16),
        "H128": H128,
        "H64B": H64B.astype(ml_dtypes.bfloat16),
    }
    return [dict(common, x=np.ascontiguousarray(x[i])) for i in range(8)]


def kernel(**inputs):
    in_maps = _host_layout(inputs)
    nc = build(UPTO)
    res = run_bass_kernel_spmd(nc, in_maps, core_ids=list(range(8)))
    return np.stack([np.asarray(r["out"], dtype=np.float32) for r in res.results])


# version 04
# speedup vs baseline: 1.3619x; 1.0499x over previous
import numpy as np
import ml_dtypes
import concourse.bass as bass
import concourse.mybir as mybir
from concourse.bass_utils import run_bass_kernel_spmd

F32 = mybir.dt.float32
BF16 = mybir.dt.bfloat16
I32 = mybir.dt.int32
OP = mybir.AluOpType
AF = mybir.ActivationFunctionType
AX = mybir.AxisListType.X

T = 2048
D = 2048
NCH = 16
NH = 16
HD = 128
KV = 1024
FH = 8192
DI = 4096
DCH = 32
NST = 16
MT = 256
NMT = T // MT
FT = 512
SQ8192 = float(np.float32(1.0 / np.sqrt(np.float64(8192.0)) / (2.0 ** -7)))
LN255 = float(np.float32(np.log(np.float64(255.0))))
INV127 = float(np.float32(1.0 / np.float64(127.0)))
SCALEQ = float(np.float32(1.0 / np.sqrt(np.float64(128.0))))

UPTO = 99


def _had(n):
    H = np.ones((n, n), np.float64)
    for i in range(n):
        for j in range(n):
            if bin(i & j).count("1") % 2:
                H[i, j] = -1.0
    return H


class Sem:
    def __init__(self, nc, name):
        self.h = nc.alloc_semaphore(name)
        self.n = 0

    def inc(self, instr, k=1):
        instr.then_inc(self.h, k)
        self.n += k
        return instr

    def wait(self, eng, lag=0):
        if self.n - lag > 0:
            eng.wait_ge(self.h, self.n - lag)


def row_ap(vec_d, n, off=0):
    return bass.AP(vec_d.tensor, vec_d.offset + off, [[n, 1], [1, n]])


def build(upto=99, dump=(), skip_p1=False, skip_prep=False):
    nc = bass.Bass("TRN2", target_bir_lowering=False, detect_race_conditions=False)
    REG = {}

    def din(name, shape, dt=F32):
        return nc.dram_tensor(name, list(shape), dt, kind="ExternalInput").ap()

    x_d = din("x", [T, D])
    condT_d = din("condT", [D, KV])
    Wq_d = din("Wq", [D, D])
    bq_d = din("bq", [D])
    Wkv_d = din("Wkv", [D, 2 * D])
    bkv_d = din("bkv", [2 * D])
    Wp_d = din("Wp", [D, D])
    bp_d = din("bp", [D])
    W1T_d = din("W1T", [D, FH])
    W3T_d = din("W3T", [D, FH])
    W2TR_d = din("W2TR", [FH, D])
    WinT_d = din("WinT", [D, 2 * DI])
    Wconv_d = din("Wconv", [DI, 4])
    bconv_d = din("bconv", [DI])
    WxT_d = din("WxT", [DI, 160])
    WdtT_d = din("WdtT", [128, DI])
    bdt_d = din("bdt", [DI])
    WoutT_d = din("WoutT", [DI, D])
    Alog_d = din("Alog", [DI, NST])
    Dv_d = din("Dvec", [DI])
    wgt_d = din("wgt", [D, D])
    bias1_d = din("bias1", [D])
    onesbf_d = din("onesbf", [1, 128], BF16)
    onesf_d = din("onesf", [1, 128])
    I128_d = din("I128", [128, 128], BF16)
    H128_d = din("H128", [128, 128], BF16)
    H64B_d = din("H64B", [128, 128], BF16)

    out_d = nc.dram_tensor("out", [T, D], F32, kind="ExternalOutput").ap()

    def dscr(name, shape, dt=BF16):
        return nc.dram_tensor(name, list(shape), dt, kind="Internal").ap()

    xnT_s = dscr("xnT_s", [D, T])
    w2b_s = dscr("w2b_s", [FH, D])
    kscT_s = dscr("kscT_s", [D, KV])
    vtok_s = dscr("vtok_s", [KV, D])
    cqT_s = dscr("cqT_s", [D, T])
    otT_s = dscr("otT_s", [D, T])
    wtT_s = dscr("wtT_s", [D, D])
    winb_s = dscr("winb_s", [D, 2 * DI])
    woutb_s = dscr("woutb_s", [DI, D])

    def sbt(name, shape, dt):
        return nc.alloc_sbuf_tensor(name, list(shape), dt).ap()

    x_tm = [sbt(f"x_tm{i}", [128, D], BF16) for i in range(NCH)]
    ARN = sbt("ARN", [128, 32768], BF16)
    ROWS = sbt("ROWS", [1, 4096], BF16)
    SCRA = sbt("SCRA", [128, D], F32)
    SCRB = sbt("SCRB", [128, D], F32)
    SCRD = sbt("SCRD", [128, 1024], F32)
    SCRI = sbt("SCRI", [128, D], I32)
    sbfA = sbt("sbfA", [128, D], BF16)
    sbfB = sbt("sbfB", [128, D], BF16)
    WB0 = sbt("WB0", [128, 4096], BF16)
    WB1 = sbt("WB1", [128, 2048], BF16)
    WB2 = sbt("WB2", [128, 2048], BF16)
    onesbf = sbt("onesbf_t", [1, 128], BF16)
    onesf = sbt("onesf_t", [1, 128], F32)
    I128 = sbt("I128_t", [128, 128], BF16)
    H128 = sbt("H128_t", [128, 128], BF16)
    H64B = sbt("H64B_t", [128, 128], BF16)
    ln_m = sbt("ln_m", [128, NCH], F32)
    ln_r = sbt("ln_r", [128, NCH], F32)
    ln_b = sbt("ln_b", [128, NCH], F32)
    colA = sbt("colA", [128, NCH], F32)
    colB = sbt("colB", [128, NCH], F32)
    colC = sbt("colC", [128, NCH], F32)
    sq_all = sbt("sq_all", [128, 256], F32)
    nsq_all = sbt("nsq_all", [128, 256], F32)
    acols = sbt("acols", [128, 512], F32)
    ccols = sbt("ccols", [128, 128], F32)
    CARR = sbt("CARR", [128, 512], F32)
    BCR = sbt("BCR", [32, MT], BF16)

    import os as _osd
    DBGB = (sbt("DBGB", [128, 1024], BF16)
            if (_osd.environ.get("P1DEBUG") or _osd.environ.get("V1DEBUG")
                or _osd.environ.get("P2DEBUG")) else None)
    ROWSC = sbt("ROWSC", [1, 4096], BF16)
    psA = nc.alloc_psum_tensor("psA", [128, 2048], F32).ap()
    psB = nc.alloc_psum_tensor("psB", [128, 1024], F32).ap()
    psT = nc.alloc_psum_tensor("psT", [128, 1024], BF16).ap()
    psU = nc.alloc_psum_tensor("psU", [128, 1024], BF16).ap()

    v, g, s, t, sy = nc.vector, nc.gpsimd, nc.scalar, nc.tensor, nc.sync

    sdma = Sem(nc, "sdma")
    gdma = Sem(nc, "gdma")
    adma = Sem(nc, "adma")
    spe = Sem(nc, "spe")
    sdv = Sem(nc, "sdv")
    sac = Sem(nc, "sac")
    sgp = Sem(nc, "sgp")
    sout = Sem(nc, "sout")

    REG.update(locals())

    def vseal(instr):
        sdv.inc(instr)
        sdv.wait(v)
        return instr

    def sseal(instr):
        sac.inc(instr)
        sac.wait(s)
        return instr

    def castload(dst_ap, src_ap, slow=False):
        return gdma.inc(g.dma_start(out=dst_ap, in_=src_ap,
                                    allow_slow_non_contiguous=slow), 16)

    def load(dst_ap, src_ap, eng=None, sem=None, slow=False):
        e = eng or sy
        sm = sem or sdma
        return sm.inc(e.dma_start(out=dst_ap, in_=src_ap,
                                  allow_slow_non_contiguous=slow), 16)

    def load_cols(dst_ap, vec_d, n, off=0):
        src = bass.AP(vec_d.tensor, vec_d.offset + off, [[1, 128], [128, n]])
        return sdma.inc(sy.dma_start(out=dst_ap, in_=src,
                                     allow_slow_non_contiguous=True), 16)

    constp = sbt("constp", [128, 8], F32)
    cvals = [1.0, -1.0, 0.5, -0.5, 1e-5]
    for i_, val_ in enumerate(cvals):
        v.memset(constp[:, i_:i_ + 1], val_)
        nc.const_aps.register(F32, val_, constp[:, i_:i_ + 1])
    sdv.inc(v.nop())
    sdv.wait(s)

    load(onesbf, onesbf_d)
    load(onesf, onesf_d)
    load(I128, I128_d)
    load(H128, H128_d)
    load(H64B, H64B_d)
    
    import os as _os4
    if not _os4.environ.get("SKIP_COLS"):
      for k in range(4):
        src = bass.AP(Wconv_d.tensor, Wconv_d.offset + k, [[4, 128], [512, 32]])
        sdma.inc(sy.dma_start(out=ccols[:, k * 32:(k + 1) * 32], in_=src,
                              allow_slow_non_contiguous=True), 16)
    bdt_cols = sbt("bdt_cols", [128, 32], F32)
    bcv_cols = sbt("bcv_cols", [128, 32], F32)
    dv_cols = sbt("dv_cols", [128, 32], F32)
    if not _os4.environ.get("SKIP_COLS"):
        load_cols(bdt_cols, bdt_d, 32)
        load_cols(bcv_cols, bconv_d, 32)
        load_cols(dv_cols, Dv_d, 32)
        srcA = bass.AP(Alog_d.tensor, Alog_d.offset,
                       [[NST, 128], [128 * NST, 32], [1, NST]])
        sdma.inc(sy.dma_start(out=acols, in_=srcA, allow_slow_non_contiguous=True), 16)
    for i in range(NCH):
        castload(x_tm[i], x_d[i * 128:(i + 1) * 128, :])
    sdma.wait(sy); sdma.wait(t); sdma.wait(v); sdma.wait(s)
    gdma.wait(g)
    nc.all_engine_barrier()
    import os as _os2
    if not _os2.environ.get("SKIP_ACOLS"):
        sac.inc(s.activation(out=acols, in_=acols, func=AF.Exp))
        sac.wait(v)
        sdv.inc(v.tensor_scalar(out=acols, in0=acols, scalar1=-1.0, scalar2=None,
                                op0=OP.mult))

    for ch in range(FH // 128 if not skip_prep else 0):
        wb = WB1 if ch % 2 == 0 else WB2
        if ch >= 2:
            spe.wait(g)
        castload(wb[:, 0:D], W2TR_d[ch * 128:(ch + 1) * 128, :])
        gdma.wait(t)
        sdv.wait(t)
        for gc in range(4):
            spe.inc(t.matmul(psA[:, gc * 512:(gc + 1) * 512], H64B,
                             wb[:, gc * 512:(gc + 1) * 512], start=True, stop=True))
        spe.wait(v)
        sdma.wait(v)
        sdv.inc(v.tensor_scalar(out=sbfB[:, 0:D], in0=psA, scalar1=SQ8192,
                                scalar2=None, op0=OP.mult))
        sdv.wait(sy)
        dst = bass.AP(w2b_s.tensor, w2b_s.offset + (2 * ch) * D,
                      [[D, 2], [128 * D, 64], [1, D]])
        sdma.inc(sy.dma_start(out=dst, in_=sbfB[:, 0:D]), 16)

    for ic in range(NCH if not skip_prep else 0):
        if ic > 0:
            sdv.wait(g)
            sac.wait(g)
        castload(WB0[:, 0:D], wgt_d[ic * 128:(ic + 1) * 128, :])
        gdma.wait(s)
        gdma.wait(v)
        sac.inc(s.activation(out=SCRA, in_=WB0[:, 0:D], func=AF.Abs))
        sac.wait(v)
        vseal(v.reduce_sum(colA[:, 0:1], SCRA, axis=AX))
        vseal(v.tensor_scalar(out=colA[:, 0:1], in0=colA[:, 0:1], scalar1=1.0 / D,
                              scalar2=None, op0=OP.mult))
        vseal(v.tensor_scalar(out=colB[:, 0:1], in0=colA[:, 0:1], scalar1=1e-8,
                              scalar2=None, op0=OP.add))
        vseal(v.reciprocal(colB[:, 1:2], colB[:, 0:1]))
        v.tensor_scalar(out=SCRA, in0=WB0[:, 0:D], scalar1=colB[:, 1:2],
                        scalar2=None, op0=OP.mult)
        v.tensor_copy(SCRI, SCRA)
        v.scalar_tensor_tensor(out=SCRB, in0=SCRA, scalar=-0.5, in1=SCRI,
                               op0=OP.add, op1=OP.is_ge)
        v.tensor_add(SCRB, SCRI, SCRB)
        for hh in range(2):
            v.scalar_tensor_tensor(out=SCRD, in0=SCRA[:, hh * 1024:(hh + 1) * 1024],
                                   scalar=0.5, in1=SCRI[:, hh * 1024:(hh + 1) * 1024],
                                   op0=OP.add, op1=OP.is_le)
            v.tensor_sub(SCRB[:, hh * 1024:(hh + 1) * 1024],
                         SCRB[:, hh * 1024:(hh + 1) * 1024], SCRD)
        v.tensor_scalar(out=SCRB, in0=SCRB, scalar1=-1.0, scalar2=1.0,
                        op0=OP.max, op1=OP.min)
        sdv.inc(v.tensor_scalar(out=sbfA[:, 0:D], in0=SCRB, scalar1=colA[:, 0:1],
                                scalar2=None, op0=OP.mult))
        sdv.wait(t)
        sdma.wait(v)
        for kc in range(NCH):
            slot = psT if kc % 2 == 0 else psU
            sdv.wait(t, lag=1)
            t.transpose(slot[:, 0:128], sbfA[:, kc * 128:(kc + 1) * 128], I128)
            spe.inc(t.nop())
            spe.wait(v)
            sdv.inc(v.tensor_copy(sbfB[:, kc * 128:(kc + 1) * 128], slot[:, 0:128]))
        sdv.wait(sy)
        for kc in range(NCH):
            sdma.inc(sy.dma_start(
                out=wtT_s[kc * 128:(kc + 1) * 128, ic * 128:(ic + 1) * 128],
                in_=sbfB[:, kc * 128:(kc + 1) * 128]), 16)

    sdv.wait(g)
    sac.wait(g)
    for src_d, dst_s, rows, cols in (() if skip_prep else
                                     ((WinT_d, winb_s, D, 2 * DI),
                                      (WoutT_d, woutb_s, DI, D))):
        for rc in range(rows // 128):
            for cb in range(0, cols, 4096):
                cw = min(4096, cols - cb)
                gdma.wait(g)
                castload(WB0[:, 0:cw], src_d[rc * 128:(rc + 1) * 128, cb:cb + cw])
                gdma.wait(g)
                gdma.inc(g.dma_start(out=dst_s[rc * 128:(rc + 1) * 128, cb:cb + cw],
                                     in_=WB0[:, 0:cw]), 16)
    nc.all_engine_barrier()
    if upto == 0:
        _emit_dump(nc, REG, dump)
        _finalize(nc, REG)
        return nc

    cndT = [ARN[:, i * KV:(i + 1) * KV] for i in range(NCH)]
    if skip_p1:
        nc.all_engine_barrier()
    else:
        pass
    if not skip_p1:
      for i in range(NCH):
        castload(cndT[i], condT_d[i * 128:(i + 1) * 128, :])
      castload(ROWS[:, 0:2 * D], row_ap(bkv_d, 2 * D))
      gdma.wait(t)
      sdma.wait(t)
    bkvrow = ROWS[:, 0:2 * D]

    def fq_signed_perpart(src_f32, w, scol, icol, out_bf, scale_back):
        v.tensor_scalar(out=SCRA[:, 0:w], in0=src_f32, scalar1=icol, scalar2=None,
                        op0=OP.mult)
        v.tensor_copy(SCRI[:, 0:w], SCRA[:, 0:w])
        v.scalar_tensor_tensor(out=SCRB[:, 0:w], in0=SCRA[:, 0:w], scalar=-0.5,
                               in1=SCRI[:, 0:w], op0=OP.add, op1=OP.is_ge)
        v.tensor_add(SCRB[:, 0:w], SCRI[:, 0:w], SCRB[:, 0:w])
        for hb in range(0, w, 1024):
            hw = min(1024, w - hb)
            v.scalar_tensor_tensor(out=SCRD[:, 0:hw], in0=SCRA[:, hb:hb + hw],
                                   scalar=0.5, in1=SCRI[:, hb:hb + hw],
                                   op0=OP.add, op1=OP.is_le)
            v.tensor_sub(SCRB[:, hb:hb + hw], SCRB[:, hb:hb + hw], SCRD[:, 0:hw])
        if scale_back:
            sdv.inc(v.tensor_scalar(out=out_bf, in0=SCRB[:, 0:w], scalar1=scol,
                                    scalar2=None, op0=OP.mult))
        else:
            sdv.inc(v.tensor_copy(out_bf, SCRB[:, 0:w]))

    import os as _os9
    _p1dbg = bool(_os9.environ.get("P1DEBUG")) and not skip_p1
    _v1dbg = bool(_os9.environ.get("V1DEBUG")) and not skip_p1
    if _v1dbg:
        _p1dbg = True
    for chalf in range((2 if not _p1dbg else (0 if _v1dbg else 1)) if not skip_p1 else 0):
        for jp in range(4 if not _p1dbg else 1):
            sdv.wait(t)
            for dm in range(NCH):
                wb = WB1 if dm % 2 == 0 else WB2
                spe.wait(g)
                castload(wb[:, 0:1024],
                         Wkv_d[dm * 128:(dm + 1) * 128, chalf * 1024:(chalf + 1) * 1024])
                gdma.wait(t)
                nob = bool(_os9.environ.get("NOBIAS"))
                for jj in range(2):
                    jc = jp * 2 + jj
                    pk = psA[:, jj * 1024:(jj + 1) * 1024]
                    if dm == 0 and not nob:
                        spe.inc(t.matmul(pk[:, 0:512], onesbf,
                                         bkvrow[:, chalf * 1024:chalf * 1024 + 512],
                                         start=True, stop=False))
                        spe.inc(t.matmul(pk[:, 512:1024], onesbf,
                                         bkvrow[:, chalf * 1024 + 512:(chalf + 1) * 1024],
                                         start=True, stop=False))
                    for gc in range(2):
                        spe.inc(t.matmul(pk[:, gc * 512:(gc + 1) * 512],
                                         cndT[dm][:, jc * 128:(jc + 1) * 128],
                                         wb[:, gc * 512:(gc + 1) * 512],
                                         start=(dm == 0 and nob),
                                         stop=(dm == NCH - 1)))
            spe.wait(v)
            if _p1dbg:
                v.tensor_copy(DBGB, psA[:, 0:1024])
            for jj in range(2 if not _p1dbg else 1):
                jc = jp * 2 + jj
                pk = psA[:, jj * 1024:(jj + 1) * 1024]
                vseal(v.reduce_max(colA[:, 0:8], pk.rearrange("p (h q) -> p h q", q=128),
                                   axis=AX, apply_absolute_value=True))
                vseal(v.tensor_scalar(out=colA[:, 0:8], in0=colA[:, 0:8], scalar1=INV127,
                                      scalar2=1e-8, op0=OP.mult, op1=OP.max))
                vseal(v.reciprocal(colB[:, 0:8], colA[:, 0:8]))
                v.tensor_tensor(out=SCRA[:, 0:1024].rearrange("p (h q) -> p h q", q=128),
                                in0=pk.rearrange("p (h q) -> p h q", q=128),
                                in1=colB[:, 0:8].unsqueeze(2).broadcast_to(
                                    (128, 8, 128)),
                                op=OP.mult)
                v.tensor_copy(SCRI[:, 0:1024], SCRA[:, 0:1024])
                v.scalar_tensor_tensor(out=SCRB[:, 0:1024], in0=SCRA[:, 0:1024],
                                       scalar=-0.5, in1=SCRI[:, 0:1024],
                                       op0=OP.add, op1=OP.is_ge)
                v.tensor_add(SCRB[:, 0:1024], SCRI[:, 0:1024], SCRB[:, 0:1024])
                v.scalar_tensor_tensor(out=SCRD, in0=SCRA[:, 0:1024], scalar=0.5,
                                       in1=SCRI[:, 0:1024], op0=OP.add, op1=OP.is_le)
                v.tensor_sub(SCRB[:, 0:1024], SCRB[:, 0:1024], SCRD)
                spe.wait(v)
                sdv.inc(v.tensor_tensor(
                    out=sbfA[:, 0:1024].rearrange("p (h q) -> p h q", q=128),
                    in0=SCRB[:, 0:1024].rearrange("p (h q) -> p h q", q=128),
                    in1=colA[:, 0:8].unsqueeze(2).broadcast_to((128, 8, 128)),
                    op=OP.mult))
                sdv.wait(t)
                sdma.wait(v)
                for hh in range(8):
                    h = chalf * 8 + hh
                    slot = psT if hh % 2 == 0 else psU
                    sdv.wait(t, lag=1)
                    t.transpose(slot[:, 0:128], sbfA[:, hh * 128:(hh + 1) * 128], I128)
                    spe.inc(t.nop())
                    spe.wait(v)
                    sdv.inc(v.tensor_copy(sbfB[:, hh * 128:(hh + 1) * 128],
                                          slot[:, 0:128]))
                sdv.wait(sy)
                for hh in range(8):
                    h = chalf * 8 + hh
                    sdma.inc(sy.dma_start(
                        out=kscT_s[h * 128:(h + 1) * 128, jc * 128:(jc + 1) * 128],
                        in_=sbfB[:, hh * 128:(hh + 1) * 128]), 16)

    if not skip_p1 and (not _p1dbg or _v1dbg):
        load_cols(colC[:, 0:NCH], bkv_d, NCH, off=2048)
    for cc in range((NCH if not _p1dbg else (1 if _v1dbg else 0)) if not skip_p1 else 0):
        spe.wait(g)
        castload(WB1[:, 0:2048].rearrange("p (a b) -> p a b", b=128),
                 Wkv_d[:, 2048 + cc * 128:2048 + (cc + 1) * 128]
                 .rearrange("(a p) b -> p a b", p=128), slow=True)
        gdma.wait(t)
        spe.wait(v, lag=0)
        sdv.wait(t)
        for dm in range(NCH):
            for gc in range(2):
                spe.inc(t.matmul(psA[:, gc * 512:(gc + 1) * 512],
                                 WB1[:, dm * 128:(dm + 1) * 128],
                                 cndT[dm][:, gc * 512:(gc + 1) * 512],
                                 start=(dm == 0), stop=(dm == NCH - 1)))
        spe.wait(v)
        if _v1dbg:
            v.tensor_copy(DBGB, psA[:, 0:1024])
        vcol = colC[:, cc:cc + 1]
        vseal(v.tensor_scalar(out=SCRA[:, 0:1024], in0=psA[:, 0:1024], scalar1=vcol,
                              scalar2=None, op0=OP.add))
        vseal(v.reduce_max(colA[:, 0:1], SCRA[:, 0:1024], axis=AX,
                           apply_absolute_value=True))
        vseal(v.tensor_scalar(out=colA[:, 0:1], in0=colA[:, 0:1], scalar1=INV127,
                              scalar2=1e-8, op0=OP.mult, op1=OP.max))
        vseal(v.reciprocal(colB[:, 0:1], colA[:, 0:1]))
        spe.wait(v)
        fq_signed_perpart(SCRA[:, 0:1024], 1024, colA[:, 0:1], colB[:, 0:1],
                          sbfA[:, 0:1024], True)
        sdv.wait(t)
        sdma.wait(v)
        for jc in range(8):
            slot = psT if jc % 2 == 0 else psU
            sdv.wait(t, lag=1)
            t.transpose(slot[:, 0:128], sbfA[:, jc * 128:(jc + 1) * 128], I128)
            spe.inc(t.nop())
            spe.wait(v)
            sdv.inc(v.tensor_copy(sbfB[:, jc * 128:(jc + 1) * 128], slot[:, 0:128]))
        sdv.wait(sy)
        for jc in range(8):
            sdma.inc(sy.dma_start(
                out=vtok_s[jc * 128:(jc + 1) * 128, cc * 128:(cc + 1) * 128],
                in_=sbfB[:, jc * 128:(jc + 1) * 128]), 16)
    nc.all_engine_barrier()
    if upto == 1:
        _emit_dump(nc, REG, dump)
        _finalize(nc, REG)
        return nc

    def emit_layernorm(src_tiles):
        sdv.wait(s)
        spe.wait(s)
        for ic in range(NCH):
            sdv.inc(v.reduce_sum(colA[:, ic:ic + 1], src_tiles[ic], axis=AX))
            sdv.wait(s)
            sac.inc(s.activation(out=SCRA, in_=src_tiles[ic], func=AF.Square))
            sac.wait(v)
            sdv.inc(v.reduce_sum(colB[:, ic:ic + 1], SCRA, axis=AX))
        sdv.wait(v)
        vseal(v.tensor_scalar(out=ln_m, in0=colA, scalar1=1.0 / D, scalar2=None,
                              op0=OP.mult))
        vseal(v.tensor_scalar(out=colB, in0=colB, scalar1=1.0 / D, scalar2=None,
                              op0=OP.mult))
        vseal(v.tensor_mul(colA, ln_m, ln_m))
        vseal(v.tensor_sub(colB, colB, colA))
        sdv.wait(s)
        sseal(s.activation(out=colB, in_=colB, func=AF.Ln, bias=1e-5, scale=1.0))
        sseal(s.activation(out=ln_r, in_=colB, func=AF.Exp, scale=-0.5))
        sac.wait(v)
        vseal(v.tensor_mul(ln_b, ln_m, ln_r))
        vseal(v.tensor_scalar(out=ln_b, in0=ln_b, scalar1=-1.0, scalar2=None,
                              op0=OP.mult))
        sdv.wait(s)

    def emit_xnT_to_hbm(src_tiles):
        for ic in range(NCH):
            spe.wait(s)
            sac.inc(s.activation(out=sbfA[:, 0:D], in_=src_tiles[ic],
                                 func=AF.Identity, scale=ln_r[:, ic:ic + 1],
                                 bias=ln_b[:, ic:ic + 1]))
            sac.wait(t)
            sdma.wait(v)
            for dc in range(NCH):
                slot = psT if dc % 2 == 0 else psU
                sdv.wait(t, lag=1)
                t.transpose(slot[:, 0:128], sbfA[:, dc * 128:(dc + 1) * 128], I128)
                spe.inc(t.nop())
                spe.wait(v)
                sdv.inc(v.tensor_copy(sbfB[:, dc * 128:(dc + 1) * 128],
                                      slot[:, 0:128]))
            sdv.wait(sy)
            for dc in range(NCH):
                sdma.inc(sy.dma_start(
                    out=xnT_s[dc * 128:(dc + 1) * 128, ic * 128:(ic + 1) * 128],
                    in_=sbfB[:, dc * 128:(dc + 1) * 128]), 16)

    emit_layernorm(x_tm)
    import os as _os3
    if not _os3.environ.get("SKIP_XNT"):
        emit_xnT_to_hbm(x_tm)
    castload(ROWS[:, 0:D], row_ap(bq_d, D))
    gdma.wait(t)
    nc.all_engine_barrier()
    if upto == 2:
        _emit_dump(nc, REG, dump)
        _finalize(nc, REG)
        return nc

    _p2dbg = bool(_os9.environ.get("P2DEBUG"))
    bqrow = ROWS[:, 0:D]
    for dqb in range(8):
        spe.wait(g)
        castload(WB0[:, 0:4096].rearrange("p (a b) -> p a b", b=256),
                 Wq_d[:, dqb * 256:(dqb + 1) * 256]
                 .rearrange("(a p) b -> p a b", p=128), slow=True)
        gdma.wait(t)
        for ic in range(NCH):
            spe.wait(sy)
            load(WB1[:, 0:2048].rearrange("p (a b) -> p a b", b=128),
                 xnT_s[:, ic * 128:(ic + 1) * 128]
                 .rearrange("(a p) b -> p a b", p=128), eng=sy, slow=True)
            sdma.wait(t)
            sdv.wait(t)
            ps_q = psA[:, 0:256]
            spe.inc(t.matmul(ps_q[:, 0:128], onesbf,
                             bqrow[:, dqb * 256:dqb * 256 + 128],
                             start=True, stop=False))
            spe.inc(t.matmul(ps_q[:, 128:256], onesbf,
                             bqrow[:, dqb * 256 + 128:(dqb + 1) * 256],
                             start=False, stop=False))
            for dm in range(NCH):
                spe.inc(t.matmul(ps_q,
                                 WB1[:, dm * 128:(dm + 1) * 128],
                                 WB0[:, dm * 256:(dm + 1) * 256],
                                 start=False, stop=(dm == NCH - 1)))
            spe.wait(v)
            spe.wait(s)
            if _p2dbg and dqb == 0 and ic == 0:
                v.tensor_copy(DBGB[:, 0:256], ps_q)
            vseal(v.reduce_max(colA[:, 0:2],
                               ps_q.rearrange("p (h q) -> p h q", q=128),
                               axis=AX, apply_absolute_value=True))
            vseal(v.tensor_scalar(out=colA[:, 0:2], in0=colA[:, 0:2], scalar1=INV127,
                                  scalar2=1e-8, op0=OP.mult, op1=OP.max))
            vseal(v.reciprocal(colB[:, 0:2], colA[:, 0:2]))
            v.tensor_scalar(out=sq_all[:, ic * 16 + dqb * 2:ic * 16 + dqb * 2 + 2],
                            in0=colA[:, 0:2], scalar1=SCALEQ, scalar2=None,
                            op0=OP.mult)
            v.tensor_tensor(out=SCRA[:, 0:256].rearrange("p (h q) -> p h q", q=128),
                            in0=ps_q.rearrange("p (h q) -> p h q", q=128),
                            in1=colB[:, 0:2].unsqueeze(2)
                            .broadcast_to((128, 2, 128)), op=OP.mult)
            v.tensor_copy(SCRI[:, 0:256], SCRA[:, 0:256])
            v.scalar_tensor_tensor(out=SCRB[:, 0:256], in0=SCRA[:, 0:256],
                                   scalar=-0.5, in1=SCRI[:, 0:256],
                                   op0=OP.add, op1=OP.is_ge)
            v.tensor_add(SCRB[:, 0:256], SCRI[:, 0:256], SCRB[:, 0:256])
            v.scalar_tensor_tensor(out=SCRD[:, 0:256], in0=SCRA[:, 0:256],
                                   scalar=0.5, in1=SCRI[:, 0:256],
                                   op0=OP.add, op1=OP.is_le)
            v.tensor_sub(SCRB[:, 0:256], SCRB[:, 0:256], SCRD[:, 0:256])
            spe.wait(v)
            sdv.inc(v.tensor_copy(sbfA[:, 0:256], SCRB[:, 0:256]))
            sdv.wait(t)
            sdma.wait(v)
            for hh in range(2):
                slot = psT if hh % 2 == 0 else psU
                sdv.wait(t, lag=1)
                t.transpose(slot[:, 0:128],
                            sbfA[:, hh * 128:(hh + 1) * 128], I128)
                spe.inc(t.nop())
                spe.wait(v)
                sdv.inc(v.tensor_copy(sbfB[:, hh * 128:(hh + 1) * 128],
                                      slot[:, 0:128]))
            sdv.wait(sy)
            for hh in range(2):
                dq0 = dqb * 256 + hh * 128
                sdma.inc(sy.dma_start(
                    out=cqT_s[dq0:dq0 + 128, ic * 128:(ic + 1) * 128],
                    in_=sbfB[:, hh * 128:(hh + 1) * 128]), 16)
    spe.wait(v)
    sdv.inc(v.tensor_scalar(out=nsq_all, in0=sq_all, scalar1=-1.0, scalar2=None,
                            op0=OP.mult))
    nc.all_engine_barrier()
    if upto == 3:
        _emit_dump(nc, REG, dump)
        _finalize(nc, REG)
        return nc

    for h in range(NH):
        sdv.wait(sy)
        spe.wait(sy)
        load(WB0[:, 0:2048], cqT_s[h * 128:(h + 1) * 128, :])
        load(WB1[:, 0:1024], kscT_s[h * 128:(h + 1) * 128, :])
        load(WB2[:, 0:1024].rearrange("p (a b) -> p a b", b=128),
             vtok_s[:, h * 128:(h + 1) * 128].rearrange("(a p) b -> p a b", p=128),
             slow=True)
        sdma.wait(t)
        sdma.wait(s)
        sdma.wait(v)
        for ic in range(NCH):
            par = ic % 2
            psS = psA[:, par * 1024:(par + 1) * 1024]
            sac.wait(t, lag=1)
            spe.inc(t.matmul(psS[:, 0:512], WB0[:, ic * 128:(ic + 1) * 128],
                             WB1[:, 0:512], start=True, stop=True))
            spe.inc(t.matmul(psS[:, 512:1024], WB0[:, ic * 128:(ic + 1) * 128],
                             WB1[:, 512:1024], start=True, stop=True))
            spe.wait(v)
            sqc = sq_all[:, ic * 16 + h:ic * 16 + h + 1]
            nsqc = nsq_all[:, ic * 16 + h:ic * 16 + h + 1]
            vseal(v.reduce_max(colA[:, 0:1], psS, axis=AX))
            vseal(v.tensor_scalar(out=colB[:, 0:1], in0=colA[:, 0:1], scalar1=nsqc,
                                  scalar2=LN255, op0=OP.mult, op1=OP.add))
            sdv.wait(s)
            E = SCRA[:, par * 1024:(par + 1) * 1024]
            sac.inc(s.activation(out=E, in_=psS, func=AF.Exp, scale=sqc,
                                 bias=colB[:, 0:1]))
            sac.wait(v)
            Ri = SCRI[:, par * 1024:(par + 1) * 1024]
            Cg = SCRB[:, par * 1024:(par + 1) * 1024]
            vseal(v.reduce_sum(colC[:, 0:1], E, axis=AX))
            vseal(v.reciprocal(colC[:, 1:2], colC[:, 0:1]))
            v.tensor_copy(Ri, E)
            v.scalar_tensor_tensor(out=Cg, in0=E, scalar=-0.5, in1=Ri,
                                   op0=OP.add, op1=OP.is_ge)
            v.tensor_add(E, Ri, Cg)
            ca = sbfA if par == 0 else sbfB
            sdma.wait(v, lag=80)
            adma.wait(v, lag=64)
            sdv.inc(v.tensor_scalar(out=ca[:, 0:1024], in0=E, scalar1=colC[:, 1:2],
                                    scalar2=None, op0=OP.mult))
            sdv.wait(sy)
            sdv.wait(s)
            caT = ca[:, 1024:2048]
            for jc in range(8):
                src = ca[:, jc * 128:(jc + 1) * 128]
                dst = caT[:, jc * 128:(jc + 1) * 128]
                if jc % 2 == 0:
                    sdma.inc(sy.dma_start_transpose(dst, src), 16)
                else:
                    adma.inc(s.dma_start_transpose(dst, src), 16)
            sdma.wait(t)
            adma.wait(t)
            sdv.wait(t, lag=5)
            psO = psB[:, par * 512:par * 512 + 128]
            for jc in range(8):
                spe.inc(t.matmul(psO, WB2[:, jc * 128:(jc + 1) * 128],
                                 caT[:, jc * 128:(jc + 1) * 128],
                                 start=(jc == 0), stop=(jc == 7)))
            spe.wait(v)
            sdv.inc(v.tensor_copy(sbfA[:, 1024:1152] if par == 0 else
                                  sbfB[:, 1024:1152], psO))
            sdv.wait(sy)
            sdma.inc(sy.dma_start(
                out=otT_s[h * 128:(h + 1) * 128, ic * 128:(ic + 1) * 128],
                in_=(sbfA if par == 0 else sbfB)[:, 1024:1152]), 16)
    castload(ROWS[:, 0:D], row_ap(bp_d, D))
    gdma.wait(t)
    nc.all_engine_barrier()
    if upto == 4:
        _emit_dump(nc, REG, dump)
        _finalize(nc, REG)
        return nc

    bprow = ROWS[:, 0:D]
    for ig in range(8):
        sdv.wait(sy)
        spe.wait(sy)
        for icb in range(2):
            ic = ig * 2 + icb
            load(WB0[:, icb * 2048:icb * 2048 + 2048]
                 .rearrange("p (a b) -> p a b", b=128),
                 otT_s[:, ic * 128:(ic + 1) * 128]
                 .rearrange("(a p) b -> p a b", p=128), eng=sy, slow=True)
        sdma.wait(t)
        for dpb in range(4):
            sdv.wait(t)
            spe.wait(g)
            castload(WB1[:, 0:512], Wp_d[0:128, dpb * 512:(dpb + 1) * 512])
            for dc in range(NCH):
                wb = WB1 if dc % 2 == 0 else WB2
                if dc + 1 < NCH:
                    other = WB2 if dc % 2 == 0 else WB1
                    spe.wait(g)
                    castload(other[:, 0:512],
                             Wp_d[(dc + 1) * 128:(dc + 2) * 128,
                                  dpb * 512:(dpb + 1) * 512])
                gdma.wait(t, lag=16 if dc + 1 < NCH else 0)
                for icb in range(2):
                    psP = psA[:, icb * 512:(icb + 1) * 512]
                    if dc == 0:
                        spe.inc(t.matmul(psP[:, 0:128], onesbf,
                                         bprow[:, dpb * 512:dpb * 512 + 128],
                                         start=True, stop=False))
                        spe.inc(t.matmul(psP[:, 128:512], onesbf,
                                         bprow[:, dpb * 512 + 128:(dpb + 1) * 512],
                                         start=False, stop=False))
                    spe.inc(t.matmul(psP,
                                     WB0[:, icb * 2048 + dc * 128:icb * 2048 + (dc + 1) * 128],
                                     wb[:, 0:512], start=False, stop=(dc == NCH - 1)))
            spe.wait(v)
            for icb in range(2):
                ic = ig * 2 + icb
                psP = psA[:, icb * 512:(icb + 1) * 512]
                xs = x_tm[ic][:, dpb * 512:(dpb + 1) * 512]
                sdv.inc(v.scalar_tensor_tensor(out=xs, in0=psP, scalar=1.0, in1=xs,
                                               op0=OP.mult, op1=OP.add))
    nc.all_engine_barrier()
    if upto == 5:
        _emit_dump(nc, REG, dump)
        _finalize(nc, REG)
        return nc

    xsl = [ARN[:, i * 512:(i + 1) * 512] for i in range(NCH)]
    v2 = [ARN[:, 8192 + i * 512:8192 + (i + 1) * 512] for i in range(DCH)]
    w2c = [ARN[:, 24576 + i * 512:24576 + (i + 1) * 512] for i in range(NCH)]
    for ft in range(4):
        sdv.wait(sy)
        spe.wait(sy)
        for dm in range(NCH):
            load(xsl[dm], xnT_s[dm * 128:(dm + 1) * 128, ft * 512:(ft + 1) * 512])
        sdma.wait(t)
        for half in range(2):
            sdv.wait(t)
            def _ldw13(hc2, a2):
                castload(WB0[:, a2 * 2048:(a2 + 1) * 2048]
                         .rearrange("p (a b) -> p a b", b=128),
                         W1T_d[:, hc2 * 128:(hc2 + 1) * 128]
                         .rearrange("(a p) b -> p a b", p=128), slow=True)
                castload((WB1 if a2 == 0 else WB2)[:, 0:2048]
                         .rearrange("p (a b) -> p a b", b=128),
                         W3T_d[:, hc2 * 128:(hc2 + 1) * 128]
                         .rearrange("(a p) b -> p a b", p=128), slow=True)
            spe.wait(g)
            sdv.wait(g)
            sac.wait(g)
            _ldw13(half * DCH, 0)
            for hcl in range(DCH):
                hc = half * DCH + hcl
                a = hcl % 2
                w1b = WB0[:, a * 2048:(a + 1) * 2048]
                w3b = WB1 if a == 0 else WB2
                if hcl + 1 < DCH:
                    spe.wait(g)
                    _ldw13(hc + 1, 1 - a)
                gdma.wait(t, lag=32 if hcl + 1 < DCH else 0)
                sac.wait(t)
                sdv.wait(t)
                for dm in range(NCH):
                    spe.inc(t.matmul(psA[:, 1024:1536], w1b[:, dm * 128:(dm + 1) * 128],
                                     xsl[dm], start=(dm == 0), stop=(dm == NCH - 1)))
                for dm in range(NCH):
                    spe.inc(t.matmul(psA[:, 1536:2048], w3b[:, dm * 128:(dm + 1) * 128],
                                     xsl[dm], start=(dm == 0), stop=(dm == NCH - 1)))
                spe.wait(s)
                sac.inc(s.activation(out=sbfA[:, 0:512], in_=psA[:, 1024:1536],
                                     func=AF.Silu))
                sac.wait(v)
                sdv.inc(v.tensor_mul(sbfB[:, 0:512], sbfA[:, 0:512],
                                     psA[:, 1536:2048]))
                sdv.wait(t)
                spe.inc(t.matmul(psB[:, 0:512], H128, sbfB[:, 0:512],
                                 start=True, stop=True))
                spe.wait(v)
                sdv.inc(v.tensor_copy(v2[hcl], psB[:, 0:512]))
                sdv.wait(t)
            for dpb in range(4):
                for sub in range(2):
                    sdv.wait(sy)
                    spe.wait(sy)
                    for hcl in range(NCH):
                        hh = half * DCH + sub * NCH + hcl
                        load(w2c[hcl],
                             w2b_s[hh * 128:(hh + 1) * 128, dpb * 512:(dpb + 1) * 512])
                    sdma.wait(t)
                    for icq in range(4):
                        ic = ft * 4 + icq
                        psF = psA[:, icq * 512:(icq + 1) * 512]
                        if sub == 0 and icq == 0:
                            sdv.wait(t)
                        for hcl in range(NCH):
                            spe.inc(t.matmul(
                                psF, v2[sub * NCH + hcl][:, icq * 128:(icq + 1) * 128],
                                w2c[hcl], start=(hcl == 0 and sub == 0),
                                stop=(hcl == NCH - 1 and sub == 1)))
                        if sub == 1:
                            spe.wait(v)
                            xs = x_tm[ic][:, dpb * 512:(dpb + 1) * 512]
                            sdv.inc(v.scalar_tensor_tensor(
                                out=xs, in0=psF, scalar=1.0, in1=xs,
                                op0=OP.mult, op1=OP.add))
    nc.all_engine_barrier()
    if upto == 6:
        _emit_dump(nc, REG, dump)
        _finalize(nc, REG)
        return nc

    emit_layernorm(x_tm)
    castload(ROWS[:, 0:D], row_ap(bias1_d, D))
    gdma.wait(t)
    spe.wait(v)
    b1bc = ARN[:, 28672:32768].bitcast(F32)
    sdv.wait(t)
    for gc in range(4):
        spe.inc(t.matmul(psA[:, gc * 512:(gc + 1) * 512], onesbf,
                         ROWS[:, gc * 512:(gc + 1) * 512], start=True, stop=True))
    spe.wait(v)
    sdv.inc(v.tensor_copy(b1bc, psA))
    nc.all_engine_barrier()
    if upto == 7:
        _emit_dump(nc, REG, dump)
        _finalize(nc, REG)
        return nc

    hsT = [ARN[:, i * 256:(i + 1) * 256] for i in range(DCH)]
    gtT = [ARN[:, 8192 + i * 256:8192 + (i + 1) * 256] for i in range(DCH)]
    yT = [ARN[:, 16384 + i * 256:16384 + (i + 1) * 256] for i in range(DCH)]
    xqT = [ARN[:, 24576 + i * 256:24576 + (i + 1) * 256] for i in range(NCH)]
    xn1T = xqT
    sdv.inc(v.memset(CARR, 0.0))
    hs_stash = sbt("hs_stash", [128, DCH * 4], BF16)
    sdv.inc(v.memset(hs_stash, 0.0))
    sdv.wait(t); sdv.wait(s); sdv.wait(g)

    for mt in range(NMT):
        ics = [mt * (MT // 128) + k for k in range(MT // 128)]
        for k, ic in enumerate(ics):
            spe.wait(s)
            sdv.wait(s)
            sac.inc(s.activation(out=sbfA[:, 0:D], in_=x_tm[ic], func=AF.Identity,
                                 scale=ln_r[:, ic:ic + 1], bias=ln_b[:, ic:ic + 1]))
            sac.wait(t)
            for dm in range(NCH):
                slot = psT if dm % 2 == 0 else psU
                sdv.wait(t, lag=1)
                t.transpose(slot[:, 0:128], sbfA[:, dm * 128:(dm + 1) * 128], I128)
                spe.inc(t.nop())
                spe.wait(v)
                sdv.inc(v.tensor_copy(xn1T[dm][:, k * 128:(k + 1) * 128],
                                      slot[:, 0:128]))
            sdv.wait(t)
        spe.wait(sy)
        load(WB1[:, 0:2048].rearrange("p (a b) -> p a b", b=128),
             winb_s[:, 0:128].rearrange("(a p) b -> p a b", p=128),
             eng=sy, slow=True)
        for ec in range(2 * DCH):
            wb = WB1 if ec % 2 == 0 else WB2
            if ec + 1 < 2 * DCH:
                other = WB2 if ec % 2 == 0 else WB1
                spe.wait(sy)
                load(other[:, 0:2048].rearrange("p (a b) -> p a b", b=128),
                     winb_s[:, (ec + 1) * 128:(ec + 2) * 128]
                     .rearrange("(a p) b -> p a b", p=128), eng=sy, slow=True)
            sdma.wait(t, lag=16 if ec + 1 < 2 * DCH else 0)
            par = ec % 2
            psW = psA[:, 1024 + par * 512:1024 + par * 512 + MT]
            sdv.wait(t)
            sac.wait(t)
            for dm in range(NCH):
                spe.inc(t.matmul(psW, wb[:, dm * 128:(dm + 1) * 128],
                                 xn1T[dm][:, 0:MT], start=(dm == 0),
                                 stop=(dm == NCH - 1)))
            spe.wait(v)
            if ec < DCH:
                dc = ec
                st = hs_stash[:, dc * 4:dc * 4 + 4]
                v.tensor_copy(SCRB[:, 0:3], st[:, 1:4])
                vseal(v.tensor_copy(SCRB[:, 3:MT + 3], psW))
                vseal(v.tensor_copy(st[:, 1:4], SCRB[:, MT:MT + 3]))
                v.tensor_scalar(out=SCRA[:, 0:MT], in0=SCRB[:, 3:MT + 3],
                                scalar1=ccols[:, 96 + dc:96 + dc + 1],
                                scalar2=None, op0=OP.mult)
                ii_ = None
                for kk in range(3):
                    ii_ = v.scalar_tensor_tensor(
                        out=SCRA[:, 0:MT], in0=SCRB[:, kk:MT + kk],
                        scalar=ccols[:, kk * 32 + dc:kk * 32 + dc + 1],
                        in1=SCRA[:, 0:MT], op0=OP.mult, op1=OP.add)
                sdv.inc(ii_)
                sdv.wait(s)
                sac.inc(s.activation(out=hsT[dc], in_=SCRA[:, 0:MT], func=AF.Silu,
                                     bias=bcv_cols[:, dc:dc + 1], scale=1.0))
            else:
                dc = ec - DCH
                sdv.inc(v.tensor_copy(gtT[dc], psW))
        sac.wait(t)
        sdv.wait(t)
        sac.wait(v)
        spe.wait(g)
        castload(WB1[:, 0:160], WxT_d[0:128, :])
        for dc in range(DCH):
            wb = WB1 if dc % 2 == 0 else WB2
            if dc + 1 < DCH:
                other = WB2 if dc % 2 == 0 else WB1
                spe.wait(g)
                castload(other[:, 0:160], WxT_d[(dc + 1) * 128:(dc + 2) * 128, :])
            gdma.wait(t, lag=16 if dc + 1 < DCH else 0)
            spe.inc(t.matmul(psB[:, 0:MT], wb[:, 0:128], hsT[dc],
                             start=(dc == 0), stop=(dc == DCH - 1)))
            spe.inc(t.matmul(psB[0:32, 512:512 + MT], wb[:, 128:160], hsT[dc],
                             start=(dc == 0), stop=(dc == DCH - 1)))
        spe.wait(v)
        sdv.inc(v.tensor_copy(sbfB[:, 0:MT], psB[:, 0:MT]))
        sdv.inc(v.tensor_copy(BCR, psB[0:32, 512:512 + MT]))
        sdv.wait(g)
        gdma.inc(g.dma_start(out=ROWS[:, 0:16 * MT], in_=BCR[0:16, :]), 16)
        gdma.inc(g.dma_start(out=ROWSC[:, 0:16 * MT], in_=BCR[16:32, :]), 16)
        gdma.wait(t)
        sdv.wait(t)
        spe.wait(g)
        castload(WB1[:, 0:128], WdtT_d[:, 0:128], slow=True)
        for dc in range(DCH):
            wb = WB1 if dc % 2 == 0 else WB2
            if dc + 1 < DCH:
                other = WB2 if dc % 2 == 0 else WB1
                spe.wait(g)
                castload(other[:, 0:128], WdtT_d[:, (dc + 1) * 128:(dc + 2) * 128],
                         slow=True)
            gdma.wait(t, lag=16 if dc + 1 < DCH else 0)
            sac.wait(t)
            sdv.wait(t)
            spe.inc(t.matmul(psB[:, 512:512 + MT], wb[:, 0:128], sbfB[:, 0:MT],
                             start=True, stop=True))
            spe.wait(s)
            spe.wait(v)
            sac.wait(v)
            sseal(s.activation(out=SCRB[:, 256:256 + MT], in_=psB[:, 512:512 + MT],
                               func=AF.Abs, bias=bdt_cols[:, dc:dc + 1], scale=1.0))
            sseal(s.activation(out=SCRB[:, 256:256 + MT], in_=SCRB[:, 256:256 + MT],
                               func=AF.Exp, scale=-1.0))
            sseal(s.activation(out=SCRB[:, 256:256 + MT], in_=SCRB[:, 256:256 + MT],
                               func=AF.Ln, bias=1.0, scale=1.0))
            sac.wait(v)
            v.tensor_scalar(out=SCRA[:, 0:MT], in0=psB[:, 512:512 + MT],
                            scalar1=bdt_cols[:, dc:dc + 1], scalar2=0.0,
                            op0=OP.add, op1=OP.max)
            vseal(v.tensor_add(SCRA[:, 0:MT], SCRA[:, 0:MT], SCRB[:, 256:256 + MT]))
            sdv.wait(g)
            sdv.inc(v.tensor_mul(sbfA[:, 0:MT], SCRA[:, 0:MT], hsT[dc]))
            sdv.wait(s)
            sdv.wait(g)
            for n in range(NST):
                npar = n % 2
                bsl = psA[:, npar * 1024:npar * 1024 + MT]
                csl = psA[:, npar * 1024 + 512:npar * 1024 + 512 + MT]
                dasl = SCRB[:, npar * 256:npar * 256 + MT]
                tmpsl = sbfB[:, 256 + npar * 256:256 + npar * 256 + MT]
                if n >= 2:
                    sdv.wait(t, lag=4)
                spe.inc(t.matmul(bsl, onesbf, ROWS[:, n * MT:(n + 1) * MT],
                                 start=True, stop=True))
                spe.inc(t.matmul(csl, onesbf, ROWSC[:, n * MT:(n + 1) * MT],
                                 start=True, stop=True))
                if n >= 2:
                    sdv.wait(s, lag=4)
                sac.inc(s.activation(out=dasl, in_=SCRA[:, 0:MT], func=AF.Exp,
                                     scale=acols[:, dc * 16 + n:dc * 16 + n + 1]))
                spe.wait(v, lag=0)
                sdv.inc(v.tensor_mul(SCRD[:, 0:MT], sbfA[:, 0:MT], bsl))
                sac.wait(v)
                cini = CARR[:, dc * 16 + n:dc * 16 + n + 1]
                vseal(v.tensor_tensor_scan(out=SCRB[:, 768:768 + MT],
                                           data0=dasl,
                                           data1=SCRD[:, 0:MT], initial=cini,
                                           op0=OP.mult, op1=OP.add))
                vseal(v.tensor_copy(cini, SCRB[:, 768 + MT - 1:768 + MT]))
                if n >= 2:
                    spe.wait(v, lag=1)
                sdv.inc(v.tensor_mul(tmpsl, SCRB[:, 768:768 + MT], csl))
                sdv.wait(t)
                spe.inc(t.matmul(psB[:, 0:MT], I128, tmpsl,
                                 start=(n == 0), stop=(n == NST - 1)))
            spe.wait(s)
            sac.inc(s.activation(out=sbfB[:, 512:512 + MT], in_=gtT[dc],
                                 func=AF.Silu))
            sac.wait(v)
            spe.wait(v)
            v.scalar_tensor_tensor(out=SCRB[:, 0:MT], in0=hsT[dc],
                                   scalar=dv_cols[:, dc:dc + 1], in1=psB[:, 0:MT],
                                   op0=OP.mult, op1=OP.add)
            sdv.inc(v.tensor_mul(yT[dc], SCRB[:, 0:MT], sbfB[:, 512:512 + MT]))
            sdv.wait(t)
        for k, ic in enumerate(ics):
            spe.wait(s)
            sdv.wait(s)
            sac.inc(s.activation(out=sbfA[:, 0:D], in_=x_tm[ic], func=AF.Identity,
                                 scale=ln_r[:, ic:ic + 1], bias=ln_b[:, ic:ic + 1]))
            sseal(s.activation(out=SCRB, in_=sbfA[:, 0:D], func=AF.Square))
            sac.wait(v)
            vseal(v.reduce_sum(colA[:, 0:1], SCRB, axis=AX))
            vseal(v.tensor_scalar(out=colA[:, 0:1], in0=colA[:, 0:1],
                                  scalar1=1.0 / D, scalar2=1e-8,
                                  op0=OP.mult, op1=OP.add))
            sdv.wait(s)
            sseal(s.activation(out=colA[:, 1:2], in_=colA[:, 0:1], func=AF.Ln))
            sseal(s.activation(out=colA[:, 2:3], in_=colA[:, 1:2], func=AF.Exp,
                               scale=0.5))
            sac.wait(v)
            vseal(v.reduce_max(colA[:, 3:4], sbfA[:, 0:D], axis=AX,
                               apply_absolute_value=True))
            vseal(v.reciprocal(colA[:, 4:5], colA[:, 2:3]))
            vseal(v.tensor_mul(colA[:, 5:6], colA[:, 3:4], colA[:, 4:5]))
            vseal(v.tensor_scalar(out=colA[:, 5:6], in0=colA[:, 5:6], scalar1=INV127,
                                  scalar2=1e-8, op0=OP.mult, op1=OP.add))
            vseal(v.tensor_mul(colA[:, 6:7], colA[:, 2:3], colA[:, 5:6]))
            vseal(v.reciprocal(colA[:, 7:8], colA[:, 6:7]))
            vseal(v.tensor_mul(colA[:, 8:9], colA[:, 3:4], colA[:, 4:5]))
            vseal(v.tensor_scalar(out=colA[:, 8:9], in0=colA[:, 8:9], scalar1=INV127,
                                  scalar2=None, op0=OP.mult))
            for hb in range(2):
                sl = slice(hb * 1024, (hb + 1) * 1024)
                v.tensor_scalar(out=SCRB[:, sl], in0=sbfA[:, sl],
                                scalar1=colA[:, 7:8], scalar2=None, op0=OP.mult)
                v.tensor_copy(SCRI[:, sl], SCRB[:, sl])
                v.scalar_tensor_tensor(out=SCRD, in0=SCRB[:, sl], scalar=-0.5,
                                       in1=SCRI[:, sl], op0=OP.add, op1=OP.is_ge)
                v.tensor_add(SCRD, SCRI[:, sl], SCRD)
                v.scalar_tensor_tensor(out=SCRB[:, sl], in0=SCRB[:, sl], scalar=0.5,
                                       in1=SCRI[:, sl], op0=OP.add, op1=OP.is_le)
                v.tensor_sub(SCRD, SCRD, SCRB[:, sl])
                spe.wait(v)
                sdv.inc(v.tensor_copy(sbfB[:, sl], SCRD))
            sdv.wait(t)
            for dm in range(NCH):
                slot = psT if dm % 2 == 0 else psU
                sdv.wait(t, lag=1)
                t.transpose(slot[:, 0:128], sbfB[:, dm * 128:(dm + 1) * 128], I128)
                spe.inc(t.nop())
                spe.wait(v)
                sdv.inc(v.tensor_copy(xqT[dm][:, k * 128:(k + 1) * 128],
                                      slot[:, 0:128]))
            sdv.wait(t)
            spe.wait(sy)
            load(WB1[:, 0:2048], wtT_s[0:128, :], eng=sy)
            sdv.wait(t)
            for kc in range(NCH):
                wb = WB1 if kc % 2 == 0 else WB2
                if kc + 1 < NCH:
                    other = WB2 if kc % 2 == 0 else WB1
                    spe.wait(sy)
                    load(other[:, 0:2048],
                         wtT_s[(kc + 1) * 128:(kc + 2) * 128, :], eng=sy)
                sdma.wait(t, lag=16 if kc + 1 < NCH else 0)
                for gc in range(4):
                    spe.inc(t.matmul(psA[:, gc * 512:(gc + 1) * 512],
                                     xqT[kc][:, k * 128:(k + 1) * 128],
                                     wb[:, gc * 512:(gc + 1) * 512],
                                     start=(kc == 0), stop=(kc == NCH - 1)))
            spe.wait(v)
            spe.wait(s)
            if DBGB is not None and mt == 0 and k == 0:
                v.tensor_copy(DBGB[:, 0:512], psA[:, 1536:2048])
            sac.inc(s.activation(out=sbfA[:, 0:D], in_=psA, func=AF.Identity,
                                 scale=colA[:, 8:9]))
            sac.wait(t)
            sac.wait(v)
            spe.wait(sy)
            load(WB1[:, 0:2048], woutb_s[0:128, :], eng=sy)
            for dc in range(DCH):
                wb = WB1 if dc % 2 == 0 else WB2
                if dc + 1 < DCH:
                    other = WB2 if dc % 2 == 0 else WB1
                    spe.wait(sy)
                    load(other[:, 0:2048],
                         woutb_s[(dc + 1) * 128:(dc + 2) * 128, :], eng=sy)
                sdma.wait(t, lag=16 if dc + 1 < DCH else 0)
                for gc in range(4):
                    spe.inc(t.matmul(psA[:, gc * 512:(gc + 1) * 512],
                                     yT[dc][:, k * 128:(k + 1) * 128],
                                     wb[:, gc * 512:(gc + 1) * 512],
                                     start=(dc == 0), stop=(dc == DCH - 1)))
            spe.wait(v)
            if DBGB is not None and mt == 0 and k == 0:
                v.tensor_copy(DBGB[:, 512:1024], psA[:, 1536:2048])
            sout.wait(v)
            v.scalar_tensor_tensor(out=SCRA, in0=psA, scalar=1.0, in1=b1bc,
                                   op0=OP.mult, op1=OP.add)
            v.tensor_add(SCRA, SCRA, x_tm[ic])
            sdv.inc(v.tensor_add(SCRA, SCRA, sbfA[:, 0:D]))
            sdv.wait(sy)
            sout.inc(sy.dma_start(out=out_d[ic * 128:(ic + 1) * 128, :], in_=SCRA), 16)
    nc.all_engine_barrier()
    sout.wait(sy)
    _emit_dump(nc, REG, dump)
    return nc


def _emit_dump(nc, R, dump):
    if not dump:
        return
    nc.all_engine_barrier()
    sy = R["sy"]
    sdma = R["sdma"]
    for name in dump:
        obj = R[name]
        if isinstance(obj, list):
            w = obj[0].shape[1]
            dt = obj[0].dtype
            dd = nc.dram_tensor(f"dbg_{name}", [len(obj), 128, w], dt,
                                kind="ExternalOutput").ap()
            for i, ap in enumerate(obj):
                sdma.inc(sy.dma_start(out=dd[i], in_=ap), 16)
        else:
            sh = list(obj.shape)
            dd = nc.dram_tensor(f"dbg_{name}", sh, obj.dtype,
                                kind="ExternalOutput").ap()
            sdma.inc(sy.dma_start(out=dd, in_=obj), 16)
    sdma.wait(sy)


def _finalize(nc, R):
    nc.all_engine_barrier()
    sy, v = R["sy"], R["v"]
    sdv, sout = R["sdv"], R["sout"]
    out_d, x_tm = R["out_d"], R["x_tm"]
    scro = R["SCRA"]
    for ic in range(NCH):
        sout.wait(v)
        sdv.inc(v.tensor_copy(scro, x_tm[ic]))
        sdv.wait(sy)
        sout.inc(sy.dma_start(out=out_d[ic * 128:(ic + 1) * 128, :], in_=scro), 16)
    sout.wait(sy)


def _host_layout(inputs):
    f32 = np.float32
    x = np.ascontiguousarray(np.asarray(inputs["L0_attn_x"], dtype=f32))
    cond = np.asarray(inputs["L0_attn_cond"], dtype=f32).reshape(KV, D)
    condT = np.ascontiguousarray(cond.T)
    W2 = np.asarray(inputs["L0_ffn_W2"], dtype=f32)
    W2TR = np.ascontiguousarray(
        W2.T.reshape(64, 128, D).transpose(1, 0, 2).reshape(FH, D))
    H128 = _had(128).astype(ml_dtypes.bfloat16)
    H64 = _had(64) * (2.0 ** -7)
    H64B = np.zeros((128, 128), np.float64)
    H64B[:64, :64] = H64
    H64B[64:, 64:] = H64
    common = {
        "condT": condT,
        "Wq": np.ascontiguousarray(np.asarray(inputs["L0_attn_Wq"], dtype=f32)),
        "bq": np.ascontiguousarray(np.asarray(inputs["L0_attn_bq"], dtype=f32)),
        "Wkv": np.ascontiguousarray(np.asarray(inputs["L0_attn_Wkv"], dtype=f32)),
        "bkv": np.ascontiguousarray(np.asarray(inputs["L0_attn_bkv"], dtype=f32)),
        "Wp": np.ascontiguousarray(np.asarray(inputs["L0_attn_Wp"], dtype=f32)),
        "bp": np.ascontiguousarray(np.asarray(inputs["L0_attn_bp"], dtype=f32)),
        "W1T": np.ascontiguousarray(np.asarray(inputs["L0_ffn_W1"], dtype=f32).T),
        "W3T": np.ascontiguousarray(np.asarray(inputs["L0_ffn_W3"], dtype=f32).T),
        "W2TR": W2TR,
        "WinT": np.ascontiguousarray(np.asarray(inputs["L1_attn_Win"], dtype=f32).T),
        "Wconv": np.ascontiguousarray(
            np.asarray(inputs["L1_attn_Wconv"], dtype=f32).reshape(DI, 4)),
        "bconv": np.ascontiguousarray(np.asarray(inputs["L1_attn_bconv"], dtype=f32)),
        "WxT": np.ascontiguousarray(np.asarray(inputs["L1_attn_Wx"], dtype=f32).T),
        "WdtT": np.ascontiguousarray(np.asarray(inputs["L1_attn_Wdt"], dtype=f32).T),
        "bdt": np.ascontiguousarray(np.asarray(inputs["L1_attn_bdt"], dtype=f32)),
        "WoutT": np.ascontiguousarray(np.asarray(inputs["L1_attn_Wout"], dtype=f32).T),
        "Alog": np.ascontiguousarray(np.asarray(inputs["L1_attn_A_log"], dtype=f32)),
        "Dvec": np.ascontiguousarray(np.asarray(inputs["L1_attn_D"], dtype=f32)),
        "wgt": np.ascontiguousarray(np.asarray(inputs["L1_ffn_weight"], dtype=f32)),
        "bias1": np.ascontiguousarray(np.asarray(inputs["L1_ffn_bias"], dtype=f32)),
        "onesbf": np.ones((1, 128), ml_dtypes.bfloat16),
        "onesf": np.ones((1, 128), f32),
        "I128": np.eye(128).astype(ml_dtypes.bfloat16),
        "H128": H128,
        "H64B": H64B.astype(ml_dtypes.bfloat16),
    }
    return [dict(common, x=np.ascontiguousarray(x[i])) for i in range(8)]


def kernel(**inputs):
    in_maps = _host_layout(inputs)
    nc = build(UPTO)
    res = run_bass_kernel_spmd(nc, in_maps, core_ids=list(range(8)))
    return np.stack([np.asarray(r["out"], dtype=np.float32) for r in res.results])


# version 05
# speedup vs baseline: 1.5221x; 1.1177x over previous
import numpy as np
import ml_dtypes
import concourse.bass as bass
import concourse.mybir as mybir
from concourse.bass_utils import run_bass_kernel_spmd

F32 = mybir.dt.float32
BF16 = mybir.dt.bfloat16
I32 = mybir.dt.int32
OP = mybir.AluOpType
AF = mybir.ActivationFunctionType
AX = mybir.AxisListType.X

T = 2048
D = 2048
NCH = 16
NH = 16
HD = 128
KV = 1024
FH = 8192
DI = 4096
DCH = 32
NST = 16
MT = 256
NMT = T // MT
FT = 512
SQ8192 = float(np.float32(1.0 / np.sqrt(np.float64(8192.0)) / (2.0 ** -7)))
LN255 = float(np.float32(np.log(np.float64(255.0))))
INV127 = float(np.float32(1.0 / np.float64(127.0)))
SCALEQ = float(np.float32(1.0 / np.sqrt(np.float64(128.0))))

UPTO = 99


def _had(n):
    H = np.ones((n, n), np.float64)
    for i in range(n):
        for j in range(n):
            if bin(i & j).count("1") % 2:
                H[i, j] = -1.0
    return H


class Sem:
    def __init__(self, nc, name):
        self.h = nc.alloc_semaphore(name)
        self.n = 0

    def inc(self, instr, k=1):
        instr.then_inc(self.h, k)
        self.n += k
        return instr

    def wait(self, eng, lag=0):
        if self.n - lag > 0:
            eng.wait_ge(self.h, self.n - lag)


def row_ap(vec_d, n, off=0):
    return bass.AP(vec_d.tensor, vec_d.offset + off, [[n, 1], [1, n]])


def build(upto=99, dump=(), skip_p1=False, skip_prep=False):
    nc = bass.Bass("TRN2", target_bir_lowering=False, detect_race_conditions=False)
    REG = {}

    def din(name, shape, dt=F32):
        return nc.dram_tensor(name, list(shape), dt, kind="ExternalInput").ap()

    x_d = din("x", [T, D])
    condT_d = din("condT", [D, KV])
    Wq_d = din("Wq", [D, D])
    bq_d = din("bq", [D])
    Wkv_d = din("Wkv", [D, 2 * D])
    bkv_d = din("bkv", [2 * D])
    Wp_d = din("Wp", [D, D])
    bp_d = din("bp", [D])
    W1T_d = din("W1T", [D, FH])
    W3T_d = din("W3T", [D, FH])
    W2TR_d = din("W2TR", [FH, D])
    WinT_d = din("WinT", [D, 2 * DI])
    Wconv_d = din("Wconv", [DI, 4])
    bconv_d = din("bconv", [DI])
    WxT_d = din("WxT", [DI, 160])
    WdtT_d = din("WdtT", [128, DI])
    bdt_d = din("bdt", [DI])
    WoutT_d = din("WoutT", [DI, D])
    Alog_d = din("Alog", [DI, NST])
    Dv_d = din("Dvec", [DI])
    wgt_d = din("wgt", [D, D])
    bias1_d = din("bias1", [D])
    onesbf_d = din("onesbf", [1, 128], BF16)
    onesf_d = din("onesf", [1, 128])
    I128_d = din("I128", [128, 128], BF16)
    H128_d = din("H128", [128, 128], BF16)
    H64B_d = din("H64B", [128, 128], BF16)

    out_d = nc.dram_tensor("out", [T, D], F32, kind="ExternalOutput").ap()

    def dscr(name, shape, dt=BF16):
        return nc.dram_tensor(name, list(shape), dt, kind="Internal").ap()

    xnT_s = dscr("xnT_s", [D, T])
    w2b_s = dscr("w2b_s", [FH, D])
    kscT_s = dscr("kscT_s", [D, KV])
    vtok_s = dscr("vtok_s", [KV, D])
    cqT_s = dscr("cqT_s", [D, T])
    otT_s = dscr("otT_s", [D, T])
    wtT_s = dscr("wtT_s", [D, D])
    winb_s = dscr("winb_s", [D, 2 * DI])
    woutb_s = dscr("woutb_s", [DI, D])

    def sbt(name, shape, dt):
        return nc.alloc_sbuf_tensor(name, list(shape), dt).ap()

    x_tm = [sbt(f"x_tm{i}", [128, D], BF16) for i in range(NCH)]
    ARN = sbt("ARN", [128, 32768], BF16)
    ROWS = sbt("ROWS", [1, 4096], BF16)
    SCRA = sbt("SCRA", [128, D], F32)
    SCRB = sbt("SCRB", [128, D], F32)
    SCRD = sbt("SCRD", [128, 1024], F32)
    SCRI = sbt("SCRI", [128, D], I32)
    sbfA = sbt("sbfA", [128, D], BF16)
    sbfB = sbt("sbfB", [128, D], BF16)
    WB0 = sbt("WB0", [128, 4096], BF16)
    WB1 = sbt("WB1", [128, 2048], BF16)
    WB2 = sbt("WB2", [128, 2048], BF16)
    onesbf = sbt("onesbf_t", [1, 128], BF16)
    onesf = sbt("onesf_t", [1, 128], F32)
    I128 = sbt("I128_t", [128, 128], BF16)
    H128 = sbt("H128_t", [128, 128], BF16)
    H64B = sbt("H64B_t", [128, 128], BF16)
    ln_m = sbt("ln_m", [128, NCH], F32)
    ln_r = sbt("ln_r", [128, NCH], F32)
    ln_b = sbt("ln_b", [128, NCH], F32)
    colA = sbt("colA", [128, NCH], F32)
    colB = sbt("colB", [128, NCH], F32)
    colC = sbt("colC", [128, NCH], F32)
    sq_all = sbt("sq_all", [128, 256], F32)
    nsq_all = sbt("nsq_all", [128, 256], F32)
    acols = sbt("acols", [128, 512], F32)
    ccols = sbt("ccols", [128, 128], F32)
    CARR = sbt("CARR", [128, 512], F32)
    BCR = sbt("BCR", [32, MT], BF16)

    import os as _osd
    DBGB = (sbt("DBGB", [128, 1024], BF16)
            if (_osd.environ.get("P1DEBUG") or _osd.environ.get("V1DEBUG")
                or _osd.environ.get("P2DEBUG")) else None)
    ROWSC = sbt("ROWSC", [1, 4096], BF16)
    psA = nc.alloc_psum_tensor("psA", [128, 2048], F32).ap()
    psB = nc.alloc_psum_tensor("psB", [128, 1024], F32).ap()
    psT = nc.alloc_psum_tensor("psT", [128, 1024], BF16).ap()
    psU = nc.alloc_psum_tensor("psU", [128, 1024], BF16).ap()

    v, g, s, t, sy = nc.vector, nc.gpsimd, nc.scalar, nc.tensor, nc.sync

    sdma = Sem(nc, "sdma")
    gdma = Sem(nc, "gdma")
    adma = Sem(nc, "adma")
    spe = Sem(nc, "spe")
    sdv = Sem(nc, "sdv")
    sac = Sem(nc, "sac")
    sgp = Sem(nc, "sgp")
    sout = Sem(nc, "sout")

    REG.update(locals())

    def vseal(instr):
        sdv.inc(instr)
        sdv.wait(v)
        return instr

    def sseal(instr):
        sac.inc(instr)
        sac.wait(s)
        return instr

    def castload(dst_ap, src_ap, slow=False):
        return gdma.inc(g.dma_start(out=dst_ap, in_=src_ap,
                                    allow_slow_non_contiguous=slow), 16)

    def load(dst_ap, src_ap, eng=None, sem=None, slow=False):
        e = eng or sy
        sm = sem or sdma
        return sm.inc(e.dma_start(out=dst_ap, in_=src_ap,
                                  allow_slow_non_contiguous=slow), 16)

    def load_cols(dst_ap, vec_d, n, off=0):
        src = bass.AP(vec_d.tensor, vec_d.offset + off, [[1, 128], [128, n]])
        return sdma.inc(sy.dma_start(out=dst_ap, in_=src,
                                     allow_slow_non_contiguous=True), 16)

    constp = sbt("constp", [128, 8], F32)
    cvals = [1.0, -1.0, 0.5, -0.5, 1e-5]
    for i_, val_ in enumerate(cvals):
        v.memset(constp[:, i_:i_ + 1], val_)
        nc.const_aps.register(F32, val_, constp[:, i_:i_ + 1])
    sdv.inc(v.nop())
    sdv.wait(s)

    load(onesbf, onesbf_d)
    load(onesf, onesf_d)
    load(I128, I128_d)
    load(H128, H128_d)
    load(H64B, H64B_d)
    
    import os as _os4
    if not _os4.environ.get("SKIP_COLS"):
      for k in range(4):
        src = bass.AP(Wconv_d.tensor, Wconv_d.offset + k, [[4, 128], [512, 32]])
        sdma.inc(sy.dma_start(out=ccols[:, k * 32:(k + 1) * 32], in_=src,
                              allow_slow_non_contiguous=True), 16)
    bdt_cols = sbt("bdt_cols", [128, 32], F32)
    bcv_cols = sbt("bcv_cols", [128, 32], F32)
    dv_cols = sbt("dv_cols", [128, 32], F32)
    if not _os4.environ.get("SKIP_COLS"):
        load_cols(bdt_cols, bdt_d, 32)
        load_cols(bcv_cols, bconv_d, 32)
        load_cols(dv_cols, Dv_d, 32)
        srcA = bass.AP(Alog_d.tensor, Alog_d.offset,
                       [[NST, 128], [128 * NST, 32], [1, NST]])
        sdma.inc(sy.dma_start(out=acols, in_=srcA, allow_slow_non_contiguous=True), 16)
    for i in range(NCH):
        castload(x_tm[i], x_d[i * 128:(i + 1) * 128, :])
    sdma.wait(sy); sdma.wait(t); sdma.wait(v); sdma.wait(s)
    gdma.wait(g)
    nc.all_engine_barrier()
    import os as _os2
    if not _os2.environ.get("SKIP_ACOLS"):
        sac.inc(s.activation(out=acols, in_=acols, func=AF.Exp))
        sac.wait(v)
        sdv.inc(v.tensor_scalar(out=acols, in0=acols, scalar1=-1.0, scalar2=None,
                                op0=OP.mult))

    for ch in range(FH // 128 if not skip_prep else 0):
        wb = WB1 if ch % 2 == 0 else WB2
        if ch >= 2:
            spe.wait(g)
        castload(wb[:, 0:D], W2TR_d[ch * 128:(ch + 1) * 128, :])
        gdma.wait(t)
        sdv.wait(t)
        for gc in range(4):
            spe.inc(t.matmul(psA[:, gc * 512:(gc + 1) * 512], H64B,
                             wb[:, gc * 512:(gc + 1) * 512], start=True, stop=True))
        spe.wait(v)
        sdma.wait(v)
        sdv.inc(v.tensor_scalar(out=sbfB[:, 0:D], in0=psA, scalar1=SQ8192,
                                scalar2=None, op0=OP.mult))
        sdv.wait(sy)
        dst = bass.AP(w2b_s.tensor, w2b_s.offset + (2 * ch) * D,
                      [[D, 2], [128 * D, 64], [1, D]])
        sdma.inc(sy.dma_start(out=dst, in_=sbfB[:, 0:D]), 16)

    for ic in range(NCH if not skip_prep else 0):
        if ic > 0:
            sdv.wait(g)
            sac.wait(g)
        castload(WB0[:, 0:D], wgt_d[ic * 128:(ic + 1) * 128, :])
        gdma.wait(s)
        gdma.wait(v)
        sac.inc(s.activation(out=SCRA, in_=WB0[:, 0:D], func=AF.Abs))
        sac.wait(v)
        vseal(v.reduce_sum(colA[:, 0:1], SCRA, axis=AX))
        vseal(v.tensor_scalar(out=colA[:, 0:1], in0=colA[:, 0:1], scalar1=1.0 / D,
                              scalar2=None, op0=OP.mult))
        vseal(v.tensor_scalar(out=colB[:, 0:1], in0=colA[:, 0:1], scalar1=1e-8,
                              scalar2=None, op0=OP.add))
        vseal(v.reciprocal(colB[:, 1:2], colB[:, 0:1]))
        v.tensor_scalar(out=SCRA, in0=WB0[:, 0:D], scalar1=colB[:, 1:2],
                        scalar2=None, op0=OP.mult)
        v.tensor_copy(SCRI, SCRA)
        v.scalar_tensor_tensor(out=SCRB, in0=SCRA, scalar=-0.5, in1=SCRI,
                               op0=OP.add, op1=OP.is_ge)
        v.tensor_add(SCRB, SCRI, SCRB)
        for hh in range(2):
            v.scalar_tensor_tensor(out=SCRD, in0=SCRA[:, hh * 1024:(hh + 1) * 1024],
                                   scalar=0.5, in1=SCRI[:, hh * 1024:(hh + 1) * 1024],
                                   op0=OP.add, op1=OP.is_le)
            v.tensor_sub(SCRB[:, hh * 1024:(hh + 1) * 1024],
                         SCRB[:, hh * 1024:(hh + 1) * 1024], SCRD)
        v.tensor_scalar(out=SCRB, in0=SCRB, scalar1=-1.0, scalar2=1.0,
                        op0=OP.max, op1=OP.min)
        sdv.inc(v.tensor_scalar(out=sbfA[:, 0:D], in0=SCRB, scalar1=colA[:, 0:1],
                                scalar2=None, op0=OP.mult))
        sdv.wait(t)
        sdma.wait(v)
        for kc in range(NCH):
            slot = psT if kc % 2 == 0 else psU
            sdv.wait(t, lag=1)
            t.transpose(slot[:, 0:128], sbfA[:, kc * 128:(kc + 1) * 128], I128)
            spe.inc(t.nop())
            spe.wait(v)
            sdv.inc(v.tensor_copy(sbfB[:, kc * 128:(kc + 1) * 128], slot[:, 0:128]))
        sdv.wait(sy)
        for kc in range(NCH):
            sdma.inc(sy.dma_start(
                out=wtT_s[kc * 128:(kc + 1) * 128, ic * 128:(ic + 1) * 128],
                in_=sbfB[:, kc * 128:(kc + 1) * 128]), 16)

    sdv.wait(g)
    sac.wait(g)
    for src_d, dst_s, rows, cols in (() if skip_prep else
                                     ((WinT_d, winb_s, D, 2 * DI),
                                      (WoutT_d, woutb_s, DI, D))):
        for rc in range(rows // 128):
            for cb in range(0, cols, 4096):
                cw = min(4096, cols - cb)
                gdma.wait(g)
                castload(WB0[:, 0:cw], src_d[rc * 128:(rc + 1) * 128, cb:cb + cw])
                gdma.wait(g)
                gdma.inc(g.dma_start(out=dst_s[rc * 128:(rc + 1) * 128, cb:cb + cw],
                                     in_=WB0[:, 0:cw]), 16)
    nc.all_engine_barrier()
    if upto == 0:
        _emit_dump(nc, REG, dump)
        _finalize(nc, REG)
        return nc

    cndT = [ARN[:, i * KV:(i + 1) * KV] for i in range(NCH)]
    if skip_p1:
        nc.all_engine_barrier()
    else:
        pass
    if not skip_p1:
      for i in range(NCH):
        castload(cndT[i], condT_d[i * 128:(i + 1) * 128, :])
      castload(ROWS[:, 0:2 * D], row_ap(bkv_d, 2 * D))
      gdma.wait(t)
      sdma.wait(t)
    bkvrow = ROWS[:, 0:2 * D]

    def fq_signed_perpart(src_f32, w, scol, icol, out_bf, scale_back):
        v.tensor_scalar(out=SCRA[:, 0:w], in0=src_f32, scalar1=icol, scalar2=None,
                        op0=OP.mult)
        v.tensor_copy(SCRI[:, 0:w], SCRA[:, 0:w])
        v.scalar_tensor_tensor(out=SCRB[:, 0:w], in0=SCRA[:, 0:w], scalar=-0.5,
                               in1=SCRI[:, 0:w], op0=OP.add, op1=OP.is_ge)
        v.tensor_add(SCRB[:, 0:w], SCRI[:, 0:w], SCRB[:, 0:w])
        for hb in range(0, w, 1024):
            hw = min(1024, w - hb)
            v.scalar_tensor_tensor(out=SCRD[:, 0:hw], in0=SCRA[:, hb:hb + hw],
                                   scalar=0.5, in1=SCRI[:, hb:hb + hw],
                                   op0=OP.add, op1=OP.is_le)
            v.tensor_sub(SCRB[:, hb:hb + hw], SCRB[:, hb:hb + hw], SCRD[:, 0:hw])
        if scale_back:
            sdv.inc(v.tensor_scalar(out=out_bf, in0=SCRB[:, 0:w], scalar1=scol,
                                    scalar2=None, op0=OP.mult))
        else:
            sdv.inc(v.tensor_copy(out_bf, SCRB[:, 0:w]))

    import os as _os9
    _p1dbg = bool(_os9.environ.get("P1DEBUG")) and not skip_p1
    _v1dbg = bool(_os9.environ.get("V1DEBUG")) and not skip_p1
    if _v1dbg:
        _p1dbg = True
    for chalf in range((2 if not _p1dbg else (0 if _v1dbg else 1)) if not skip_p1 else 0):
        for jp in range(4 if not _p1dbg else 1):
            sdv.wait(t)
            for dm in range(NCH):
                wb = WB1 if dm % 2 == 0 else WB2
                spe.wait(g)
                castload(wb[:, 0:1024],
                         Wkv_d[dm * 128:(dm + 1) * 128, chalf * 1024:(chalf + 1) * 1024])
                gdma.wait(t)
                nob = bool(_os9.environ.get("NOBIAS"))
                for jj in range(2):
                    jc = jp * 2 + jj
                    pk = psA[:, jj * 1024:(jj + 1) * 1024]
                    if dm == 0 and not nob:
                        spe.inc(t.matmul(pk[:, 0:512], onesbf,
                                         bkvrow[:, chalf * 1024:chalf * 1024 + 512],
                                         start=True, stop=False))
                        spe.inc(t.matmul(pk[:, 512:1024], onesbf,
                                         bkvrow[:, chalf * 1024 + 512:(chalf + 1) * 1024],
                                         start=True, stop=False))
                    for gc in range(2):
                        spe.inc(t.matmul(pk[:, gc * 512:(gc + 1) * 512],
                                         cndT[dm][:, jc * 128:(jc + 1) * 128],
                                         wb[:, gc * 512:(gc + 1) * 512],
                                         start=(dm == 0 and nob),
                                         stop=(dm == NCH - 1)))
            spe.wait(v)
            if _p1dbg:
                v.tensor_copy(DBGB, psA[:, 0:1024])
            for jj in range(2 if not _p1dbg else 1):
                jc = jp * 2 + jj
                pk = psA[:, jj * 1024:(jj + 1) * 1024]
                vseal(v.reduce_max(colA[:, 0:8], pk.rearrange("p (h q) -> p h q", q=128),
                                   axis=AX, apply_absolute_value=True))
                vseal(v.tensor_scalar(out=colA[:, 0:8], in0=colA[:, 0:8], scalar1=INV127,
                                      scalar2=1e-8, op0=OP.mult, op1=OP.max))
                vseal(v.reciprocal(colB[:, 0:8], colA[:, 0:8]))
                v.tensor_tensor(out=SCRA[:, 0:1024].rearrange("p (h q) -> p h q", q=128),
                                in0=pk.rearrange("p (h q) -> p h q", q=128),
                                in1=colB[:, 0:8].unsqueeze(2).broadcast_to(
                                    (128, 8, 128)),
                                op=OP.mult)
                v.tensor_copy(SCRI[:, 0:1024], SCRA[:, 0:1024])
                v.scalar_tensor_tensor(out=SCRB[:, 0:1024], in0=SCRA[:, 0:1024],
                                       scalar=-0.5, in1=SCRI[:, 0:1024],
                                       op0=OP.add, op1=OP.is_ge)
                v.tensor_add(SCRB[:, 0:1024], SCRI[:, 0:1024], SCRB[:, 0:1024])
                v.scalar_tensor_tensor(out=SCRD, in0=SCRA[:, 0:1024], scalar=0.5,
                                       in1=SCRI[:, 0:1024], op0=OP.add, op1=OP.is_le)
                v.tensor_sub(SCRB[:, 0:1024], SCRB[:, 0:1024], SCRD)
                spe.wait(v)
                sdv.inc(v.tensor_tensor(
                    out=sbfA[:, 0:1024].rearrange("p (h q) -> p h q", q=128),
                    in0=SCRB[:, 0:1024].rearrange("p (h q) -> p h q", q=128),
                    in1=colA[:, 0:8].unsqueeze(2).broadcast_to((128, 8, 128)),
                    op=OP.mult))
                sdv.wait(t)
                sdma.wait(v)
                for hh in range(8):
                    h = chalf * 8 + hh
                    slot = psT if hh % 2 == 0 else psU
                    sdv.wait(t, lag=1)
                    t.transpose(slot[:, 0:128], sbfA[:, hh * 128:(hh + 1) * 128], I128)
                    spe.inc(t.nop())
                    spe.wait(v)
                    sdv.inc(v.tensor_copy(sbfB[:, hh * 128:(hh + 1) * 128],
                                          slot[:, 0:128]))
                sdv.wait(sy)
                for hh in range(8):
                    h = chalf * 8 + hh
                    sdma.inc(sy.dma_start(
                        out=kscT_s[h * 128:(h + 1) * 128, jc * 128:(jc + 1) * 128],
                        in_=sbfB[:, hh * 128:(hh + 1) * 128]), 16)

    if not skip_p1 and (not _p1dbg or _v1dbg):
        load_cols(colC[:, 0:NCH], bkv_d, NCH, off=2048)
    for cc in range((NCH if not _p1dbg else (1 if _v1dbg else 0)) if not skip_p1 else 0):
        spe.wait(g)
        castload(WB1[:, 0:2048].rearrange("p (a b) -> p a b", b=128),
                 Wkv_d[:, 2048 + cc * 128:2048 + (cc + 1) * 128]
                 .rearrange("(a p) b -> p a b", p=128), slow=True)
        gdma.wait(t)
        spe.wait(v, lag=0)
        sdv.wait(t)
        for dm in range(NCH):
            for gc in range(2):
                spe.inc(t.matmul(psA[:, gc * 512:(gc + 1) * 512],
                                 WB1[:, dm * 128:(dm + 1) * 128],
                                 cndT[dm][:, gc * 512:(gc + 1) * 512],
                                 start=(dm == 0), stop=(dm == NCH - 1)))
        spe.wait(v)
        if _v1dbg:
            v.tensor_copy(DBGB, psA[:, 0:1024])
        vcol = colC[:, cc:cc + 1]
        vseal(v.tensor_scalar(out=SCRA[:, 0:1024], in0=psA[:, 0:1024], scalar1=vcol,
                              scalar2=None, op0=OP.add))
        vseal(v.reduce_max(colA[:, 0:1], SCRA[:, 0:1024], axis=AX,
                           apply_absolute_value=True))
        vseal(v.tensor_scalar(out=colA[:, 0:1], in0=colA[:, 0:1], scalar1=INV127,
                              scalar2=1e-8, op0=OP.mult, op1=OP.max))
        vseal(v.reciprocal(colB[:, 0:1], colA[:, 0:1]))
        spe.wait(v)
        fq_signed_perpart(SCRA[:, 0:1024], 1024, colA[:, 0:1], colB[:, 0:1],
                          sbfA[:, 0:1024], True)
        sdv.wait(t)
        sdma.wait(v)
        for jc in range(8):
            slot = psT if jc % 2 == 0 else psU
            sdv.wait(t, lag=1)
            t.transpose(slot[:, 0:128], sbfA[:, jc * 128:(jc + 1) * 128], I128)
            spe.inc(t.nop())
            spe.wait(v)
            sdv.inc(v.tensor_copy(sbfB[:, jc * 128:(jc + 1) * 128], slot[:, 0:128]))
        sdv.wait(sy)
        for jc in range(8):
            sdma.inc(sy.dma_start(
                out=vtok_s[jc * 128:(jc + 1) * 128, cc * 128:(cc + 1) * 128],
                in_=sbfB[:, jc * 128:(jc + 1) * 128]), 16)
    nc.all_engine_barrier()
    if upto == 1:
        _emit_dump(nc, REG, dump)
        _finalize(nc, REG)
        return nc

    def emit_layernorm(src_tiles):
        sdv.wait(s)
        spe.wait(s)
        for ic in range(NCH):
            sdv.inc(v.reduce_sum(colA[:, ic:ic + 1], src_tiles[ic], axis=AX))
            sdv.wait(s)
            sac.inc(s.activation(out=SCRA, in_=src_tiles[ic], func=AF.Square))
            sac.wait(v)
            sdv.inc(v.reduce_sum(colB[:, ic:ic + 1], SCRA, axis=AX))
        sdv.wait(v)
        vseal(v.tensor_scalar(out=ln_m, in0=colA, scalar1=1.0 / D, scalar2=None,
                              op0=OP.mult))
        vseal(v.tensor_scalar(out=colB, in0=colB, scalar1=1.0 / D, scalar2=None,
                              op0=OP.mult))
        vseal(v.tensor_mul(colA, ln_m, ln_m))
        vseal(v.tensor_sub(colB, colB, colA))
        sdv.wait(s)
        sseal(s.activation(out=colB, in_=colB, func=AF.Ln, bias=1e-5, scale=1.0))
        sseal(s.activation(out=ln_r, in_=colB, func=AF.Exp, scale=-0.5))
        sac.wait(v)
        vseal(v.tensor_mul(ln_b, ln_m, ln_r))
        vseal(v.tensor_scalar(out=ln_b, in0=ln_b, scalar1=-1.0, scalar2=None,
                              op0=OP.mult))
        sdv.wait(s)

    def emit_xnT_to_hbm(src_tiles):
        for ic in range(NCH):
            spe.wait(s)
            sac.inc(s.activation(out=sbfA[:, 0:D], in_=src_tiles[ic],
                                 func=AF.Identity, scale=ln_r[:, ic:ic + 1],
                                 bias=ln_b[:, ic:ic + 1]))
            sac.wait(t)
            sdma.wait(v)
            for dc in range(NCH):
                slot = psT if dc % 2 == 0 else psU
                sdv.wait(t, lag=1)
                t.transpose(slot[:, 0:128], sbfA[:, dc * 128:(dc + 1) * 128], I128)
                spe.inc(t.nop())
                spe.wait(v)
                sdv.inc(v.tensor_copy(sbfB[:, dc * 128:(dc + 1) * 128],
                                      slot[:, 0:128]))
            sdv.wait(sy)
            for dc in range(NCH):
                sdma.inc(sy.dma_start(
                    out=xnT_s[dc * 128:(dc + 1) * 128, ic * 128:(ic + 1) * 128],
                    in_=sbfB[:, dc * 128:(dc + 1) * 128]), 16)

    emit_layernorm(x_tm)
    import os as _os3
    if not _os3.environ.get("SKIP_XNT"):
        emit_xnT_to_hbm(x_tm)
    castload(ROWS[:, 0:D], row_ap(bq_d, D))
    gdma.wait(t)
    nc.all_engine_barrier()
    if upto == 2:
        _emit_dump(nc, REG, dump)
        _finalize(nc, REG)
        return nc

    _p2dbg = bool(_os9.environ.get("P2DEBUG"))
    bqrow = ROWS[:, 0:D]
    for dqb in range(8):
        spe.wait(g)
        castload(WB0[:, 0:4096].rearrange("p (a b) -> p a b", b=256),
                 Wq_d[:, dqb * 256:(dqb + 1) * 256]
                 .rearrange("(a p) b -> p a b", p=128), slow=True)
        gdma.wait(t)
        for ic in range(NCH):
            spe.wait(sy)
            load(WB1[:, 0:2048].rearrange("p (a b) -> p a b", b=128),
                 xnT_s[:, ic * 128:(ic + 1) * 128]
                 .rearrange("(a p) b -> p a b", p=128), eng=sy, slow=True)
            sdma.wait(t)
            sdv.wait(t)
            ps_q = psA[:, 0:256]
            spe.inc(t.matmul(ps_q[:, 0:128], onesbf,
                             bqrow[:, dqb * 256:dqb * 256 + 128],
                             start=True, stop=False))
            spe.inc(t.matmul(ps_q[:, 128:256], onesbf,
                             bqrow[:, dqb * 256 + 128:(dqb + 1) * 256],
                             start=False, stop=False))
            for dm in range(NCH):
                spe.inc(t.matmul(ps_q,
                                 WB1[:, dm * 128:(dm + 1) * 128],
                                 WB0[:, dm * 256:(dm + 1) * 256],
                                 start=False, stop=(dm == NCH - 1)))
            spe.wait(v)
            spe.wait(s)
            if _p2dbg and dqb == 0 and ic == 0:
                v.tensor_copy(DBGB[:, 0:256], ps_q)
            vseal(v.reduce_max(colA[:, 0:2],
                               ps_q.rearrange("p (h q) -> p h q", q=128),
                               axis=AX, apply_absolute_value=True))
            vseal(v.tensor_scalar(out=colA[:, 0:2], in0=colA[:, 0:2], scalar1=INV127,
                                  scalar2=1e-8, op0=OP.mult, op1=OP.max))
            vseal(v.reciprocal(colB[:, 0:2], colA[:, 0:2]))
            v.tensor_scalar(out=sq_all[:, ic * 16 + dqb * 2:ic * 16 + dqb * 2 + 2],
                            in0=colA[:, 0:2], scalar1=SCALEQ, scalar2=None,
                            op0=OP.mult)
            v.tensor_tensor(out=SCRA[:, 0:256].rearrange("p (h q) -> p h q", q=128),
                            in0=ps_q.rearrange("p (h q) -> p h q", q=128),
                            in1=colB[:, 0:2].unsqueeze(2)
                            .broadcast_to((128, 2, 128)), op=OP.mult)
            v.tensor_copy(SCRI[:, 0:256], SCRA[:, 0:256])
            v.scalar_tensor_tensor(out=SCRB[:, 0:256], in0=SCRA[:, 0:256],
                                   scalar=-0.5, in1=SCRI[:, 0:256],
                                   op0=OP.add, op1=OP.is_ge)
            v.tensor_add(SCRB[:, 0:256], SCRI[:, 0:256], SCRB[:, 0:256])
            v.scalar_tensor_tensor(out=SCRD[:, 0:256], in0=SCRA[:, 0:256],
                                   scalar=0.5, in1=SCRI[:, 0:256],
                                   op0=OP.add, op1=OP.is_le)
            v.tensor_sub(SCRB[:, 0:256], SCRB[:, 0:256], SCRD[:, 0:256])
            spe.wait(v)
            sdv.inc(v.tensor_copy(sbfA[:, 0:256], SCRB[:, 0:256]))
            sdv.wait(t)
            sdma.wait(v)
            for hh in range(2):
                slot = psT if hh % 2 == 0 else psU
                sdv.wait(t, lag=1)
                t.transpose(slot[:, 0:128],
                            sbfA[:, hh * 128:(hh + 1) * 128], I128)
                spe.inc(t.nop())
                spe.wait(v)
                sdv.inc(v.tensor_copy(sbfB[:, hh * 128:(hh + 1) * 128],
                                      slot[:, 0:128]))
            sdv.wait(sy)
            for hh in range(2):
                dq0 = dqb * 256 + hh * 128
                sdma.inc(sy.dma_start(
                    out=cqT_s[dq0:dq0 + 128, ic * 128:(ic + 1) * 128],
                    in_=sbfB[:, hh * 128:(hh + 1) * 128]), 16)
    spe.wait(v)
    sdv.inc(v.tensor_scalar(out=nsq_all, in0=sq_all, scalar1=-1.0, scalar2=None,
                            op0=OP.mult))
    nc.all_engine_barrier()
    if upto == 3:
        _emit_dump(nc, REG, dump)
        _finalize(nc, REG)
        return nc

    for h in range(NH):
        sdv.wait(sy)
        spe.wait(sy)
        load(WB0[:, 0:2048], cqT_s[h * 128:(h + 1) * 128, :])
        load(WB1[:, 0:1024], kscT_s[h * 128:(h + 1) * 128, :])
        load(WB2[:, 0:1024].rearrange("p (a b) -> p a b", b=128),
             vtok_s[:, h * 128:(h + 1) * 128].rearrange("(a p) b -> p a b", p=128),
             slow=True)
        sdma.wait(t)
        sdma.wait(s)
        sdma.wait(v)
        for ic in range(NCH):
            par = ic % 2
            psS = psA[:, par * 1024:(par + 1) * 1024]
            sac.wait(t, lag=1)
            spe.inc(t.matmul(psS[:, 0:512], WB0[:, ic * 128:(ic + 1) * 128],
                             WB1[:, 0:512], start=True, stop=True))
            spe.inc(t.matmul(psS[:, 512:1024], WB0[:, ic * 128:(ic + 1) * 128],
                             WB1[:, 512:1024], start=True, stop=True))
            spe.wait(v)
            sqc = sq_all[:, ic * 16 + h:ic * 16 + h + 1]
            nsqc = nsq_all[:, ic * 16 + h:ic * 16 + h + 1]
            vseal(v.reduce_max(colA[:, 0:1], psS, axis=AX))
            vseal(v.tensor_scalar(out=colB[:, 0:1], in0=colA[:, 0:1], scalar1=nsqc,
                                  scalar2=LN255, op0=OP.mult, op1=OP.add))
            sdv.wait(s)
            E = SCRA[:, par * 1024:(par + 1) * 1024]
            sac.inc(s.activation(out=E, in_=psS, func=AF.Exp, scale=sqc,
                                 bias=colB[:, 0:1]))
            sac.wait(v)
            Ri = SCRI[:, par * 1024:(par + 1) * 1024]
            Cg = SCRB[:, par * 1024:(par + 1) * 1024]
            vseal(v.reduce_sum(colC[:, 0:1], E, axis=AX))
            vseal(v.reciprocal(colC[:, 1:2], colC[:, 0:1]))
            v.tensor_copy(Ri, E)
            v.scalar_tensor_tensor(out=Cg, in0=E, scalar=-0.5, in1=Ri,
                                   op0=OP.add, op1=OP.is_ge)
            v.tensor_add(E, Ri, Cg)
            ca = sbfA if par == 0 else sbfB
            sdma.wait(v, lag=80)
            adma.wait(v, lag=64)
            sdv.inc(v.tensor_scalar(out=ca[:, 0:1024], in0=E, scalar1=colC[:, 1:2],
                                    scalar2=None, op0=OP.mult))
            sdv.wait(sy)
            sdv.wait(s)
            caT = ca[:, 1024:2048]
            for jc in range(8):
                src = ca[:, jc * 128:(jc + 1) * 128]
                dst = caT[:, jc * 128:(jc + 1) * 128]
                if jc % 2 == 0:
                    sdma.inc(sy.dma_start_transpose(dst, src), 16)
                else:
                    adma.inc(s.dma_start_transpose(dst, src), 16)
            sdma.wait(t)
            adma.wait(t)
            sdv.wait(t, lag=5)
            psO = psB[:, par * 512:par * 512 + 128]
            for jc in range(8):
                spe.inc(t.matmul(psO, WB2[:, jc * 128:(jc + 1) * 128],
                                 caT[:, jc * 128:(jc + 1) * 128],
                                 start=(jc == 0), stop=(jc == 7)))
            spe.wait(v)
            sdv.inc(v.tensor_copy(sbfA[:, 1024:1152] if par == 0 else
                                  sbfB[:, 1024:1152], psO))
            sdv.wait(sy)
            sdma.inc(sy.dma_start(
                out=otT_s[h * 128:(h + 1) * 128, ic * 128:(ic + 1) * 128],
                in_=(sbfA if par == 0 else sbfB)[:, 1024:1152]), 16)
    castload(ROWS[:, 0:D], row_ap(bp_d, D))
    gdma.wait(t)
    nc.all_engine_barrier()
    if upto == 4:
        _emit_dump(nc, REG, dump)
        _finalize(nc, REG)
        return nc

    bprow = ROWS[:, 0:D]
    for ig in range(8):
        sdv.wait(sy)
        spe.wait(sy)
        for icb in range(2):
            ic = ig * 2 + icb
            load(WB0[:, icb * 2048:icb * 2048 + 2048]
                 .rearrange("p (a b) -> p a b", b=128),
                 otT_s[:, ic * 128:(ic + 1) * 128]
                 .rearrange("(a p) b -> p a b", p=128), eng=sy, slow=True)
        sdma.wait(t)
        for dpb in range(4):
            sdv.wait(t)
            spe.wait(g)
            castload(WB1[:, 0:512], Wp_d[0:128, dpb * 512:(dpb + 1) * 512])
            for dc in range(NCH):
                wb = WB1 if dc % 2 == 0 else WB2
                if dc + 1 < NCH:
                    other = WB2 if dc % 2 == 0 else WB1
                    spe.wait(g)
                    castload(other[:, 0:512],
                             Wp_d[(dc + 1) * 128:(dc + 2) * 128,
                                  dpb * 512:(dpb + 1) * 512])
                gdma.wait(t, lag=16 if dc + 1 < NCH else 0)
                for icb in range(2):
                    psP = psA[:, icb * 512:(icb + 1) * 512]
                    if dc == 0:
                        spe.inc(t.matmul(psP[:, 0:128], onesbf,
                                         bprow[:, dpb * 512:dpb * 512 + 128],
                                         start=True, stop=False))
                        spe.inc(t.matmul(psP[:, 128:512], onesbf,
                                         bprow[:, dpb * 512 + 128:(dpb + 1) * 512],
                                         start=False, stop=False))
                    spe.inc(t.matmul(psP,
                                     WB0[:, icb * 2048 + dc * 128:icb * 2048 + (dc + 1) * 128],
                                     wb[:, 0:512], start=False, stop=(dc == NCH - 1)))
            spe.wait(v)
            for icb in range(2):
                ic = ig * 2 + icb
                psP = psA[:, icb * 512:(icb + 1) * 512]
                xs = x_tm[ic][:, dpb * 512:(dpb + 1) * 512]
                sdv.inc(v.scalar_tensor_tensor(out=xs, in0=psP, scalar=1.0, in1=xs,
                                               op0=OP.mult, op1=OP.add))
    nc.all_engine_barrier()
    if upto == 5:
        _emit_dump(nc, REG, dump)
        _finalize(nc, REG)
        return nc

    xsl = [ARN[:, i * 512:(i + 1) * 512] for i in range(NCH)]
    v2 = [ARN[:, 8192 + i * 512:8192 + (i + 1) * 512] for i in range(DCH)]
    w2c = [ARN[:, 24576 + i * 512:24576 + (i + 1) * 512] for i in range(NCH)]
    for ft in range(4):
        sdv.wait(sy)
        spe.wait(sy)
        for dm in range(NCH):
            load(xsl[dm], xnT_s[dm * 128:(dm + 1) * 128, ft * 512:(ft + 1) * 512])
        sdma.wait(t)
        for half in range(2):
            sdv.wait(t)
            def _ldw13(hc2, a2):
                castload(WB0[:, a2 * 2048:(a2 + 1) * 2048]
                         .rearrange("p (a b) -> p a b", b=128),
                         W1T_d[:, hc2 * 128:(hc2 + 1) * 128]
                         .rearrange("(a p) b -> p a b", p=128), slow=True)
                castload((WB1 if a2 == 0 else WB2)[:, 0:2048]
                         .rearrange("p (a b) -> p a b", b=128),
                         W3T_d[:, hc2 * 128:(hc2 + 1) * 128]
                         .rearrange("(a p) b -> p a b", p=128), slow=True)
            spe.wait(g)
            sdv.wait(g)
            sac.wait(g)
            _ldw13(half * DCH, 0)
            for hcl in range(DCH):
                hc = half * DCH + hcl
                a = hcl % 2
                w1b = WB0[:, a * 2048:(a + 1) * 2048]
                w3b = WB1 if a == 0 else WB2
                if hcl + 1 < DCH:
                    spe.wait(g)
                    _ldw13(hc + 1, 1 - a)
                gdma.wait(t, lag=32 if hcl + 1 < DCH else 0)
                sac.wait(t)
                sdv.wait(t)
                for dm in range(NCH):
                    spe.inc(t.matmul(psA[:, 1024:1536], w1b[:, dm * 128:(dm + 1) * 128],
                                     xsl[dm], start=(dm == 0), stop=(dm == NCH - 1)))
                for dm in range(NCH):
                    spe.inc(t.matmul(psA[:, 1536:2048], w3b[:, dm * 128:(dm + 1) * 128],
                                     xsl[dm], start=(dm == 0), stop=(dm == NCH - 1)))
                spe.wait(s)
                sac.inc(s.activation(out=sbfA[:, 0:512], in_=psA[:, 1024:1536],
                                     func=AF.Silu))
                sac.wait(v)
                sdv.inc(v.tensor_mul(sbfB[:, 0:512], sbfA[:, 0:512],
                                     psA[:, 1536:2048]))
                sdv.wait(t)
                spe.inc(t.matmul(psB[:, 0:512], H128, sbfB[:, 0:512],
                                 start=True, stop=True))
                spe.wait(v)
                sdv.inc(v.tensor_copy(v2[hcl], psB[:, 0:512]))
                sdv.wait(t)
            for dpb in range(4):
                for sub in range(2):
                    sdv.wait(sy)
                    spe.wait(sy)
                    for hcl in range(NCH):
                        hh = half * DCH + sub * NCH + hcl
                        load(w2c[hcl],
                             w2b_s[hh * 128:(hh + 1) * 128, dpb * 512:(dpb + 1) * 512])
                    sdma.wait(t)
                    for icq in range(4):
                        ic = ft * 4 + icq
                        psF = psA[:, icq * 512:(icq + 1) * 512]
                        if sub == 0 and icq == 0:
                            sdv.wait(t)
                        for hcl in range(NCH):
                            spe.inc(t.matmul(
                                psF, v2[sub * NCH + hcl][:, icq * 128:(icq + 1) * 128],
                                w2c[hcl], start=(hcl == 0 and sub == 0),
                                stop=(hcl == NCH - 1 and sub == 1)))
                        if sub == 1:
                            spe.wait(v)
                            xs = x_tm[ic][:, dpb * 512:(dpb + 1) * 512]
                            sdv.inc(v.scalar_tensor_tensor(
                                out=xs, in0=psF, scalar=1.0, in1=xs,
                                op0=OP.mult, op1=OP.add))
    nc.all_engine_barrier()
    if upto == 6:
        _emit_dump(nc, REG, dump)
        _finalize(nc, REG)
        return nc

    emit_layernorm(x_tm)
    castload(ROWS[:, 0:D], row_ap(bias1_d, D))
    gdma.wait(t)
    spe.wait(v)
    b1bc = ARN[:, 28672:32768].bitcast(F32)
    sdv.wait(t)
    for gc in range(4):
        spe.inc(t.matmul(psA[:, gc * 512:(gc + 1) * 512], onesbf,
                         ROWS[:, gc * 512:(gc + 1) * 512], start=True, stop=True))
    spe.wait(v)
    sdv.inc(v.tensor_copy(b1bc, psA))
    nc.all_engine_barrier()
    if upto == 7:
        _emit_dump(nc, REG, dump)
        _finalize(nc, REG)
        return nc

    hsT = [ARN[:, i * 256:(i + 1) * 256] for i in range(DCH)]
    gtT = [ARN[:, 8192 + i * 256:8192 + (i + 1) * 256] for i in range(DCH)]
    yT = [ARN[:, 16384 + i * 256:16384 + (i + 1) * 256] for i in range(DCH)]
    xqT = [ARN[:, 24576 + i * 256:24576 + (i + 1) * 256] for i in range(NCH)]
    xn1T = xqT
    sdv.inc(v.memset(CARR, 0.0))
    hs_stash = sbt("hs_stash", [128, DCH * 4], BF16)
    sdv.inc(v.memset(hs_stash, 0.0))
    sdv.wait(t); sdv.wait(s); sdv.wait(g)

    for mt in range(NMT):
        ics = [mt * (MT // 128) + k for k in range(MT // 128)]
        for k, ic in enumerate(ics):
            spe.wait(s)
            sdv.wait(s)
            sac.inc(s.activation(out=sbfA[:, 0:D], in_=x_tm[ic], func=AF.Identity,
                                 scale=ln_r[:, ic:ic + 1], bias=ln_b[:, ic:ic + 1]))
            sac.wait(t)
            for dm in range(NCH):
                slot = psT if dm % 2 == 0 else psU
                sdv.wait(t, lag=1)
                t.transpose(slot[:, 0:128], sbfA[:, dm * 128:(dm + 1) * 128], I128)
                spe.inc(t.nop())
                spe.wait(v)
                sdv.inc(v.tensor_copy(xn1T[dm][:, k * 128:(k + 1) * 128],
                                      slot[:, 0:128]))
            sdv.wait(t)
        spe.wait(sy)
        load(WB1[:, 0:2048].rearrange("p (a b) -> p a b", b=128),
             winb_s[:, 0:128].rearrange("(a p) b -> p a b", p=128),
             eng=sy, slow=True)
        for ec in range(2 * DCH):
            wb = WB1 if ec % 2 == 0 else WB2
            if ec + 1 < 2 * DCH:
                other = WB2 if ec % 2 == 0 else WB1
                spe.wait(sy)
                load(other[:, 0:2048].rearrange("p (a b) -> p a b", b=128),
                     winb_s[:, (ec + 1) * 128:(ec + 2) * 128]
                     .rearrange("(a p) b -> p a b", p=128), eng=sy, slow=True)
            sdma.wait(t, lag=16 if ec + 1 < 2 * DCH else 0)
            par = ec % 2
            psW = psA[:, 1024 + par * 512:1024 + par * 512 + MT]
            sdv.wait(t)
            sac.wait(t)
            for dm in range(NCH):
                spe.inc(t.matmul(psW, wb[:, dm * 128:(dm + 1) * 128],
                                 xn1T[dm][:, 0:MT], start=(dm == 0),
                                 stop=(dm == NCH - 1)))
            spe.wait(v)
            if ec < DCH:
                dc = ec
                st = hs_stash[:, dc * 4:dc * 4 + 4]
                v.tensor_copy(SCRB[:, 0:3], st[:, 1:4])
                vseal(v.tensor_copy(SCRB[:, 3:MT + 3], psW))
                vseal(v.tensor_copy(st[:, 1:4], SCRB[:, MT:MT + 3]))
                v.tensor_scalar(out=SCRA[:, 0:MT], in0=SCRB[:, 3:MT + 3],
                                scalar1=ccols[:, 96 + dc:96 + dc + 1],
                                scalar2=None, op0=OP.mult)
                ii_ = None
                for kk in range(3):
                    ii_ = v.scalar_tensor_tensor(
                        out=SCRA[:, 0:MT], in0=SCRB[:, kk:MT + kk],
                        scalar=ccols[:, kk * 32 + dc:kk * 32 + dc + 1],
                        in1=SCRA[:, 0:MT], op0=OP.mult, op1=OP.add)
                sdv.inc(ii_)
                sdv.wait(s)
                sac.inc(s.activation(out=hsT[dc], in_=SCRA[:, 0:MT], func=AF.Silu,
                                     bias=bcv_cols[:, dc:dc + 1], scale=1.0))
            else:
                dc = ec - DCH
                sdv.inc(v.tensor_copy(gtT[dc], psW))
        sac.wait(t)
        sdv.wait(t)
        sac.wait(v)
        spe.wait(g)
        castload(WB1[:, 0:160], WxT_d[0:128, :])
        for dc in range(DCH):
            wb = WB1 if dc % 2 == 0 else WB2
            if dc + 1 < DCH:
                other = WB2 if dc % 2 == 0 else WB1
                spe.wait(g)
                castload(other[:, 0:160], WxT_d[(dc + 1) * 128:(dc + 2) * 128, :])
            gdma.wait(t, lag=16 if dc + 1 < DCH else 0)
            spe.inc(t.matmul(psB[:, 0:MT], wb[:, 0:128], hsT[dc],
                             start=(dc == 0), stop=(dc == DCH - 1)))
            spe.inc(t.matmul(psB[0:32, 512:512 + MT], wb[:, 128:160], hsT[dc],
                             start=(dc == 0), stop=(dc == DCH - 1)))
        spe.wait(v)
        sdv.inc(v.tensor_copy(sbfB[:, 0:MT], psB[:, 0:MT]))
        sdv.inc(v.tensor_copy(BCR, psB[0:32, 512:512 + MT]))
        sdv.wait(g)
        gdma.inc(g.dma_start(out=ROWS[:, 0:16 * MT], in_=BCR[0:16, :]), 16)
        gdma.inc(g.dma_start(out=ROWSC[:, 0:16 * MT], in_=BCR[16:32, :]), 16)
        gdma.wait(t)
        sdv.wait(t)
        spe.wait(g)
        castload(WB1[:, 0:128], WdtT_d[:, 0:128], slow=True)
        for dc in range(DCH):
            wb = WB1 if dc % 2 == 0 else WB2
            if dc + 1 < DCH:
                other = WB2 if dc % 2 == 0 else WB1
                spe.wait(g)
                castload(other[:, 0:128], WdtT_d[:, (dc + 1) * 128:(dc + 2) * 128],
                         slow=True)
            gdma.wait(t, lag=16 if dc + 1 < DCH else 0)
            sac.wait(t)
            sdv.wait(t)
            spe.inc(t.matmul(psB[:, 512:512 + MT], wb[:, 0:128], sbfB[:, 0:MT],
                             start=True, stop=True))
            spe.wait(s)
            spe.wait(v)
            sac.wait(v)
            sseal(s.activation(out=SCRB[:, 256:256 + MT], in_=psB[:, 512:512 + MT],
                               func=AF.Abs, bias=bdt_cols[:, dc:dc + 1], scale=1.0))
            sseal(s.activation(out=SCRB[:, 256:256 + MT], in_=SCRB[:, 256:256 + MT],
                               func=AF.Exp, scale=-1.0))
            sseal(s.activation(out=SCRB[:, 256:256 + MT], in_=SCRB[:, 256:256 + MT],
                               func=AF.Ln, bias=1.0, scale=1.0))
            sac.wait(v)
            v.tensor_scalar(out=SCRA[:, 0:MT], in0=psB[:, 512:512 + MT],
                            scalar1=bdt_cols[:, dc:dc + 1], scalar2=0.0,
                            op0=OP.add, op1=OP.max)
            vseal(v.tensor_add(SCRA[:, 0:MT], SCRA[:, 0:MT], SCRB[:, 256:256 + MT]))
            sdv.wait(g)
            sdv.inc(v.tensor_mul(sbfA[:, 0:MT], SCRA[:, 0:MT], hsT[dc]))
            sdv.wait(s)
            sdv.wait(g)
            for ng in range(4):
                bsl = psA[:, 0:1024]
                csl = psA[:, 1024:2048]
                if ng >= 1:
                    sdv.wait(t)
                for j in range(2):
                    spe.inc(t.matmul(bsl[:, j * 512:(j + 1) * 512], onesbf,
                                     ROWS[:, ng * 4 * MT + j * 512:
                                          ng * 4 * MT + (j + 1) * 512],
                                     start=True, stop=True))
                for j in range(2):
                    spe.inc(t.matmul(csl[:, j * 512:(j + 1) * 512], onesbf,
                                     ROWSC[:, ng * 4 * MT + j * 512:
                                           ng * 4 * MT + (j + 1) * 512],
                                     start=True, stop=True))
                if ng >= 1:
                    sdv.wait(s, lag=2)
                for jn in range(4):
                    n = ng * 4 + jn
                    sac.inc(s.activation(out=SCRB[:, jn * MT:(jn + 1) * MT],
                                         in_=SCRA[:, 0:MT], func=AF.Exp,
                                         scale=acols[:, dc * 16 + n:dc * 16 + n + 1]))
                spe.wait(v)
                sdv.inc(v.tensor_tensor(
                    out=SCRD.rearrange("p (a b) -> p a b", b=MT),
                    in0=bsl.rearrange("p (a b) -> p a b", b=MT),
                    in1=sbfA[:, 0:MT].unsqueeze(1).broadcast_to((128, 4, MT)),
                    op=OP.mult))
                sac.wait(v)
                last = None
                for jn in range(4):
                    n = ng * 4 + jn
                    cini = CARR[:, dc * 16 + n:dc * 16 + n + 1]
                    last = v.tensor_tensor_scan(
                        out=SCRB[:, 1024 + jn * MT:1024 + (jn + 1) * MT],
                        data0=SCRB[:, jn * MT:(jn + 1) * MT],
                        data1=SCRD[:, jn * MT:(jn + 1) * MT], initial=cini,
                        op0=OP.mult, op1=OP.add)
                sdv.inc(last)
                sdv.wait(v)
                ccp = None
                for jn in range(4):
                    n = ng * 4 + jn
                    ccp = v.tensor_copy(CARR[:, dc * 16 + n:dc * 16 + n + 1],
                                        SCRB[:, 1024 + (jn + 1) * MT - 1:
                                             1024 + (jn + 1) * MT])
                sdv.inc(ccp)
                spe.wait(v, lag=1)
                sdv.inc(v.tensor_tensor(
                    out=sbfB[:, 256:256 + 4 * MT].rearrange("p (a b) -> p a b", b=MT),
                    in0=SCRB[:, 1024:1024 + 4 * MT].rearrange("p (a b) -> p a b", b=MT),
                    in1=csl.rearrange("p (a b) -> p a b", b=MT),
                    op=OP.mult))
                sdv.wait(t)
                for jn in range(4):
                    n = ng * 4 + jn
                    spe.inc(t.matmul(psB[:, 0:MT], I128,
                                     sbfB[:, 256 + jn * MT:256 + (jn + 1) * MT],
                                     start=(n == 0), stop=(n == NST - 1)))
            spe.wait(s)
            sac.inc(s.activation(out=sbfB[:, 512:512 + MT], in_=gtT[dc],
                                 func=AF.Silu))
            sac.wait(v)
            spe.wait(v)
            v.scalar_tensor_tensor(out=SCRB[:, 0:MT], in0=hsT[dc],
                                   scalar=dv_cols[:, dc:dc + 1], in1=psB[:, 0:MT],
                                   op0=OP.mult, op1=OP.add)
            sdv.inc(v.tensor_mul(yT[dc], SCRB[:, 0:MT], sbfB[:, 512:512 + MT]))
            sdv.wait(t)
        for k, ic in enumerate(ics):
            spe.wait(s)
            sdv.wait(s)
            sac.inc(s.activation(out=sbfA[:, 0:D], in_=x_tm[ic], func=AF.Identity,
                                 scale=ln_r[:, ic:ic + 1], bias=ln_b[:, ic:ic + 1]))
            sseal(s.activation(out=SCRB, in_=sbfA[:, 0:D], func=AF.Square))
            sac.wait(v)
            vseal(v.reduce_sum(colA[:, 0:1], SCRB, axis=AX))
            vseal(v.tensor_scalar(out=colA[:, 0:1], in0=colA[:, 0:1],
                                  scalar1=1.0 / D, scalar2=1e-8,
                                  op0=OP.mult, op1=OP.add))
            sdv.wait(s)
            sseal(s.activation(out=colA[:, 1:2], in_=colA[:, 0:1], func=AF.Ln))
            sseal(s.activation(out=colA[:, 2:3], in_=colA[:, 1:2], func=AF.Exp,
                               scale=0.5))
            sac.wait(v)
            vseal(v.reduce_max(colA[:, 3:4], sbfA[:, 0:D], axis=AX,
                               apply_absolute_value=True))
            vseal(v.reciprocal(colA[:, 4:5], colA[:, 2:3]))
            vseal(v.tensor_mul(colA[:, 5:6], colA[:, 3:4], colA[:, 4:5]))
            vseal(v.tensor_scalar(out=colA[:, 5:6], in0=colA[:, 5:6], scalar1=INV127,
                                  scalar2=1e-8, op0=OP.mult, op1=OP.add))
            vseal(v.tensor_mul(colA[:, 6:7], colA[:, 2:3], colA[:, 5:6]))
            vseal(v.reciprocal(colA[:, 7:8], colA[:, 6:7]))
            vseal(v.tensor_mul(colA[:, 8:9], colA[:, 3:4], colA[:, 4:5]))
            vseal(v.tensor_scalar(out=colA[:, 8:9], in0=colA[:, 8:9], scalar1=INV127,
                                  scalar2=None, op0=OP.mult))
            for hb in range(2):
                sl = slice(hb * 1024, (hb + 1) * 1024)
                v.tensor_scalar(out=SCRB[:, sl], in0=sbfA[:, sl],
                                scalar1=colA[:, 7:8], scalar2=None, op0=OP.mult)
                v.tensor_copy(SCRI[:, sl], SCRB[:, sl])
                v.scalar_tensor_tensor(out=SCRD, in0=SCRB[:, sl], scalar=-0.5,
                                       in1=SCRI[:, sl], op0=OP.add, op1=OP.is_ge)
                v.tensor_add(SCRD, SCRI[:, sl], SCRD)
                v.scalar_tensor_tensor(out=SCRB[:, sl], in0=SCRB[:, sl], scalar=0.5,
                                       in1=SCRI[:, sl], op0=OP.add, op1=OP.is_le)
                v.tensor_sub(SCRD, SCRD, SCRB[:, sl])
                spe.wait(v)
                sdv.inc(v.tensor_copy(sbfB[:, sl], SCRD))
            sdv.wait(t)
            for dm in range(NCH):
                slot = psT if dm % 2 == 0 else psU
                sdv.wait(t, lag=1)
                t.transpose(slot[:, 0:128], sbfB[:, dm * 128:(dm + 1) * 128], I128)
                spe.inc(t.nop())
                spe.wait(v)
                sdv.inc(v.tensor_copy(xqT[dm][:, k * 128:(k + 1) * 128],
                                      slot[:, 0:128]))
            sdv.wait(t)
            spe.wait(sy)
            load(WB1[:, 0:2048], wtT_s[0:128, :], eng=sy)
            sdv.wait(t)
            for kc in range(NCH):
                wb = WB1 if kc % 2 == 0 else WB2
                if kc + 1 < NCH:
                    other = WB2 if kc % 2 == 0 else WB1
                    spe.wait(sy)
                    load(other[:, 0:2048],
                         wtT_s[(kc + 1) * 128:(kc + 2) * 128, :], eng=sy)
                sdma.wait(t, lag=16 if kc + 1 < NCH else 0)
                for gc in range(4):
                    spe.inc(t.matmul(psA[:, gc * 512:(gc + 1) * 512],
                                     xqT[kc][:, k * 128:(k + 1) * 128],
                                     wb[:, gc * 512:(gc + 1) * 512],
                                     start=(kc == 0), stop=(kc == NCH - 1)))
            spe.wait(v)
            spe.wait(s)
            if DBGB is not None and mt == 0 and k == 0:
                v.tensor_copy(DBGB[:, 0:512], psA[:, 1536:2048])
            sac.inc(s.activation(out=sbfA[:, 0:D], in_=psA, func=AF.Identity,
                                 scale=colA[:, 8:9]))
            sac.wait(t)
            sac.wait(v)
            spe.wait(sy)
            load(WB1[:, 0:2048], woutb_s[0:128, :], eng=sy)
            for dc in range(DCH):
                wb = WB1 if dc % 2 == 0 else WB2
                if dc + 1 < DCH:
                    other = WB2 if dc % 2 == 0 else WB1
                    spe.wait(sy)
                    load(other[:, 0:2048],
                         woutb_s[(dc + 1) * 128:(dc + 2) * 128, :], eng=sy)
                sdma.wait(t, lag=16 if dc + 1 < DCH else 0)
                for gc in range(4):
                    spe.inc(t.matmul(psA[:, gc * 512:(gc + 1) * 512],
                                     yT[dc][:, k * 128:(k + 1) * 128],
                                     wb[:, gc * 512:(gc + 1) * 512],
                                     start=(dc == 0), stop=(dc == DCH - 1)))
            spe.wait(v)
            if DBGB is not None and mt == 0 and k == 0:
                v.tensor_copy(DBGB[:, 512:1024], psA[:, 1536:2048])
            sout.wait(v)
            v.scalar_tensor_tensor(out=SCRA, in0=psA, scalar=1.0, in1=b1bc,
                                   op0=OP.mult, op1=OP.add)
            v.tensor_add(SCRA, SCRA, x_tm[ic])
            sdv.inc(v.tensor_add(SCRA, SCRA, sbfA[:, 0:D]))
            sdv.wait(sy)
            sout.inc(sy.dma_start(out=out_d[ic * 128:(ic + 1) * 128, :], in_=SCRA), 16)
    nc.all_engine_barrier()
    sout.wait(sy)
    _emit_dump(nc, REG, dump)
    return nc


def _emit_dump(nc, R, dump):
    if not dump:
        return
    nc.all_engine_barrier()
    sy = R["sy"]
    sdma = R["sdma"]
    for name in dump:
        obj = R[name]
        if isinstance(obj, list):
            w = obj[0].shape[1]
            dt = obj[0].dtype
            dd = nc.dram_tensor(f"dbg_{name}", [len(obj), 128, w], dt,
                                kind="ExternalOutput").ap()
            for i, ap in enumerate(obj):
                sdma.inc(sy.dma_start(out=dd[i], in_=ap), 16)
        else:
            sh = list(obj.shape)
            dd = nc.dram_tensor(f"dbg_{name}", sh, obj.dtype,
                                kind="ExternalOutput").ap()
            sdma.inc(sy.dma_start(out=dd, in_=obj), 16)
    sdma.wait(sy)


def _finalize(nc, R):
    nc.all_engine_barrier()
    sy, v = R["sy"], R["v"]
    sdv, sout = R["sdv"], R["sout"]
    out_d, x_tm = R["out_d"], R["x_tm"]
    scro = R["SCRA"]
    for ic in range(NCH):
        sout.wait(v)
        sdv.inc(v.tensor_copy(scro, x_tm[ic]))
        sdv.wait(sy)
        sout.inc(sy.dma_start(out=out_d[ic * 128:(ic + 1) * 128, :], in_=scro), 16)
    sout.wait(sy)


def _host_layout(inputs):
    f32 = np.float32
    x = np.ascontiguousarray(np.asarray(inputs["L0_attn_x"], dtype=f32))
    cond = np.asarray(inputs["L0_attn_cond"], dtype=f32).reshape(KV, D)
    condT = np.ascontiguousarray(cond.T)
    W2 = np.asarray(inputs["L0_ffn_W2"], dtype=f32)
    W2TR = np.ascontiguousarray(
        W2.T.reshape(64, 128, D).transpose(1, 0, 2).reshape(FH, D))
    H128 = _had(128).astype(ml_dtypes.bfloat16)
    H64 = _had(64) * (2.0 ** -7)
    H64B = np.zeros((128, 128), np.float64)
    H64B[:64, :64] = H64
    H64B[64:, 64:] = H64
    common = {
        "condT": condT,
        "Wq": np.ascontiguousarray(np.asarray(inputs["L0_attn_Wq"], dtype=f32)),
        "bq": np.ascontiguousarray(np.asarray(inputs["L0_attn_bq"], dtype=f32)),
        "Wkv": np.ascontiguousarray(np.asarray(inputs["L0_attn_Wkv"], dtype=f32)),
        "bkv": np.ascontiguousarray(np.asarray(inputs["L0_attn_bkv"], dtype=f32)),
        "Wp": np.ascontiguousarray(np.asarray(inputs["L0_attn_Wp"], dtype=f32)),
        "bp": np.ascontiguousarray(np.asarray(inputs["L0_attn_bp"], dtype=f32)),
        "W1T": np.ascontiguousarray(np.asarray(inputs["L0_ffn_W1"], dtype=f32).T),
        "W3T": np.ascontiguousarray(np.asarray(inputs["L0_ffn_W3"], dtype=f32).T),
        "W2TR": W2TR,
        "WinT": np.ascontiguousarray(np.asarray(inputs["L1_attn_Win"], dtype=f32).T),
        "Wconv": np.ascontiguousarray(
            np.asarray(inputs["L1_attn_Wconv"], dtype=f32).reshape(DI, 4)),
        "bconv": np.ascontiguousarray(np.asarray(inputs["L1_attn_bconv"], dtype=f32)),
        "WxT": np.ascontiguousarray(np.asarray(inputs["L1_attn_Wx"], dtype=f32).T),
        "WdtT": np.ascontiguousarray(np.asarray(inputs["L1_attn_Wdt"], dtype=f32).T),
        "bdt": np.ascontiguousarray(np.asarray(inputs["L1_attn_bdt"], dtype=f32)),
        "WoutT": np.ascontiguousarray(np.asarray(inputs["L1_attn_Wout"], dtype=f32).T),
        "Alog": np.ascontiguousarray(np.asarray(inputs["L1_attn_A_log"], dtype=f32)),
        "Dvec": np.ascontiguousarray(np.asarray(inputs["L1_attn_D"], dtype=f32)),
        "wgt": np.ascontiguousarray(np.asarray(inputs["L1_ffn_weight"], dtype=f32)),
        "bias1": np.ascontiguousarray(np.asarray(inputs["L1_ffn_bias"], dtype=f32)),
        "onesbf": np.ones((1, 128), ml_dtypes.bfloat16),
        "onesf": np.ones((1, 128), f32),
        "I128": np.eye(128).astype(ml_dtypes.bfloat16),
        "H128": H128,
        "H64B": H64B.astype(ml_dtypes.bfloat16),
    }
    return [dict(common, x=np.ascontiguousarray(x[i])) for i in range(8)]


def kernel(**inputs):
    in_maps = _host_layout(inputs)
    nc = build(UPTO)
    res = run_bass_kernel_spmd(nc, in_maps, core_ids=list(range(8)))
    return np.stack([np.asarray(r["out"], dtype=np.float32) for r in res.results])


# version 06 (final)
# speedup vs baseline: 1.5401x; 1.0118x over previous
import numpy as np
import ml_dtypes
import concourse.bass as bass
import concourse.mybir as mybir
from concourse.bass_utils import run_bass_kernel_spmd

F32 = mybir.dt.float32
BF16 = mybir.dt.bfloat16
I32 = mybir.dt.int32
OP = mybir.AluOpType
AF = mybir.ActivationFunctionType
AX = mybir.AxisListType.X

T = 2048
D = 2048
NCH = 16
NH = 16
HD = 128
KV = 1024
FH = 8192
DI = 4096
DCH = 32
NST = 16
MT = 256
NMT = T // MT
FT = 512
SQ8192 = float(np.float32(1.0 / np.sqrt(np.float64(8192.0)) / (2.0 ** -7)))
LN255 = float(np.float32(np.log(np.float64(255.0))))
INV127 = float(np.float32(1.0 / np.float64(127.0)))
SCALEQ = float(np.float32(1.0 / np.sqrt(np.float64(128.0))))

UPTO = 99


def _had(n):
    H = np.ones((n, n), np.float64)
    for i in range(n):
        for j in range(n):
            if bin(i & j).count("1") % 2:
                H[i, j] = -1.0
    return H


class Sem:
    def __init__(self, nc, name):
        self.h = nc.alloc_semaphore(name)
        self.n = 0

    def inc(self, instr, k=1):
        instr.then_inc(self.h, k)
        self.n += k
        return instr

    def wait(self, eng, lag=0):
        if self.n - lag > 0:
            eng.wait_ge(self.h, self.n - lag)


def row_ap(vec_d, n, off=0):
    return bass.AP(vec_d.tensor, vec_d.offset + off, [[n, 1], [1, n]])


def build(upto=99, dump=(), skip_p1=False, skip_prep=False):
    nc = bass.Bass("TRN2", target_bir_lowering=False, detect_race_conditions=False)
    REG = {}

    def din(name, shape, dt=F32):
        return nc.dram_tensor(name, list(shape), dt, kind="ExternalInput").ap()

    x_d = din("x", [T, D])
    condT_d = din("condT", [D, KV])
    Wq_d = din("Wq", [D, D])
    bq_d = din("bq", [D])
    Wkv_d = din("Wkv", [D, 2 * D])
    bkv_d = din("bkv", [2 * D])
    Wp_d = din("Wp", [D, D])
    bp_d = din("bp", [D])
    W1T_d = din("W1T", [D, FH])
    W3T_d = din("W3T", [D, FH])
    W2TR_d = din("W2TR", [FH, D])
    WinT_d = din("WinT", [D, 2 * DI])
    Wconv_d = din("Wconv", [DI, 4])
    bconv_d = din("bconv", [DI])
    WxT_d = din("WxT", [DI, 160])
    WdtT_d = din("WdtT", [128, DI])
    bdt_d = din("bdt", [DI])
    WoutT_d = din("WoutT", [DI, D])
    Alog_d = din("Alog", [DI, NST])
    Dv_d = din("Dvec", [DI])
    wgt_d = din("wgt", [D, D])
    bias1_d = din("bias1", [D])
    onesbf_d = din("onesbf", [1, 128], BF16)
    onesf_d = din("onesf", [1, 128])
    I128_d = din("I128", [128, 128], BF16)
    H128_d = din("H128", [128, 128], BF16)
    H64B_d = din("H64B", [128, 128], BF16)

    out_d = nc.dram_tensor("out", [T, D], F32, kind="ExternalOutput").ap()

    def dscr(name, shape, dt=BF16):
        return nc.dram_tensor(name, list(shape), dt, kind="Internal").ap()

    xnT_s = dscr("xnT_s", [D, T])
    w2b_s = dscr("w2b_s", [FH, D])
    kscT_s = dscr("kscT_s", [D, KV])
    vtok_s = dscr("vtok_s", [KV, D])
    cqT_s = dscr("cqT_s", [D, T])
    otT_s = dscr("otT_s", [D, T])
    wtT_s = dscr("wtT_s", [D, D])
    winb_s = dscr("winb_s", [D, 2 * DI])
    woutb_s = dscr("woutb_s", [DI, D])

    def sbt(name, shape, dt):
        return nc.alloc_sbuf_tensor(name, list(shape), dt).ap()

    x_tm = [sbt(f"x_tm{i}", [128, D], BF16) for i in range(NCH)]
    ARN = sbt("ARN", [128, 32768], BF16)
    ROWS = sbt("ROWS", [1, 4096], BF16)
    SCRA = sbt("SCRA", [128, D], F32)
    SCRB = sbt("SCRB", [128, D], F32)
    SCRD = sbt("SCRD", [128, 1024], F32)
    SCRI = sbt("SCRI", [128, D], I32)
    sbfA = sbt("sbfA", [128, D], BF16)
    sbfB = sbt("sbfB", [128, D], BF16)
    WB0 = sbt("WB0", [128, 4096], BF16)
    WB1 = sbt("WB1", [128, 2048], BF16)
    WB2 = sbt("WB2", [128, 2048], BF16)
    onesbf = sbt("onesbf_t", [1, 128], BF16)
    onesf = sbt("onesf_t", [1, 128], F32)
    I128 = sbt("I128_t", [128, 128], BF16)
    H128 = sbt("H128_t", [128, 128], BF16)
    H64B = sbt("H64B_t", [128, 128], BF16)
    ln_m = sbt("ln_m", [128, NCH], F32)
    ln_r = sbt("ln_r", [128, NCH], F32)
    ln_b = sbt("ln_b", [128, NCH], F32)
    colA = sbt("colA", [128, NCH], F32)
    colB = sbt("colB", [128, NCH], F32)
    colC = sbt("colC", [128, NCH], F32)
    sq_all = sbt("sq_all", [128, 256], F32)
    nsq_all = sbt("nsq_all", [128, 256], F32)
    acols = sbt("acols", [128, 512], F32)
    ccols = sbt("ccols", [128, 128], F32)
    CARR = sbt("CARR", [128, 512], F32)
    BCR = sbt("BCR", [32, MT], BF16)

    import os as _osd
    DBGB = (sbt("DBGB", [128, 1024], BF16)
            if (_osd.environ.get("P1DEBUG") or _osd.environ.get("V1DEBUG")
                or _osd.environ.get("P2DEBUG")) else None)
    ROWSC = sbt("ROWSC", [1, 4096], BF16)
    psA = nc.alloc_psum_tensor("psA", [128, 2048], F32).ap()
    psB = nc.alloc_psum_tensor("psB", [128, 1024], F32).ap()
    psT = nc.alloc_psum_tensor("psT", [128, 1024], BF16).ap()
    psU = nc.alloc_psum_tensor("psU", [128, 1024], BF16).ap()

    v, g, s, t, sy = nc.vector, nc.gpsimd, nc.scalar, nc.tensor, nc.sync

    sdma = Sem(nc, "sdma")
    gdma = Sem(nc, "gdma")
    adma = Sem(nc, "adma")
    spe = Sem(nc, "spe")
    sdv = Sem(nc, "sdv")
    sac = Sem(nc, "sac")
    sgp = Sem(nc, "sgp")
    sout = Sem(nc, "sout")

    REG.update(locals())

    def vseal(instr):
        sdv.inc(instr)
        sdv.wait(v)
        return instr

    def sseal(instr):
        sac.inc(instr)
        sac.wait(s)
        return instr

    def castload(dst_ap, src_ap, slow=False):
        return gdma.inc(g.dma_start(out=dst_ap, in_=src_ap,
                                    allow_slow_non_contiguous=slow), 16)

    def load(dst_ap, src_ap, eng=None, sem=None, slow=False):
        e = eng or sy
        sm = sem or sdma
        return sm.inc(e.dma_start(out=dst_ap, in_=src_ap,
                                  allow_slow_non_contiguous=slow), 16)

    def load_cols(dst_ap, vec_d, n, off=0):
        src = bass.AP(vec_d.tensor, vec_d.offset + off, [[1, 128], [128, n]])
        return sdma.inc(sy.dma_start(out=dst_ap, in_=src,
                                     allow_slow_non_contiguous=True), 16)

    constp = sbt("constp", [128, 8], F32)
    cvals = [1.0, -1.0, 0.5, -0.5, 1e-5]
    for i_, val_ in enumerate(cvals):
        v.memset(constp[:, i_:i_ + 1], val_)
        nc.const_aps.register(F32, val_, constp[:, i_:i_ + 1])
    sdv.inc(v.nop())
    sdv.wait(s)

    load(onesbf, onesbf_d)
    load(onesf, onesf_d)
    load(I128, I128_d)
    load(H128, H128_d)
    load(H64B, H64B_d)
    
    import os as _os4
    if not _os4.environ.get("SKIP_COLS"):
      for k in range(4):
        src = bass.AP(Wconv_d.tensor, Wconv_d.offset + k, [[4, 128], [512, 32]])
        sdma.inc(sy.dma_start(out=ccols[:, k * 32:(k + 1) * 32], in_=src,
                              allow_slow_non_contiguous=True), 16)
    bdt_cols = sbt("bdt_cols", [128, 32], F32)
    bcv_cols = sbt("bcv_cols", [128, 32], F32)
    dv_cols = sbt("dv_cols", [128, 32], F32)
    if not _os4.environ.get("SKIP_COLS"):
        load_cols(bdt_cols, bdt_d, 32)
        load_cols(bcv_cols, bconv_d, 32)
        load_cols(dv_cols, Dv_d, 32)
        srcA = bass.AP(Alog_d.tensor, Alog_d.offset,
                       [[NST, 128], [128 * NST, 32], [1, NST]])
        sdma.inc(sy.dma_start(out=acols, in_=srcA, allow_slow_non_contiguous=True), 16)
    for i in range(NCH):
        castload(x_tm[i], x_d[i * 128:(i + 1) * 128, :])
    sdma.wait(sy); sdma.wait(t); sdma.wait(v); sdma.wait(s)
    gdma.wait(g)
    nc.all_engine_barrier()
    import os as _os2
    if not _os2.environ.get("SKIP_ACOLS"):
        sac.inc(s.activation(out=acols, in_=acols, func=AF.Exp))
        sac.wait(v)
        sdv.inc(v.tensor_scalar(out=acols, in0=acols, scalar1=-1.0, scalar2=None,
                                op0=OP.mult))

    for ch in range(FH // 128 if not skip_prep else 0):
        wb = WB1 if ch % 2 == 0 else WB2
        if ch >= 2:
            spe.wait(g)
        castload(wb[:, 0:D], W2TR_d[ch * 128:(ch + 1) * 128, :])
        gdma.wait(t)
        sdv.wait(t)
        for gc in range(4):
            spe.inc(t.matmul(psA[:, gc * 512:(gc + 1) * 512], H64B,
                             wb[:, gc * 512:(gc + 1) * 512], start=True, stop=True))
        spe.wait(v)
        sdma.wait(v)
        sdv.inc(v.tensor_scalar(out=sbfB[:, 0:D], in0=psA, scalar1=SQ8192,
                                scalar2=None, op0=OP.mult))
        sdv.wait(sy)
        dst = bass.AP(w2b_s.tensor, w2b_s.offset + (2 * ch) * D,
                      [[D, 2], [128 * D, 64], [1, D]])
        sdma.inc(sy.dma_start(out=dst, in_=sbfB[:, 0:D]), 16)

    for ic in range(NCH if not skip_prep else 0):
        if ic > 0:
            sdv.wait(g)
            sac.wait(g)
        castload(WB0[:, 0:D], wgt_d[ic * 128:(ic + 1) * 128, :])
        gdma.wait(s)
        gdma.wait(v)
        sac.inc(s.activation(out=SCRA, in_=WB0[:, 0:D], func=AF.Abs))
        sac.wait(v)
        vseal(v.reduce_sum(colA[:, 0:1], SCRA, axis=AX))
        vseal(v.tensor_scalar(out=colA[:, 0:1], in0=colA[:, 0:1], scalar1=1.0 / D,
                              scalar2=None, op0=OP.mult))
        vseal(v.tensor_scalar(out=colB[:, 0:1], in0=colA[:, 0:1], scalar1=1e-8,
                              scalar2=None, op0=OP.add))
        vseal(v.reciprocal(colB[:, 1:2], colB[:, 0:1]))
        v.tensor_scalar(out=SCRA, in0=WB0[:, 0:D], scalar1=colB[:, 1:2],
                        scalar2=None, op0=OP.mult)
        v.tensor_copy(SCRI, SCRA)
        v.scalar_tensor_tensor(out=SCRB, in0=SCRA, scalar=-0.5, in1=SCRI,
                               op0=OP.add, op1=OP.is_ge)
        v.tensor_add(SCRB, SCRI, SCRB)
        for hh in range(2):
            v.scalar_tensor_tensor(out=SCRD, in0=SCRA[:, hh * 1024:(hh + 1) * 1024],
                                   scalar=0.5, in1=SCRI[:, hh * 1024:(hh + 1) * 1024],
                                   op0=OP.add, op1=OP.is_le)
            v.tensor_sub(SCRB[:, hh * 1024:(hh + 1) * 1024],
                         SCRB[:, hh * 1024:(hh + 1) * 1024], SCRD)
        v.tensor_scalar(out=SCRB, in0=SCRB, scalar1=-1.0, scalar2=1.0,
                        op0=OP.max, op1=OP.min)
        sdv.inc(v.tensor_scalar(out=sbfA[:, 0:D], in0=SCRB, scalar1=colA[:, 0:1],
                                scalar2=None, op0=OP.mult))
        sdv.wait(t)
        sdma.wait(v)
        for kc in range(NCH):
            slot = psT if kc % 2 == 0 else psU
            sdv.wait(t, lag=1)
            t.transpose(slot[:, 0:128], sbfA[:, kc * 128:(kc + 1) * 128], I128)
            spe.inc(t.nop())
            spe.wait(v)
            sdv.inc(v.tensor_copy(sbfB[:, kc * 128:(kc + 1) * 128], slot[:, 0:128]))
        sdv.wait(sy)
        for kc in range(NCH):
            sdma.inc(sy.dma_start(
                out=wtT_s[kc * 128:(kc + 1) * 128, ic * 128:(ic + 1) * 128],
                in_=sbfB[:, kc * 128:(kc + 1) * 128]), 16)

    sdv.wait(g)
    sac.wait(g)
    for src_d, dst_s, rows, cols in (() if skip_prep else
                                     ((WinT_d, winb_s, D, 2 * DI),
                                      (WoutT_d, woutb_s, DI, D))):
        for rc in range(rows // 128):
            for cb in range(0, cols, 4096):
                cw = min(4096, cols - cb)
                gdma.wait(g)
                castload(WB0[:, 0:cw], src_d[rc * 128:(rc + 1) * 128, cb:cb + cw])
                gdma.wait(g)
                gdma.inc(g.dma_start(out=dst_s[rc * 128:(rc + 1) * 128, cb:cb + cw],
                                     in_=WB0[:, 0:cw]), 16)
    nc.all_engine_barrier()
    if upto == 0:
        _emit_dump(nc, REG, dump)
        _finalize(nc, REG)
        return nc

    cndT = [ARN[:, i * KV:(i + 1) * KV] for i in range(NCH)]
    if skip_p1:
        nc.all_engine_barrier()
    else:
        pass
    if not skip_p1:
      for i in range(NCH):
        castload(cndT[i], condT_d[i * 128:(i + 1) * 128, :])
      castload(ROWS[:, 0:2 * D], row_ap(bkv_d, 2 * D))
      gdma.wait(t)
      sdma.wait(t)
    bkvrow = ROWS[:, 0:2 * D]

    def fq_signed_perpart(src_f32, w, scol, icol, out_bf, scale_back):
        v.tensor_scalar(out=SCRA[:, 0:w], in0=src_f32, scalar1=icol, scalar2=None,
                        op0=OP.mult)
        v.tensor_copy(SCRI[:, 0:w], SCRA[:, 0:w])
        v.scalar_tensor_tensor(out=SCRB[:, 0:w], in0=SCRA[:, 0:w], scalar=-0.5,
                               in1=SCRI[:, 0:w], op0=OP.add, op1=OP.is_ge)
        v.tensor_add(SCRB[:, 0:w], SCRI[:, 0:w], SCRB[:, 0:w])
        for hb in range(0, w, 1024):
            hw = min(1024, w - hb)
            v.scalar_tensor_tensor(out=SCRD[:, 0:hw], in0=SCRA[:, hb:hb + hw],
                                   scalar=0.5, in1=SCRI[:, hb:hb + hw],
                                   op0=OP.add, op1=OP.is_le)
            v.tensor_sub(SCRB[:, hb:hb + hw], SCRB[:, hb:hb + hw], SCRD[:, 0:hw])
        if scale_back:
            sdv.inc(v.tensor_scalar(out=out_bf, in0=SCRB[:, 0:w], scalar1=scol,
                                    scalar2=None, op0=OP.mult))
        else:
            sdv.inc(v.tensor_copy(out_bf, SCRB[:, 0:w]))

    import os as _os9
    _p1dbg = bool(_os9.environ.get("P1DEBUG")) and not skip_p1
    _v1dbg = bool(_os9.environ.get("V1DEBUG")) and not skip_p1
    if _v1dbg:
        _p1dbg = True
    for chalf in range((2 if not _p1dbg else (0 if _v1dbg else 1)) if not skip_p1 else 0):
        for jp in range(4 if not _p1dbg else 1):
            sdv.wait(t)
            for dm in range(NCH):
                wb = WB1 if dm % 2 == 0 else WB2
                spe.wait(g)
                castload(wb[:, 0:1024],
                         Wkv_d[dm * 128:(dm + 1) * 128, chalf * 1024:(chalf + 1) * 1024])
                gdma.wait(t)
                nob = bool(_os9.environ.get("NOBIAS"))
                for jj in range(2):
                    jc = jp * 2 + jj
                    pk = psA[:, jj * 1024:(jj + 1) * 1024]
                    if dm == 0 and not nob:
                        spe.inc(t.matmul(pk[:, 0:512], onesbf,
                                         bkvrow[:, chalf * 1024:chalf * 1024 + 512],
                                         start=True, stop=False))
                        spe.inc(t.matmul(pk[:, 512:1024], onesbf,
                                         bkvrow[:, chalf * 1024 + 512:(chalf + 1) * 1024],
                                         start=True, stop=False))
                    for gc in range(2):
                        spe.inc(t.matmul(pk[:, gc * 512:(gc + 1) * 512],
                                         cndT[dm][:, jc * 128:(jc + 1) * 128],
                                         wb[:, gc * 512:(gc + 1) * 512],
                                         start=(dm == 0 and nob),
                                         stop=(dm == NCH - 1)))
            spe.wait(v)
            if _p1dbg:
                v.tensor_copy(DBGB, psA[:, 0:1024])
            for jj in range(2 if not _p1dbg else 1):
                jc = jp * 2 + jj
                pk = psA[:, jj * 1024:(jj + 1) * 1024]
                vseal(v.reduce_max(colA[:, 0:8], pk.rearrange("p (h q) -> p h q", q=128),
                                   axis=AX, apply_absolute_value=True))
                vseal(v.tensor_scalar(out=colA[:, 0:8], in0=colA[:, 0:8], scalar1=INV127,
                                      scalar2=1e-8, op0=OP.mult, op1=OP.max))
                vseal(v.reciprocal(colB[:, 0:8], colA[:, 0:8]))
                v.tensor_tensor(out=SCRA[:, 0:1024].rearrange("p (h q) -> p h q", q=128),
                                in0=pk.rearrange("p (h q) -> p h q", q=128),
                                in1=colB[:, 0:8].unsqueeze(2).broadcast_to(
                                    (128, 8, 128)),
                                op=OP.mult)
                v.tensor_copy(SCRI[:, 0:1024], SCRA[:, 0:1024])
                v.scalar_tensor_tensor(out=SCRB[:, 0:1024], in0=SCRA[:, 0:1024],
                                       scalar=-0.5, in1=SCRI[:, 0:1024],
                                       op0=OP.add, op1=OP.is_ge)
                v.tensor_add(SCRB[:, 0:1024], SCRI[:, 0:1024], SCRB[:, 0:1024])
                v.scalar_tensor_tensor(out=SCRD, in0=SCRA[:, 0:1024], scalar=0.5,
                                       in1=SCRI[:, 0:1024], op0=OP.add, op1=OP.is_le)
                v.tensor_sub(SCRB[:, 0:1024], SCRB[:, 0:1024], SCRD)
                spe.wait(v)
                sdv.inc(v.tensor_tensor(
                    out=sbfA[:, 0:1024].rearrange("p (h q) -> p h q", q=128),
                    in0=SCRB[:, 0:1024].rearrange("p (h q) -> p h q", q=128),
                    in1=colA[:, 0:8].unsqueeze(2).broadcast_to((128, 8, 128)),
                    op=OP.mult))
                sdv.wait(t)
                sdma.wait(v)
                for hh in range(8):
                    h = chalf * 8 + hh
                    slot = psT if hh % 2 == 0 else psU
                    sdv.wait(t, lag=1)
                    t.transpose(slot[:, 0:128], sbfA[:, hh * 128:(hh + 1) * 128], I128)
                    spe.inc(t.nop())
                    spe.wait(v)
                    sdv.inc(v.tensor_copy(sbfB[:, hh * 128:(hh + 1) * 128],
                                          slot[:, 0:128]))
                sdv.wait(sy)
                for hh in range(8):
                    h = chalf * 8 + hh
                    sdma.inc(sy.dma_start(
                        out=kscT_s[h * 128:(h + 1) * 128, jc * 128:(jc + 1) * 128],
                        in_=sbfB[:, hh * 128:(hh + 1) * 128]), 16)

    if not skip_p1 and (not _p1dbg or _v1dbg):
        load_cols(colC[:, 0:NCH], bkv_d, NCH, off=2048)
    for cc in range((NCH if not _p1dbg else (1 if _v1dbg else 0)) if not skip_p1 else 0):
        spe.wait(g)
        castload(WB1[:, 0:2048].rearrange("p (a b) -> p a b", b=128),
                 Wkv_d[:, 2048 + cc * 128:2048 + (cc + 1) * 128]
                 .rearrange("(a p) b -> p a b", p=128), slow=True)
        gdma.wait(t)
        spe.wait(v, lag=0)
        sdv.wait(t)
        for dm in range(NCH):
            for gc in range(2):
                spe.inc(t.matmul(psA[:, gc * 512:(gc + 1) * 512],
                                 WB1[:, dm * 128:(dm + 1) * 128],
                                 cndT[dm][:, gc * 512:(gc + 1) * 512],
                                 start=(dm == 0), stop=(dm == NCH - 1)))
        spe.wait(v)
        if _v1dbg:
            v.tensor_copy(DBGB, psA[:, 0:1024])
        vcol = colC[:, cc:cc + 1]
        vseal(v.tensor_scalar(out=SCRA[:, 0:1024], in0=psA[:, 0:1024], scalar1=vcol,
                              scalar2=None, op0=OP.add))
        vseal(v.reduce_max(colA[:, 0:1], SCRA[:, 0:1024], axis=AX,
                           apply_absolute_value=True))
        vseal(v.tensor_scalar(out=colA[:, 0:1], in0=colA[:, 0:1], scalar1=INV127,
                              scalar2=1e-8, op0=OP.mult, op1=OP.max))
        vseal(v.reciprocal(colB[:, 0:1], colA[:, 0:1]))
        spe.wait(v)
        fq_signed_perpart(SCRA[:, 0:1024], 1024, colA[:, 0:1], colB[:, 0:1],
                          sbfA[:, 0:1024], True)
        sdv.wait(t)
        sdma.wait(v)
        for jc in range(8):
            slot = psT if jc % 2 == 0 else psU
            sdv.wait(t, lag=1)
            t.transpose(slot[:, 0:128], sbfA[:, jc * 128:(jc + 1) * 128], I128)
            spe.inc(t.nop())
            spe.wait(v)
            sdv.inc(v.tensor_copy(sbfB[:, jc * 128:(jc + 1) * 128], slot[:, 0:128]))
        sdv.wait(sy)
        for jc in range(8):
            sdma.inc(sy.dma_start(
                out=vtok_s[jc * 128:(jc + 1) * 128, cc * 128:(cc + 1) * 128],
                in_=sbfB[:, jc * 128:(jc + 1) * 128]), 16)
    nc.all_engine_barrier()
    if upto == 1:
        _emit_dump(nc, REG, dump)
        _finalize(nc, REG)
        return nc

    def emit_layernorm(src_tiles):
        sdv.wait(s)
        spe.wait(s)
        for ic in range(NCH):
            sdv.inc(v.reduce_sum(colA[:, ic:ic + 1], src_tiles[ic], axis=AX))
            sdv.wait(s)
            sac.inc(s.activation(out=SCRA, in_=src_tiles[ic], func=AF.Square))
            sac.wait(v)
            sdv.inc(v.reduce_sum(colB[:, ic:ic + 1], SCRA, axis=AX))
        sdv.wait(v)
        vseal(v.tensor_scalar(out=ln_m, in0=colA, scalar1=1.0 / D, scalar2=None,
                              op0=OP.mult))
        vseal(v.tensor_scalar(out=colB, in0=colB, scalar1=1.0 / D, scalar2=None,
                              op0=OP.mult))
        vseal(v.tensor_mul(colA, ln_m, ln_m))
        vseal(v.tensor_sub(colB, colB, colA))
        sdv.wait(s)
        sseal(s.activation(out=colB, in_=colB, func=AF.Ln, bias=1e-5, scale=1.0))
        sseal(s.activation(out=ln_r, in_=colB, func=AF.Exp, scale=-0.5))
        sac.wait(v)
        vseal(v.tensor_mul(ln_b, ln_m, ln_r))
        vseal(v.tensor_scalar(out=ln_b, in0=ln_b, scalar1=-1.0, scalar2=None,
                              op0=OP.mult))
        sdv.wait(s)

    def emit_xnT_to_hbm(src_tiles):
        for ic in range(NCH):
            spe.wait(s)
            sac.inc(s.activation(out=sbfA[:, 0:D], in_=src_tiles[ic],
                                 func=AF.Identity, scale=ln_r[:, ic:ic + 1],
                                 bias=ln_b[:, ic:ic + 1]))
            sac.wait(t)
            sdma.wait(v)
            for dc in range(NCH):
                slot = psT if dc % 2 == 0 else psU
                sdv.wait(t, lag=1)
                t.transpose(slot[:, 0:128], sbfA[:, dc * 128:(dc + 1) * 128], I128)
                spe.inc(t.nop())
                spe.wait(v)
                sdv.inc(v.tensor_copy(sbfB[:, dc * 128:(dc + 1) * 128],
                                      slot[:, 0:128]))
            sdv.wait(sy)
            for dc in range(NCH):
                sdma.inc(sy.dma_start(
                    out=xnT_s[dc * 128:(dc + 1) * 128, ic * 128:(ic + 1) * 128],
                    in_=sbfB[:, dc * 128:(dc + 1) * 128]), 16)

    emit_layernorm(x_tm)
    import os as _os3
    if not _os3.environ.get("SKIP_XNT"):
        emit_xnT_to_hbm(x_tm)
    castload(ROWS[:, 0:D], row_ap(bq_d, D))
    gdma.wait(t)
    nc.all_engine_barrier()
    if upto == 2:
        _emit_dump(nc, REG, dump)
        _finalize(nc, REG)
        return nc

    _p2dbg = bool(_os9.environ.get("P2DEBUG"))
    bqrow = ROWS[:, 0:D]
    for dqb in range(8):
        spe.wait(g)
        castload(WB0[:, 0:4096].rearrange("p (a b) -> p a b", b=256),
                 Wq_d[:, dqb * 256:(dqb + 1) * 256]
                 .rearrange("(a p) b -> p a b", p=128), slow=True)
        gdma.wait(t)
        for ic in range(NCH):
            spe.wait(sy)
            load(WB1[:, 0:2048].rearrange("p (a b) -> p a b", b=128),
                 xnT_s[:, ic * 128:(ic + 1) * 128]
                 .rearrange("(a p) b -> p a b", p=128), eng=sy, slow=True)
            sdma.wait(t)
            sdv.wait(t)
            ps_q = psA[:, 0:256]
            spe.inc(t.matmul(ps_q[:, 0:128], onesbf,
                             bqrow[:, dqb * 256:dqb * 256 + 128],
                             start=True, stop=False))
            spe.inc(t.matmul(ps_q[:, 128:256], onesbf,
                             bqrow[:, dqb * 256 + 128:(dqb + 1) * 256],
                             start=False, stop=False))
            for dm in range(NCH):
                spe.inc(t.matmul(ps_q,
                                 WB1[:, dm * 128:(dm + 1) * 128],
                                 WB0[:, dm * 256:(dm + 1) * 256],
                                 start=False, stop=(dm == NCH - 1)))
            spe.wait(v)
            spe.wait(s)
            if _p2dbg and dqb == 0 and ic == 0:
                v.tensor_copy(DBGB[:, 0:256], ps_q)
            vseal(v.reduce_max(colA[:, 0:2],
                               ps_q.rearrange("p (h q) -> p h q", q=128),
                               axis=AX, apply_absolute_value=True))
            vseal(v.tensor_scalar(out=colA[:, 0:2], in0=colA[:, 0:2], scalar1=INV127,
                                  scalar2=1e-8, op0=OP.mult, op1=OP.max))
            vseal(v.reciprocal(colB[:, 0:2], colA[:, 0:2]))
            v.tensor_scalar(out=sq_all[:, ic * 16 + dqb * 2:ic * 16 + dqb * 2 + 2],
                            in0=colA[:, 0:2], scalar1=SCALEQ, scalar2=None,
                            op0=OP.mult)
            v.tensor_tensor(out=SCRA[:, 0:256].rearrange("p (h q) -> p h q", q=128),
                            in0=ps_q.rearrange("p (h q) -> p h q", q=128),
                            in1=colB[:, 0:2].unsqueeze(2)
                            .broadcast_to((128, 2, 128)), op=OP.mult)
            v.tensor_copy(SCRI[:, 0:256], SCRA[:, 0:256])
            v.scalar_tensor_tensor(out=SCRB[:, 0:256], in0=SCRA[:, 0:256],
                                   scalar=-0.5, in1=SCRI[:, 0:256],
                                   op0=OP.add, op1=OP.is_ge)
            v.tensor_add(SCRB[:, 0:256], SCRI[:, 0:256], SCRB[:, 0:256])
            v.scalar_tensor_tensor(out=SCRD[:, 0:256], in0=SCRA[:, 0:256],
                                   scalar=0.5, in1=SCRI[:, 0:256],
                                   op0=OP.add, op1=OP.is_le)
            v.tensor_sub(SCRB[:, 0:256], SCRB[:, 0:256], SCRD[:, 0:256])
            spe.wait(v)
            sdv.inc(v.tensor_copy(sbfA[:, 0:256], SCRB[:, 0:256]))
            sdv.wait(t)
            sdma.wait(v)
            for hh in range(2):
                slot = psT if hh % 2 == 0 else psU
                sdv.wait(t, lag=1)
                t.transpose(slot[:, 0:128],
                            sbfA[:, hh * 128:(hh + 1) * 128], I128)
                spe.inc(t.nop())
                spe.wait(v)
                sdv.inc(v.tensor_copy(sbfB[:, hh * 128:(hh + 1) * 128],
                                      slot[:, 0:128]))
            sdv.wait(sy)
            for hh in range(2):
                dq0 = dqb * 256 + hh * 128
                sdma.inc(sy.dma_start(
                    out=cqT_s[dq0:dq0 + 128, ic * 128:(ic + 1) * 128],
                    in_=sbfB[:, hh * 128:(hh + 1) * 128]), 16)
    spe.wait(v)
    sdv.inc(v.tensor_scalar(out=nsq_all, in0=sq_all, scalar1=-1.0, scalar2=None,
                            op0=OP.mult))
    nc.all_engine_barrier()
    if upto == 3:
        _emit_dump(nc, REG, dump)
        _finalize(nc, REG)
        return nc

    for h in range(NH):
        sdv.wait(sy)
        spe.wait(sy)
        load(WB0[:, 0:2048], cqT_s[h * 128:(h + 1) * 128, :])
        load(WB1[:, 0:1024], kscT_s[h * 128:(h + 1) * 128, :])
        load(WB2[:, 0:1024].rearrange("p (a b) -> p a b", b=128),
             vtok_s[:, h * 128:(h + 1) * 128].rearrange("(a p) b -> p a b", p=128),
             slow=True)
        sdma.wait(t)
        sdma.wait(s)
        sdma.wait(v)
        for ic in range(NCH):
            par = ic % 2
            psS = psA[:, par * 1024:(par + 1) * 1024]
            sac.wait(t, lag=1)
            spe.inc(t.matmul(psS[:, 0:512], WB0[:, ic * 128:(ic + 1) * 128],
                             WB1[:, 0:512], start=True, stop=True))
            spe.inc(t.matmul(psS[:, 512:1024], WB0[:, ic * 128:(ic + 1) * 128],
                             WB1[:, 512:1024], start=True, stop=True))
            spe.wait(v)
            sqc = sq_all[:, ic * 16 + h:ic * 16 + h + 1]
            nsqc = nsq_all[:, ic * 16 + h:ic * 16 + h + 1]
            vseal(v.reduce_max(colA[:, 0:1], psS, axis=AX))
            vseal(v.tensor_scalar(out=colB[:, 0:1], in0=colA[:, 0:1], scalar1=nsqc,
                                  scalar2=LN255, op0=OP.mult, op1=OP.add))
            sdv.wait(s)
            E = SCRA[:, par * 1024:(par + 1) * 1024]
            sac.inc(s.activation(out=E, in_=psS, func=AF.Exp, scale=sqc,
                                 bias=colB[:, 0:1]))
            sac.wait(v)
            Ri = SCRI[:, par * 1024:(par + 1) * 1024]
            Cg = SCRB[:, par * 1024:(par + 1) * 1024]
            vseal(v.reduce_sum(colC[:, 0:1], E, axis=AX))
            vseal(v.reciprocal(colC[:, 1:2], colC[:, 0:1]))
            v.tensor_copy(Ri, E)
            v.scalar_tensor_tensor(out=Cg, in0=E, scalar=-0.5, in1=Ri,
                                   op0=OP.add, op1=OP.is_ge)
            v.tensor_add(E, Ri, Cg)
            ca = sbfA if par == 0 else sbfB
            sdma.wait(v, lag=80)
            adma.wait(v, lag=64)
            sdv.inc(v.tensor_scalar(out=ca[:, 0:1024], in0=E, scalar1=colC[:, 1:2],
                                    scalar2=None, op0=OP.mult))
            sdv.wait(sy)
            sdv.wait(s)
            caT = ca[:, 1024:2048]
            for jc in range(8):
                src = ca[:, jc * 128:(jc + 1) * 128]
                dst = caT[:, jc * 128:(jc + 1) * 128]
                if jc % 2 == 0:
                    sdma.inc(sy.dma_start_transpose(dst, src), 16)
                else:
                    adma.inc(s.dma_start_transpose(dst, src), 16)
            sdma.wait(t)
            adma.wait(t)
            sdv.wait(t, lag=5)
            psO = psB[:, par * 512:par * 512 + 128]
            for jc in range(8):
                spe.inc(t.matmul(psO, WB2[:, jc * 128:(jc + 1) * 128],
                                 caT[:, jc * 128:(jc + 1) * 128],
                                 start=(jc == 0), stop=(jc == 7)))
            spe.wait(v)
            sdv.inc(v.tensor_copy(sbfA[:, 1024:1152] if par == 0 else
                                  sbfB[:, 1024:1152], psO))
            sdv.wait(sy)
            sdma.inc(sy.dma_start(
                out=otT_s[h * 128:(h + 1) * 128, ic * 128:(ic + 1) * 128],
                in_=(sbfA if par == 0 else sbfB)[:, 1024:1152]), 16)
    castload(ROWS[:, 0:D], row_ap(bp_d, D))
    gdma.wait(t)
    nc.all_engine_barrier()
    if upto == 4:
        _emit_dump(nc, REG, dump)
        _finalize(nc, REG)
        return nc

    bprow = ROWS[:, 0:D]
    for ig in range(8):
        sdv.wait(sy)
        spe.wait(sy)
        for icb in range(2):
            ic = ig * 2 + icb
            load(WB0[:, icb * 2048:icb * 2048 + 2048]
                 .rearrange("p (a b) -> p a b", b=128),
                 otT_s[:, ic * 128:(ic + 1) * 128]
                 .rearrange("(a p) b -> p a b", p=128), eng=sy, slow=True)
        sdma.wait(t)
        for dpb in range(4):
            sdv.wait(t)
            spe.wait(g)
            castload(WB1[:, 0:512], Wp_d[0:128, dpb * 512:(dpb + 1) * 512])
            for dc in range(NCH):
                wb = WB1 if dc % 2 == 0 else WB2
                if dc + 1 < NCH:
                    other = WB2 if dc % 2 == 0 else WB1
                    spe.wait(g)
                    castload(other[:, 0:512],
                             Wp_d[(dc + 1) * 128:(dc + 2) * 128,
                                  dpb * 512:(dpb + 1) * 512])
                gdma.wait(t, lag=16 if dc + 1 < NCH else 0)
                for icb in range(2):
                    psP = psA[:, icb * 512:(icb + 1) * 512]
                    if dc == 0:
                        spe.inc(t.matmul(psP[:, 0:128], onesbf,
                                         bprow[:, dpb * 512:dpb * 512 + 128],
                                         start=True, stop=False))
                        spe.inc(t.matmul(psP[:, 128:512], onesbf,
                                         bprow[:, dpb * 512 + 128:(dpb + 1) * 512],
                                         start=False, stop=False))
                    spe.inc(t.matmul(psP,
                                     WB0[:, icb * 2048 + dc * 128:icb * 2048 + (dc + 1) * 128],
                                     wb[:, 0:512], start=False, stop=(dc == NCH - 1)))
            spe.wait(v)
            for icb in range(2):
                ic = ig * 2 + icb
                psP = psA[:, icb * 512:(icb + 1) * 512]
                xs = x_tm[ic][:, dpb * 512:(dpb + 1) * 512]
                sdv.inc(v.scalar_tensor_tensor(out=xs, in0=psP, scalar=1.0, in1=xs,
                                               op0=OP.mult, op1=OP.add))
    nc.all_engine_barrier()
    if upto == 5:
        _emit_dump(nc, REG, dump)
        _finalize(nc, REG)
        return nc

    xsl = [ARN[:, i * 512:(i + 1) * 512] for i in range(NCH)]
    v2 = [ARN[:, 8192 + i * 512:8192 + (i + 1) * 512] for i in range(DCH)]
    w2c = [ARN[:, 24576 + i * 512:24576 + (i + 1) * 512] for i in range(NCH)]
    for ft in range(4):
        sdv.wait(sy)
        spe.wait(sy)
        for dm in range(NCH):
            load(xsl[dm], xnT_s[dm * 128:(dm + 1) * 128, ft * 512:(ft + 1) * 512])
        sdma.wait(t)
        for half in range(2):
            sdv.wait(t)
            def _ldw13(hc2, a2):
                castload(WB0[:, a2 * 2048:(a2 + 1) * 2048]
                         .rearrange("p (a b) -> p a b", b=128),
                         W1T_d[:, hc2 * 128:(hc2 + 1) * 128]
                         .rearrange("(a p) b -> p a b", p=128), slow=True)
                castload((WB1 if a2 == 0 else WB2)[:, 0:2048]
                         .rearrange("p (a b) -> p a b", b=128),
                         W3T_d[:, hc2 * 128:(hc2 + 1) * 128]
                         .rearrange("(a p) b -> p a b", p=128), slow=True)
            spe.wait(g)
            sdv.wait(g)
            sac.wait(g)
            _ldw13(half * DCH, 0)
            for hcl in range(DCH):
                hc = half * DCH + hcl
                a = hcl % 2
                w1b = WB0[:, a * 2048:(a + 1) * 2048]
                w3b = WB1 if a == 0 else WB2
                if hcl + 1 < DCH:
                    spe.wait(g)
                    _ldw13(hc + 1, 1 - a)
                gdma.wait(t, lag=32 if hcl + 1 < DCH else 0)
                hp = hcl % 2
                psG = psA[:, hp * 1024:hp * 1024 + 512]
                psL = psA[:, hp * 1024 + 512:hp * 1024 + 1024]
                sgb = sbfA[:, hp * 512:(hp + 1) * 512]
                hdb = sbfB[:, hp * 512:(hp + 1) * 512]
                psH = psB[:, hp * 512:(hp + 1) * 512]
                if hcl >= 2:
                    sac.wait(t, lag=1)
                    sdv.wait(t, lag=2)
                for dm in range(NCH):
                    spe.inc(t.matmul(psG, w1b[:, dm * 128:(dm + 1) * 128],
                                     xsl[dm], start=(dm == 0), stop=(dm == NCH - 1)))
                for dm in range(NCH):
                    spe.inc(t.matmul(psL, w3b[:, dm * 128:(dm + 1) * 128],
                                     xsl[dm], start=(dm == 0), stop=(dm == NCH - 1)))
                spe.wait(s)
                sac.inc(s.activation(out=sgb, in_=psG, func=AF.Silu))
                sac.wait(v)
                sdv.inc(v.tensor_mul(hdb, sgb, psL))
                sdv.wait(t)
                spe.inc(t.matmul(psH, H128, hdb, start=True, stop=True))
                spe.wait(v)
                sdv.inc(v.tensor_copy(v2[hcl], psH))
            sdv.wait(t)
            sac.wait(t)
            for dpb in range(4):
                for sub in range(2):
                    sdv.wait(sy)
                    spe.wait(sy)
                    for hcl in range(NCH):
                        hh = half * DCH + sub * NCH + hcl
                        load(w2c[hcl],
                             w2b_s[hh * 128:(hh + 1) * 128, dpb * 512:(dpb + 1) * 512])
                    sdma.wait(t)
                    for icq in range(4):
                        ic = ft * 4 + icq
                        psF = psA[:, icq * 512:(icq + 1) * 512]
                        if sub == 0 and icq == 0:
                            sdv.wait(t)
                        for hcl in range(NCH):
                            spe.inc(t.matmul(
                                psF, v2[sub * NCH + hcl][:, icq * 128:(icq + 1) * 128],
                                w2c[hcl], start=(hcl == 0 and sub == 0),
                                stop=(hcl == NCH - 1 and sub == 1)))
                        if sub == 1:
                            spe.wait(v)
                            xs = x_tm[ic][:, dpb * 512:(dpb + 1) * 512]
                            sdv.inc(v.scalar_tensor_tensor(
                                out=xs, in0=psF, scalar=1.0, in1=xs,
                                op0=OP.mult, op1=OP.add))
    nc.all_engine_barrier()
    if upto == 6:
        _emit_dump(nc, REG, dump)
        _finalize(nc, REG)
        return nc

    emit_layernorm(x_tm)
    castload(ROWS[:, 0:D], row_ap(bias1_d, D))
    gdma.wait(t)
    spe.wait(v)
    b1bc = ARN[:, 28672:32768].bitcast(F32)
    sdv.wait(t)
    for gc in range(4):
        spe.inc(t.matmul(psA[:, gc * 512:(gc + 1) * 512], onesbf,
                         ROWS[:, gc * 512:(gc + 1) * 512], start=True, stop=True))
    spe.wait(v)
    sdv.inc(v.tensor_copy(b1bc, psA))
    nc.all_engine_barrier()
    if upto == 7:
        _emit_dump(nc, REG, dump)
        _finalize(nc, REG)
        return nc

    hsT = [ARN[:, i * 256:(i + 1) * 256] for i in range(DCH)]
    gtT = [ARN[:, 8192 + i * 256:8192 + (i + 1) * 256] for i in range(DCH)]
    yT = [ARN[:, 16384 + i * 256:16384 + (i + 1) * 256] for i in range(DCH)]
    xqT = [ARN[:, 24576 + i * 256:24576 + (i + 1) * 256] for i in range(NCH)]
    xn1T = xqT
    sdv.inc(v.memset(CARR, 0.0))
    hs_stash = sbt("hs_stash", [128, DCH * 4], BF16)
    sdv.inc(v.memset(hs_stash, 0.0))
    sdv.wait(t); sdv.wait(s); sdv.wait(g)

    for mt in range(NMT):
        ics = [mt * (MT // 128) + k for k in range(MT // 128)]
        for k, ic in enumerate(ics):
            spe.wait(s)
            sdv.wait(s)
            sac.inc(s.activation(out=sbfA[:, 0:D], in_=x_tm[ic], func=AF.Identity,
                                 scale=ln_r[:, ic:ic + 1], bias=ln_b[:, ic:ic + 1]))
            sac.wait(t)
            for dm in range(NCH):
                slot = psT if dm % 2 == 0 else psU
                sdv.wait(t, lag=1)
                t.transpose(slot[:, 0:128], sbfA[:, dm * 128:(dm + 1) * 128], I128)
                spe.inc(t.nop())
                spe.wait(v)
                sdv.inc(v.tensor_copy(xn1T[dm][:, k * 128:(k + 1) * 128],
                                      slot[:, 0:128]))
            sdv.wait(t)
        spe.wait(sy)
        load(WB1[:, 0:2048].rearrange("p (a b) -> p a b", b=128),
             winb_s[:, 0:128].rearrange("(a p) b -> p a b", p=128),
             eng=sy, slow=True)
        for ec in range(2 * DCH):
            wb = WB1 if ec % 2 == 0 else WB2
            if ec + 1 < 2 * DCH:
                other = WB2 if ec % 2 == 0 else WB1
                spe.wait(sy)
                load(other[:, 0:2048].rearrange("p (a b) -> p a b", b=128),
                     winb_s[:, (ec + 1) * 128:(ec + 2) * 128]
                     .rearrange("(a p) b -> p a b", p=128), eng=sy, slow=True)
            sdma.wait(t, lag=16 if ec + 1 < 2 * DCH else 0)
            par = ec % 2
            psW = psA[:, 1024 + par * 512:1024 + par * 512 + MT]
            sdv.wait(t)
            sac.wait(t)
            for dm in range(NCH):
                spe.inc(t.matmul(psW, wb[:, dm * 128:(dm + 1) * 128],
                                 xn1T[dm][:, 0:MT], start=(dm == 0),
                                 stop=(dm == NCH - 1)))
            spe.wait(v)
            if ec < DCH:
                dc = ec
                st = hs_stash[:, dc * 4:dc * 4 + 4]
                v.tensor_copy(SCRB[:, 0:3], st[:, 1:4])
                vseal(v.tensor_copy(SCRB[:, 3:MT + 3], psW))
                vseal(v.tensor_copy(st[:, 1:4], SCRB[:, MT:MT + 3]))
                v.tensor_scalar(out=SCRA[:, 0:MT], in0=SCRB[:, 3:MT + 3],
                                scalar1=ccols[:, 96 + dc:96 + dc + 1],
                                scalar2=None, op0=OP.mult)
                ii_ = None
                for kk in range(3):
                    ii_ = v.scalar_tensor_tensor(
                        out=SCRA[:, 0:MT], in0=SCRB[:, kk:MT + kk],
                        scalar=ccols[:, kk * 32 + dc:kk * 32 + dc + 1],
                        in1=SCRA[:, 0:MT], op0=OP.mult, op1=OP.add)
                sdv.inc(ii_)
                sdv.wait(s)
                sac.inc(s.activation(out=hsT[dc], in_=SCRA[:, 0:MT], func=AF.Silu,
                                     bias=bcv_cols[:, dc:dc + 1], scale=1.0))
            else:
                dc = ec - DCH
                sdv.inc(v.tensor_copy(gtT[dc], psW))
        sac.wait(t)
        sdv.wait(t)
        sac.wait(v)
        spe.wait(g)
        castload(WB1[:, 0:160], WxT_d[0:128, :])
        for dc in range(DCH):
            wb = WB1 if dc % 2 == 0 else WB2
            if dc + 1 < DCH:
                other = WB2 if dc % 2 == 0 else WB1
                spe.wait(g)
                castload(other[:, 0:160], WxT_d[(dc + 1) * 128:(dc + 2) * 128, :])
            gdma.wait(t, lag=16 if dc + 1 < DCH else 0)
            spe.inc(t.matmul(psB[:, 0:MT], wb[:, 0:128], hsT[dc],
                             start=(dc == 0), stop=(dc == DCH - 1)))
            spe.inc(t.matmul(psB[0:32, 512:512 + MT], wb[:, 128:160], hsT[dc],
                             start=(dc == 0), stop=(dc == DCH - 1)))
        spe.wait(v)
        sdv.inc(v.tensor_copy(sbfB[:, 0:MT], psB[:, 0:MT]))
        sdv.inc(v.tensor_copy(BCR, psB[0:32, 512:512 + MT]))
        sdv.wait(g)
        gdma.inc(g.dma_start(out=ROWS[:, 0:16 * MT], in_=BCR[0:16, :]), 16)
        gdma.inc(g.dma_start(out=ROWSC[:, 0:16 * MT], in_=BCR[16:32, :]), 16)
        gdma.wait(t)
        sdv.wait(t)
        spe.wait(g)
        castload(WB1[:, 0:128], WdtT_d[:, 0:128], slow=True)
        for dc in range(DCH):
            wb = WB1 if dc % 2 == 0 else WB2
            if dc + 1 < DCH:
                other = WB2 if dc % 2 == 0 else WB1
                spe.wait(g)
                castload(other[:, 0:128], WdtT_d[:, (dc + 1) * 128:(dc + 2) * 128],
                         slow=True)
            gdma.wait(t, lag=16 if dc + 1 < DCH else 0)
            sac.wait(t)
            sdv.wait(t)
            spe.inc(t.matmul(psB[:, 512:512 + MT], wb[:, 0:128], sbfB[:, 0:MT],
                             start=True, stop=True))
            spe.wait(s)
            spe.wait(v)
            sac.wait(v)
            sseal(s.activation(out=SCRB[:, 256:256 + MT], in_=psB[:, 512:512 + MT],
                               func=AF.Abs, bias=bdt_cols[:, dc:dc + 1], scale=1.0))
            sseal(s.activation(out=SCRB[:, 256:256 + MT], in_=SCRB[:, 256:256 + MT],
                               func=AF.Exp, scale=-1.0))
            sseal(s.activation(out=SCRB[:, 256:256 + MT], in_=SCRB[:, 256:256 + MT],
                               func=AF.Ln, bias=1.0, scale=1.0))
            sac.wait(v)
            v.tensor_scalar(out=SCRA[:, 0:MT], in0=psB[:, 512:512 + MT],
                            scalar1=bdt_cols[:, dc:dc + 1], scalar2=0.0,
                            op0=OP.add, op1=OP.max)
            vseal(v.tensor_add(SCRA[:, 0:MT], SCRA[:, 0:MT], SCRB[:, 256:256 + MT]))
            sdv.wait(g)
            sdv.inc(v.tensor_mul(sbfA[:, 0:MT], SCRA[:, 0:MT], hsT[dc]))
            sdv.wait(s)
            sdv.wait(g)
            for ng in range(4):
                bsl = psA[:, 0:1024]
                csl = psA[:, 1024:2048]
                if ng >= 1:
                    sdv.wait(t)
                for j in range(2):
                    spe.inc(t.matmul(bsl[:, j * 512:(j + 1) * 512], onesbf,
                                     ROWS[:, ng * 4 * MT + j * 512:
                                          ng * 4 * MT + (j + 1) * 512],
                                     start=True, stop=True))
                for j in range(2):
                    spe.inc(t.matmul(csl[:, j * 512:(j + 1) * 512], onesbf,
                                     ROWSC[:, ng * 4 * MT + j * 512:
                                           ng * 4 * MT + (j + 1) * 512],
                                     start=True, stop=True))
                if ng >= 1:
                    sdv.wait(s, lag=2)
                for jn in range(4):
                    n = ng * 4 + jn
                    sac.inc(s.activation(out=SCRB[:, jn * MT:(jn + 1) * MT],
                                         in_=SCRA[:, 0:MT], func=AF.Exp,
                                         scale=acols[:, dc * 16 + n:dc * 16 + n + 1]))
                spe.wait(v)
                sdv.inc(v.tensor_tensor(
                    out=SCRD.rearrange("p (a b) -> p a b", b=MT),
                    in0=bsl.rearrange("p (a b) -> p a b", b=MT),
                    in1=sbfA[:, 0:MT].unsqueeze(1).broadcast_to((128, 4, MT)),
                    op=OP.mult))
                sac.wait(v)
                last = None
                for jn in range(4):
                    n = ng * 4 + jn
                    cini = CARR[:, dc * 16 + n:dc * 16 + n + 1]
                    last = v.tensor_tensor_scan(
                        out=SCRB[:, 1024 + jn * MT:1024 + (jn + 1) * MT],
                        data0=SCRB[:, jn * MT:(jn + 1) * MT],
                        data1=SCRD[:, jn * MT:(jn + 1) * MT], initial=cini,
                        op0=OP.mult, op1=OP.add)
                sdv.inc(last)
                sdv.wait(v)
                ccp = None
                for jn in range(4):
                    n = ng * 4 + jn
                    ccp = v.tensor_copy(CARR[:, dc * 16 + n:dc * 16 + n + 1],
                                        SCRB[:, 1024 + (jn + 1) * MT - 1:
                                             1024 + (jn + 1) * MT])
                sdv.inc(ccp)
                spe.wait(v, lag=1)
                sdv.inc(v.tensor_tensor(
                    out=sbfB[:, 256:256 + 4 * MT].rearrange("p (a b) -> p a b", b=MT),
                    in0=SCRB[:, 1024:1024 + 4 * MT].rearrange("p (a b) -> p a b", b=MT),
                    in1=csl.rearrange("p (a b) -> p a b", b=MT),
                    op=OP.mult))
                sdv.wait(t)
                for jn in range(4):
                    n = ng * 4 + jn
                    spe.inc(t.matmul(psB[:, 0:MT], I128,
                                     sbfB[:, 256 + jn * MT:256 + (jn + 1) * MT],
                                     start=(n == 0), stop=(n == NST - 1)))
            spe.wait(s)
            sac.inc(s.activation(out=sbfB[:, 512:512 + MT], in_=gtT[dc],
                                 func=AF.Silu))
            sac.wait(v)
            spe.wait(v)
            v.scalar_tensor_tensor(out=SCRB[:, 0:MT], in0=hsT[dc],
                                   scalar=dv_cols[:, dc:dc + 1], in1=psB[:, 0:MT],
                                   op0=OP.mult, op1=OP.add)
            sdv.inc(v.tensor_mul(yT[dc], SCRB[:, 0:MT], sbfB[:, 512:512 + MT]))
            sdv.wait(t)
        f1s = SCRB.bitcast(BF16)
        for k, ic in enumerate(ics):
            spe.wait(s)
            sdv.wait(s)
            sac.inc(s.activation(out=sbfA[:, 0:D], in_=x_tm[ic], func=AF.Identity,
                                 scale=ln_r[:, ic:ic + 1], bias=ln_b[:, ic:ic + 1]))
            sseal(s.activation(out=SCRA, in_=sbfA[:, 0:D], func=AF.Square))
            sac.wait(v)
            vseal(v.reduce_sum(colA[:, 0:1], SCRA, axis=AX))
            vseal(v.tensor_scalar(out=colA[:, 0:1], in0=colA[:, 0:1],
                                  scalar1=1.0 / D, scalar2=1e-8,
                                  op0=OP.mult, op1=OP.add))
            sdv.wait(s)
            sseal(s.activation(out=colA[:, 1:2], in_=colA[:, 0:1], func=AF.Ln))
            sseal(s.activation(out=colA[:, 2:3], in_=colA[:, 1:2], func=AF.Exp,
                               scale=0.5))
            sac.wait(v)
            vseal(v.reduce_max(colA[:, 3:4], sbfA[:, 0:D], axis=AX,
                               apply_absolute_value=True))
            vseal(v.reciprocal(colA[:, 4:5], colA[:, 2:3]))
            vseal(v.tensor_mul(colA[:, 5:6], colA[:, 3:4], colA[:, 4:5]))
            vseal(v.tensor_scalar(out=colA[:, 5:6], in0=colA[:, 5:6], scalar1=INV127,
                                  scalar2=1e-8, op0=OP.mult, op1=OP.add))
            vseal(v.tensor_mul(colA[:, 6:7], colA[:, 2:3], colA[:, 5:6]))
            vseal(v.reciprocal(colA[:, 7:8], colA[:, 6:7]))
            vseal(v.tensor_mul(colA[:, 9 + k:10 + k], colA[:, 3:4], colA[:, 4:5]))
            vseal(v.tensor_scalar(out=colA[:, 9 + k:10 + k], in0=colA[:, 9 + k:10 + k],
                                  scalar1=INV127, scalar2=None, op0=OP.mult))
            for hb in range(2):
                sl = slice(hb * 1024, (hb + 1) * 1024)
                v.tensor_scalar(out=SCRA[:, sl], in0=sbfA[:, sl],
                                scalar1=colA[:, 7:8], scalar2=None, op0=OP.mult)
                v.tensor_copy(SCRI[:, sl], SCRA[:, sl])
                v.scalar_tensor_tensor(out=SCRD, in0=SCRA[:, sl], scalar=-0.5,
                                       in1=SCRI[:, sl], op0=OP.add, op1=OP.is_ge)
                v.tensor_add(SCRD, SCRI[:, sl], SCRD)
                v.scalar_tensor_tensor(out=SCRA[:, sl], in0=SCRA[:, sl], scalar=0.5,
                                       in1=SCRI[:, sl], op0=OP.add, op1=OP.is_le)
                v.tensor_sub(SCRD, SCRD, SCRA[:, sl])
                spe.wait(v)
                sdv.inc(v.tensor_copy(sbfB[:, sl], SCRD))
            sdv.wait(t)
            for dm in range(NCH):
                slot = psT if dm % 2 == 0 else psU
                sdv.wait(t, lag=1)
                t.transpose(slot[:, 0:128], sbfB[:, dm * 128:(dm + 1) * 128], I128)
                spe.inc(t.nop())
                spe.wait(v)
                sdv.inc(v.tensor_copy(xqT[dm][:, k * 128:(k + 1) * 128],
                                      slot[:, 0:128]))
            sdv.wait(t)
        for dph in range(2):
            spe.wait(sy)
            load(WB1[:, 0:1024], wtT_s[0:128, dph * 1024:(dph + 1) * 1024], eng=sy)
            if dph == 0:
                sdv.wait(t)
                sac.wait(t)
            for kc in range(NCH):
                wb = WB1 if kc % 2 == 0 else WB2
                if kc + 1 < NCH:
                    other = WB2 if kc % 2 == 0 else WB1
                    spe.wait(sy)
                    load(other[:, 0:1024],
                         wtT_s[(kc + 1) * 128:(kc + 2) * 128,
                               dph * 1024:(dph + 1) * 1024], eng=sy)
                sdma.wait(t, lag=16 if kc + 1 < NCH else 0)
                for k in range(MT // 128):
                    for gc in range(2):
                        spe.inc(t.matmul(
                            psA[:, k * 1024 + gc * 512:k * 1024 + (gc + 1) * 512],
                            xqT[kc][:, k * 128:(k + 1) * 128],
                            wb[:, gc * 512:(gc + 1) * 512],
                            start=(kc == 0), stop=(kc == NCH - 1)))
            spe.wait(s)
            for k in range(MT // 128):
                sac.inc(s.activation(out=f1s[:, k * 2048 + dph * 1024:
                                             k * 2048 + (dph + 1) * 1024],
                                     in_=psA[:, k * 1024:(k + 1) * 1024],
                                     func=AF.Identity, scale=colA[:, 9 + k:10 + k]))
            sac.wait(t)
        for dph in range(2):
            spe.wait(sy)
            load(WB1[:, 0:1024], woutb_s[0:128, dph * 1024:(dph + 1) * 1024], eng=sy)
            for dc in range(DCH):
                wb = WB1 if dc % 2 == 0 else WB2
                if dc + 1 < DCH:
                    other = WB2 if dc % 2 == 0 else WB1
                    spe.wait(sy)
                    load(other[:, 0:1024],
                         woutb_s[(dc + 1) * 128:(dc + 2) * 128,
                                 dph * 1024:(dph + 1) * 1024], eng=sy)
                sdma.wait(t, lag=16 if dc + 1 < DCH else 0)
                if dc == 0:
                    sdv.wait(t)
                    sac.wait(t)
                for k in range(MT // 128):
                    for gc in range(2):
                        spe.inc(t.matmul(
                            psA[:, k * 1024 + gc * 512:k * 1024 + (gc + 1) * 512],
                            yT[dc][:, k * 128:(k + 1) * 128],
                            wb[:, gc * 512:(gc + 1) * 512],
                            start=(dc == 0), stop=(dc == DCH - 1)))
            spe.wait(v)
            for k, ic in enumerate(ics):
                sout.wait(v)
                hsl = slice(dph * 1024, (dph + 1) * 1024)
                v.scalar_tensor_tensor(out=SCRA[:, 0:1024],
                                       in0=psA[:, k * 1024:(k + 1) * 1024],
                                       scalar=1.0, in1=b1bc[:, hsl],
                                       op0=OP.mult, op1=OP.add)
                v.tensor_add(SCRA[:, 0:1024], SCRA[:, 0:1024], x_tm[ic][:, hsl])
                sdv.inc(v.tensor_add(SCRA[:, 0:1024], SCRA[:, 0:1024],
                                     f1s[:, k * 2048 + dph * 1024:
                                         k * 2048 + (dph + 1) * 1024]))
                sdv.wait(sy)
                sout.inc(sy.dma_start(
                    out=out_d[ic * 128:(ic + 1) * 128, dph * 1024:(dph + 1) * 1024],
                    in_=SCRA[:, 0:1024]), 16)
    nc.all_engine_barrier()
    sout.wait(sy)
    _emit_dump(nc, REG, dump)
    return nc


def _emit_dump(nc, R, dump):
    if not dump:
        return
    nc.all_engine_barrier()
    sy = R["sy"]
    sdma = R["sdma"]
    for name in dump:
        obj = R[name]
        if isinstance(obj, list):
            w = obj[0].shape[1]
            dt = obj[0].dtype
            dd = nc.dram_tensor(f"dbg_{name}", [len(obj), 128, w], dt,
                                kind="ExternalOutput").ap()
            for i, ap in enumerate(obj):
                sdma.inc(sy.dma_start(out=dd[i], in_=ap), 16)
        else:
            sh = list(obj.shape)
            dd = nc.dram_tensor(f"dbg_{name}", sh, obj.dtype,
                                kind="ExternalOutput").ap()
            sdma.inc(sy.dma_start(out=dd, in_=obj), 16)
    sdma.wait(sy)


def _finalize(nc, R):
    nc.all_engine_barrier()
    sy, v = R["sy"], R["v"]
    sdv, sout = R["sdv"], R["sout"]
    out_d, x_tm = R["out_d"], R["x_tm"]
    scro = R["SCRA"]
    for ic in range(NCH):
        sout.wait(v)
        sdv.inc(v.tensor_copy(scro, x_tm[ic]))
        sdv.wait(sy)
        sout.inc(sy.dma_start(out=out_d[ic * 128:(ic + 1) * 128, :], in_=scro), 16)
    sout.wait(sy)


def _host_layout(inputs):
    f32 = np.float32
    x = np.ascontiguousarray(np.asarray(inputs["L0_attn_x"], dtype=f32))
    cond = np.asarray(inputs["L0_attn_cond"], dtype=f32).reshape(KV, D)
    condT = np.ascontiguousarray(cond.T)
    W2 = np.asarray(inputs["L0_ffn_W2"], dtype=f32)
    W2TR = np.ascontiguousarray(
        W2.T.reshape(64, 128, D).transpose(1, 0, 2).reshape(FH, D))
    H128 = _had(128).astype(ml_dtypes.bfloat16)
    H64 = _had(64) * (2.0 ** -7)
    H64B = np.zeros((128, 128), np.float64)
    H64B[:64, :64] = H64
    H64B[64:, 64:] = H64
    common = {
        "condT": condT,
        "Wq": np.ascontiguousarray(np.asarray(inputs["L0_attn_Wq"], dtype=f32)),
        "bq": np.ascontiguousarray(np.asarray(inputs["L0_attn_bq"], dtype=f32)),
        "Wkv": np.ascontiguousarray(np.asarray(inputs["L0_attn_Wkv"], dtype=f32)),
        "bkv": np.ascontiguousarray(np.asarray(inputs["L0_attn_bkv"], dtype=f32)),
        "Wp": np.ascontiguousarray(np.asarray(inputs["L0_attn_Wp"], dtype=f32)),
        "bp": np.ascontiguousarray(np.asarray(inputs["L0_attn_bp"], dtype=f32)),
        "W1T": np.ascontiguousarray(np.asarray(inputs["L0_ffn_W1"], dtype=f32).T),
        "W3T": np.ascontiguousarray(np.asarray(inputs["L0_ffn_W3"], dtype=f32).T),
        "W2TR": W2TR,
        "WinT": np.ascontiguousarray(np.asarray(inputs["L1_attn_Win"], dtype=f32).T),
        "Wconv": np.ascontiguousarray(
            np.asarray(inputs["L1_attn_Wconv"], dtype=f32).reshape(DI, 4)),
        "bconv": np.ascontiguousarray(np.asarray(inputs["L1_attn_bconv"], dtype=f32)),
        "WxT": np.ascontiguousarray(np.asarray(inputs["L1_attn_Wx"], dtype=f32).T),
        "WdtT": np.ascontiguousarray(np.asarray(inputs["L1_attn_Wdt"], dtype=f32).T),
        "bdt": np.ascontiguousarray(np.asarray(inputs["L1_attn_bdt"], dtype=f32)),
        "WoutT": np.ascontiguousarray(np.asarray(inputs["L1_attn_Wout"], dtype=f32).T),
        "Alog": np.ascontiguousarray(np.asarray(inputs["L1_attn_A_log"], dtype=f32)),
        "Dvec": np.ascontiguousarray(np.asarray(inputs["L1_attn_D"], dtype=f32)),
        "wgt": np.ascontiguousarray(np.asarray(inputs["L1_ffn_weight"], dtype=f32)),
        "bias1": np.ascontiguousarray(np.asarray(inputs["L1_ffn_bias"], dtype=f32)),
        "onesbf": np.ones((1, 128), ml_dtypes.bfloat16),
        "onesf": np.ones((1, 128), f32),
        "I128": np.eye(128).astype(ml_dtypes.bfloat16),
        "H128": H128,
        "H64B": H64B.astype(ml_dtypes.bfloat16),
    }
    return [dict(common, x=np.ascontiguousarray(x[i])) for i in range(8)]


def kernel(**inputs):
    in_maps = _host_layout(inputs)
    nc = build(UPTO)
    res = run_bass_kernel_spmd(nc, in_maps, core_ids=list(range(8)))
    return np.stack([np.asarray(r["out"], dtype=np.float32) for r in res.results])
